# version 01
# baseline (speedup 1.0000x reference)
import sys
sys.path.insert(0, "/opt/trn_rl_repo")
import math
from contextlib import ExitStack

import numpy as np
import ml_dtypes

import concourse.bass as bass
import concourse.mybir as mybir
from concourse.alu_op_type import AluOpType as OP
from concourse.bass_utils import run_bass_kernel_spmd

F32 = mybir.dt.float32
BF16 = mybir.dt.bfloat16
AF = mybir.ActivationFunctionType
AX = mybir.AxisListType

D = 2048
T = 2048
NT = 16
LCH = 128
NCH = T // LCH
SUB = 16
NSUB = LCH // SUB
HS = 64
NHEAD = 32
HD = 128
NQH = 16
NKV = 4
DKV = NKV * HD
NEXP = 32
IFF = 768
NIT = IFF // 128
TOPK = 8
LN_EPS = 1e-5
SQRT_HD = math.sqrt(128.0)
E50 = float(np.exp(np.float32(-50.0)))

ARN = 100352

SLOT_A = 0
SLOT_B = 32768
SLOT_C = 65536
TAIL = 92160


class Prog:
    def __init__(self, nc, block, sems, L):
        self.nc = nc
        self.block = block
        self.sem = sems
        self.C = {k: 0 for k in sems}
        self.dmaset = {"dW", "dA"}
        self.L = L
        self.wcount = {}

    def inc(self, instr, name, n=1):
        if name in ("cV", "cS", "cG"):
            return instr
        instr.then_inc(self.sem[name], n * (16 if name in self.dmaset else 1))
        self.C[name] += n
        return instr

    def w(self, eng, name, count=None):
        if count is None:
            count = self.C[name]
        assert count <= self.C[name], f"wait on future count {name} {count} > {self.C[name]}"
        eng.wait_ge(self.sem[name], count * (16 if name in self.dmaset else 1))

    def dma(self, eng, q, out, in_, transpose=False, gate=True, **kw):
        if gate and self.C[q] > 0:
            eng.wait_ge(self.sem[q], self.C[q] * 16)
        if transpose:
            instr = eng.dma_start_transpose(out, in_)
        else:
            instr = eng.dma_start(out=out, in_=in_, **kw)
        self.inc(instr, q)
        return self.C[q]


CHAIN_OPS = {
    "tensor_add", "tensor_mul", "tensor_sub", "tensor_max", "tensor_copy",
    "tensor_tensor", "tensor_scalar", "tensor_scalar_add", "tensor_scalar_mul",
    "tensor_scalar_max", "tensor_scalar_min", "tensor_scalar_sub",
    "tensor_single_scalar", "tensor_reduce", "reduce_max", "reduce_sum",
    "reciprocal", "memset", "memzero", "tensor_tensor_scan",
    "scalar_tensor_tensor", "select", "copy_predicated", "activation",
    "add", "mul", "copy", "sqrt", "square", "sign", "iota",
    "partition_broadcast", "partition_all_reduce", "bn_stats", "bn_aggr",
    "stream_shuffle", "affine_select",
}


class Chained:

    def __init__(self, P, eng, ctr):
        object.__setattr__(self, "_P", P)
        object.__setattr__(self, "_e", eng)
        object.__setattr__(self, "_c", ctr)

    def __getattr__(self, name):
        attr = getattr(self._e, name)
        if name not in CHAIN_OPS:
            return attr
        P, eng, ctr = self._P, self._e, self._c

        def wrap(*a, **kw):
            if P.C[ctr] > 0:
                eng.wait_ge(P.sem[ctr], P.C[ctr])
            instr = attr(*a, **kw)
            instr.then_inc(P.sem[ctr], 1)
            P.C[ctr] += 1
            return instr
        return wrap


_CHAIN_CTR = {"vector": "cV", "scalar": "cS", "gpsimd": "cG"}


def phase_fence(P):
    for eng in ("gpsimd", "vector", "scalar", "tensor", "sync"):
        @sec(P, eng)
        def _(e):
            for s_ in ("dA", "dW", "cT", "cV", "cS", "cG"):
                P.w(e, s_)


def sec(P, engname):
    deco = getattr(P.block, engname)
    ctr = _CHAIN_CTR.get(engname)
    if ctr is None:
        return deco

    def outer(fn):
        def inner(eng):
            return fn(Chained(P, eng, ctr))
        return deco(inner)
    return outer


def ln_tile(P, xt, affine, xn_out):
    L = P.L
    sc = L["sc_sb"]
    uu = L["ln_u"]
    npass = 2 if affine else 1
    src = xt
    for p in range(npass):
        @sec(P, "scalar")
        def _(s, src=src):
            P.w(s, "dA")
            P.w(s, "cV")
            P.inc(s.activation(out=L["ln_sq"], in_=src, func=AF.Square), "cS")

        @sec(P, "vector")
        def _(v, src=src):
            P.w(v, "cS")
            v.tensor_reduce(out=sc[:, 0:1], in_=src, axis=AX.X, op=OP.add)
            v.tensor_scalar(out=sc[:, 0:1], in0=sc[:, 0:1], scalar1=1.0 / D, scalar2=None, op0=OP.mult)
            v.tensor_reduce(out=sc[:, 1:2], in_=L["ln_sq"], axis=AX.X, op=OP.add)
            v.tensor_scalar(out=sc[:, 1:2], in0=sc[:, 1:2], scalar1=1.0 / D, scalar2=None, op0=OP.mult)
            v.tensor_tensor(out=sc[:, 2:3], in0=sc[:, 0:1], in1=sc[:, 0:1], op=OP.mult)
            v.tensor_tensor(out=sc[:, 1:2], in0=sc[:, 1:2], in1=sc[:, 2:3], op=OP.subtract)
            P.inc(v.tensor_scalar(out=sc[:, 1:2], in0=sc[:, 1:2], scalar1=LN_EPS, scalar2=None, op0=OP.add), "cV")

        @sec(P, "scalar")
        def _(s):
            P.w(s, "cV")
            P.inc(s.activation(out=sc[:, 3:4], in_=sc[:, 1:2], func=AF.Sqrt), "cS")

        last = (p == npass - 1)

        @sec(P, "vector")
        def _(v, src=src, last=last, p=p):
            P.w(v, "cS")
            if last:
                P.w(v, "cT")
            v.reciprocal(out=sc[:, 4:5], in_=sc[:, 3:4])
            if not affine:
                P.inc(v.tensor_scalar(out=xn_out, in0=src, scalar1=sc[:, 0:1], scalar2=sc[:, 4:5],
                                      op0=OP.subtract, op1=OP.mult), "cV")
            elif p == 0:
                P.inc(v.tensor_scalar(out=uu, in0=src, scalar1=sc[:, 0:1], scalar2=sc[:, 4:5],
                                      op0=OP.subtract, op1=OP.mult), "cV")
            else:
                v.tensor_scalar(out=uu, in0=uu, scalar1=sc[:, 0:1], scalar2=sc[:, 4:5],
                                op0=OP.subtract, op1=OP.mult)
                v.tensor_tensor(out=uu, in0=uu, in1=L["gB"], op=OP.mult)
                P.inc(v.tensor_tensor(out=xn_out, in0=uu, in1=L["bB"], op=OP.add), "cV")
        src = uu


def transpose16(P, src_fn, dst_fn, n=16):
    L = P.L
    nh = (n + 7) // 8
    for h in range(nh):
        cnt = min(8, n - h * 8)
        pt = L["PT0"] if h % 2 == 0 else L["PT1"]

        @sec(P, "tensor")
        def _(t, h=h, cnt=cnt, pt=pt):
            P.w(t, "cV")
            for j in range(cnt):
                P.inc(t.transpose(pt[:, j * 128:j * 128 + 128], src_fn(h * 8 + j),
                                  L["identb_sb"][:]), "cT", 1 if j == cnt - 1 else 0)

        @sec(P, "vector")
        def _(v, h=h, cnt=cnt, pt=pt):
            P.w(v, "cT")
            for j in range(cnt):
                P.inc(v.tensor_copy(out=dst_fn(h * 8 + j), in_=pt[:, j * 128:j * 128 + 128]),
                      "cV", 1 if j == cnt - 1 else 0)




def build(n_sublayers=4, stage=None, rec_lvl=7, rec_nch=NCH, rec_nsub=NSUB, rec_nhf=2):
    nc = bass.Bass("TRN2", target_bir_lowering=False)

    def din(name, shape):
        return nc.dram_tensor(name, list(shape), F32, kind="ExternalInput").ap()

    L = {"n_sublayers": n_sublayers}
    for nm, shp in [
        ("L0_attn_a_x", (T, D)), ("L0_attn_a_ln_scale", (D,)), ("L0_attn_a_ln_bias", (D,)),
        ("L0_attn_a_Wx", (D, D)), ("L0_attn_a_Ww", (D, D)), ("L0_attn_a_bw", (D,)),
        ("L0_attn_a_Wk", (D, D)), ("L0_attn_a_Wv", (D, D)), ("L0_attn_a_Wr", (D, D)),
        ("L0_attn_a_Wo", (D, D)), ("L0_attn_b_Wq", (D, D)), ("L0_attn_b_Wk", (D, DKV)),
        ("L0_attn_b_Wv", (D, DKV)), ("L0_attn_b_Wo", (D, D)), ("L0_ffn_gate_w", (NEXP, D)),
        ("L0_ffn_w_gate", (NEXP, D, IFF)), ("L0_ffn_w_up", (NEXP, D, IFF)),
        ("L0_ffn_w_down", (NEXP, IFF, D)),
    ]:
        L[nm] = din(nm, shp)
    L["x_in"], L["ln_scale"], L["ln_bias"] = L["L0_attn_a_x"], L["L0_attn_a_ln_scale"], L["L0_attn_a_ln_bias"]
    L["aWx"], L["aWw"], L["a_bw"] = L["L0_attn_a_Wx"], L["L0_attn_a_Ww"], L["L0_attn_a_bw"]
    L["aWk"], L["aWv"], L["aWr"], L["aWo"] = L["L0_attn_a_Wk"], L["L0_attn_a_Wv"], L["L0_attn_a_Wr"], L["L0_attn_a_Wo"]
    L["bWq"], L["bWk"], L["bWv"], L["bWo"] = L["L0_attn_b_Wq"], L["L0_attn_b_Wk"], L["L0_attn_b_Wv"], L["L0_attn_b_Wo"]
    L["gate_w"], L["w_gate"], L["w_up"], L["w_down"] = L["L0_ffn_gate_w"], L["L0_ffn_w_gate"], L["L0_ffn_w_up"], L["L0_ffn_w_down"]

    L["ident_bf"] = nc.dram_tensor("c_ident_bf", [128, 128], BF16, kind="ExternalInput").ap()
    L["triu_f32"] = nc.dram_tensor("c_triu_f32", [128, 128], F32, kind="ExternalInput").ap()
    L["out"] = nc.dram_tensor("out", [T, D], F32, kind="ExternalOutput").ap()

    def scr(name, shape, dt):
        return nc.dram_tensor(name, list(shape), dt)

    L["resid_d"] = scr("resid_d", (T, D), F32)
    wbf = {}
    for nm in ["aWw", "aWk", "aWv", "aWr", "aWo", "bWq", "bWo"]:
        wbf[nm] = scr("wbf_" + nm, (D, D), BF16)
    wbf["bWk"] = scr("wbf_bWk", (D, DKV), BF16)
    wbf["bWv"] = scr("wbf_bWv", (D, DKV), BF16)
    wbf["wg"] = scr("wbf_wg", (NEXP, D, IFF), BF16)
    wbf["wu"] = scr("wbf_wu", (NEXP, D, IFF), BF16)
    wbf["wd"] = scr("wbf_wd", (NEXP, IFF, D), BF16)
    L["wbf"] = wbf
    L["zln_d"] = scr("zln_d", (NCH, 128, D), F32)
    L["kT_d"] = scr("kT_d", (NCH, 128, D), BF16)
    L["rT_d"] = scr("rT_d", (NCH, 128, D), BF16)
    L["vN_d"] = scr("vN_d", (NCH, 128, D), BF16)
    L["xnT_d"] = scr("xnT_d", (NT, 128, D), BF16)
    L["fwT_d"] = scr("fwT_d", (NEXP, T), F32)

    st = ExitStack()
    with st:
        e = st.enter_context
        AR = e(nc.sbuf_tensor("AR", [128, ARN], BF16))
        L["AR"] = AR
        L["state_sb"] = e(nc.sbuf_tensor("state_sb", [128, NT * HS], F32))
        L["state_bf"] = e(nc.sbuf_tensor("state_bf", [128, NT * HS], BF16))
        L["identb_sb"] = e(nc.sbuf_tensor("identb_sb", [128, 128], BF16))
        L["triu_sb"] = e(nc.sbuf_tensor("triu_sb", [128, 128], F32))
        L["e50_sb"] = e(nc.sbuf_tensor("e50_sb", [128, 128], BF16))
        L["gwT_sb"] = e(nc.sbuf_tensor("gwT_sb", [128, NT * NEXP], BF16))
        L["fw_sb"] = e(nc.sbuf_tensor("fw_sb", [128, NT * NEXP], F32))
        L["sc_sb"] = e(nc.sbuf_tensor("sc_sb", [128, 64], F32))
        L["bw_sb"] = e(nc.sbuf_tensor("bw_sb", [128, NT], F32))
        L["eL_sb"] = e(nc.sbuf_tensor("eL_sb", [128, NT], F32))
        L["rt_sb"] = e(nc.sbuf_tensor("rt_sb", [128, 96], F32))
        L["identf_sb"] = e(nc.sbuf_tensor("identf_sb", [128, 128], F32))
        L["PA"] = e(nc.psum_tensor("PA", [128, 2048], F32))
        L["PB"] = e(nc.psum_tensor("PB", [128, 1024], F32))
        L["PT0"] = e(nc.psum_tensor("PT0", [128, 1024], BF16))
        L["PT1"] = e(nc.psum_tensor("PT1", [128, 1024], BF16))
        sems = {}
        for s_ in ["dW", "dA", "cT", "cV", "cS", "cG"]:
            sems[s_] = e(nc.semaphore(s_))
        block = e(nc.Block())

        P = Prog(nc, block, sems, L)
        ARf = AR.ap().bitcast(F32)

        def bv(off, n):
            assert off + n <= ARN
            return AR[:, off:off + n]

        def fv(off, n):
            assert off % 2 == 0 and off + 2 * n <= ARN
            return ARf[:, off // 2: off // 2 + n]

        L["bv"] = bv
        L["fv"] = fv
        L["ln_sq"] = fv(TAIL, 2048)
        L["ln_u"] = fv(TAIL + 4096, 2048)

        L["stage"] = stage
        L["rec_lvl"] = rec_lvl
        L["rec_nch"] = rec_nch
        L["rec_nsub"] = rec_nsub
        L["rec_nhf"] = rec_nhf
        emit_consts_and_w0(P)
        if stage != "w0":
            emit_attn_a(P)
        if n_sublayers >= 2:
            emit_ffn(P, 0)
        if n_sublayers >= 3:
            emit_attn_b(P)
        if n_sublayers >= 4:
            emit_ffn(P, 1)
        emit_finish(P)

    return nc


def emit_consts_and_w0(P):
    L = P.L
    wbf = L["wbf"]

    @sec(P, "sync")
    def _(sy):
        P.dma(sy, "dA", L["identb_sb"][:], L["ident_bf"])
        P.dma(sy, "dA", L["triu_sb"][:], L["triu_f32"])
        P.dma(sy, "dA", L["bw_sb"][:],
              bass.AP(L["a_bw"].tensor, 0, [[1, 128], [128, NT]]),
              allow_slow_non_contiguous=True)

    @sec(P, "gpsimd")
    def _(g):
        g.memset(L["e50_sb"][:], E50)
        g.memset(L["state_sb"][:], 0.0)
        P.inc(g.memset(L["state_bf"][:], 0.0), "cG")
        for nm in ["aWw", "aWk", "aWv", "aWr", "aWo"]:
            P.wcount[nm] = P.dma(g, "dW", wbf[nm].ap(), L[nm])
        for ei in range(NEXP):
            P.dma(g, "dW", wbf["wg"].ap()[ei], L["w_gate"][ei])
            P.dma(g, "dW", wbf["wu"].ap()[ei], L["w_up"][ei])
            P.wcount[f"ffn{ei}"] = P.dma(g, "dW", wbf["wd"].ap()[ei], L["w_down"][ei])
        for nm in ["bWq", "bWk", "bWv", "bWo"]:
            P.wcount[nm] = P.dma(g, "dW", wbf[nm].ap(), L[nm])




def wt_quarter_load(P, slot, wdram, q, pre_wait=None):
    L = P.L
    bv = L["bv"]

    @sec(P, "sync")
    def _(sy):
        if pre_wait:
            for args in pre_wait:
                P.w(sy, *args)
        for j in range(NT):
            P.dma(sy, "dA", bv(SLOT_C + slot * 8192 + j * 512, 512),
                  wdram.ap()[q * 512:(q + 1) * 512, j * 128:j * 128 + 128],
                  transpose=True, gate=(j == 0))
    return P.C["dA"]


def emit_attn_a(P):
    L = P.L
    bv, fv = L["bv"], L["fv"]
    PA, PB = L["PA"], L["PB"]

    def wq(slot, j):
        return bv(SLOT_C + slot * 8192 + j * 512, 512)

    ring_n = {"i": 0}

    def next_slot(wname=None):
        pw = []
        if wname is not None:
            pw.append(("dW", P.wcount[wname]))
        if ring_n["i"] >= 3:
            pw.append(("cT", P.C["cT"]))
        slot = ring_n["i"] % 3
        ring_n["i"] += 1
        return slot, pw

    @sec(P, "gpsimd")
    def _(g):
        for j in range(NT):
            P.dma(g, "dW", bv(SLOT_A + j * 2048, 2048),
                  L["aWx"][j * 128:(j + 1) * 128, :], gate=(j == 0))
    wx_done = P.C["dW"]

    ev_cnt = {}
    for q in range(4):
        slot, pw = next_slot("aWw" if q == 0 else None)
        dbc = wt_quarter_load(P, slot, L["wbf"]["aWw"], q, pre_wait=pw)
        for dt in range(NT):
            pq = dt % 2

            @sec(P, "tensor")
            def _(t, q=q, dt=dt, pq=pq, slot=slot, dbc=dbc):
                if q == 0 and dt == 0:
                    P.w(t, "cV")
                    P.w(t, "dW", wx_done)
                if dt == 0:
                    P.w(t, "dA", dbc)
                if pq in ev_cnt:
                    P.w(t, *ev_cnt[pq])
                pa = PA[:, pq * 512:pq * 512 + 512]
                for k in range(NT):
                    P.inc(t.matmul(pa, bv(SLOT_A + k * 2048 + dt * 128, 128),
                                   wq(slot, k), start=(k == 0), stop=(k == NT - 1)),
                          "cT", 1 if k == NT - 1 else 0)

            @sec(P, "vector")
            def _(v, q=q, dt=dt, pq=pq):
                P.w(v, "cT")
                P.inc(v.tensor_copy(out=bv(SLOT_B + dt * 2048 + q * 512, 512),
                                    in_=PA[:, pq * 512:pq * 512 + 512]), "cV")
            ev_cnt[pq] = ("cV", P.C["cV"])

    if L.get("stage") == "wz":
        return
    L["gB"] = fv(SLOT_C, 2048)
    L["bB"] = fv(SLOT_C + 4096, 2048)
    xbuf = fv(SLOT_C + 8192, 2048)
    xnst = bv(SLOT_C + 16384, 2048)

    @sec(P, "sync")
    def _(sy):
        P.w(sy, "cV")
        P.dma(sy, "dA", L["gB"], bass.AP(L["ln_scale"].tensor, 0, [[0, 128], [1, 2048]]))
        P.dma(sy, "dA", L["bB"], bass.AP(L["ln_bias"].tensor, 0, [[0, 128], [1, 2048]]))

    for i in range(NT):
        @sec(P, "sync")
        def _(sy, i=i):
            P.w(sy, "cV")
            P.dma(sy, "dA", xbuf, L["x_in"][i * 128:(i + 1) * 128, :])
        ln_tile(P, xbuf, True, xnst)
        transpose16(P, lambda jj: xnst[:, jj * 128:jj * 128 + 128],
                    lambda jj, i=i: bv(SLOT_A + jj * 2048 + i * 128, 128))

    if L.get("stage") == "ln":
        return
    zl = fv(TAIL, 1024)
    zr = fv(TAIL + 2048, 1024)
    for ot in range(NT):
        for th in range(2):
            half = (2 * ot + th) % 2

            @sec(P, "tensor")
            def _(t, ot=ot, th=th, half=half):
                if ot == 0 and th == 0:
                    P.w(t, "cV")
                if half in ev_cnt:
                    P.w(t, *ev_cnt[half])
                pa = PA[:, half * 1024:half * 1024 + 1024]
                for n in range(2):
                    for k in range(NT):
                        P.inc(t.matmul(pa[:, n * 512:(n + 1) * 512],
                                       bv(SLOT_B + k * 2048 + ot * 128, 128),
                                       bv(SLOT_A + k * 2048 + th * 1024 + n * 512, 512),
                                       start=(k == 0), stop=(k == NT - 1)),
                              "cT", 1 if (k == NT - 1 and n == 1) else 0)

            @sec(P, "scalar")
            def _(s, ot=ot, half=half):
                P.w(s, "cT")
                P.w(s, "cV")
                pa = PA[:, half * 1024:half * 1024 + 1024]
                s.activation(out=zl, in_=pa, func=AF.Abs, bias=L["bw_sb"][:, ot:ot + 1])
                s.activation(out=zl, in_=zl, func=AF.Exp, scale=-1.0)
                P.inc(s.activation(out=zl, in_=zl, func=AF.Ln, bias=1.0), "cS")

            @sec(P, "vector")
            def _(v, ot=ot, half=half):
                P.w(v, "cS")
                P.w(v, "dA")
                pa = PA[:, half * 1024:half * 1024 + 1024]
                v.tensor_scalar(out=zr, in0=pa, scalar1=L["bw_sb"][:, ot:ot + 1],
                                scalar2=0.0, op0=OP.add, op1=OP.max)
                P.inc(v.scalar_tensor_tensor(out=zr, in0=zr, scalar=-1.0, in1=zl,
                                             op0=OP.mult, op1=OP.subtract), "cV")
            ev_cnt[half] = ("cV", P.C["cV"])

            @sec(P, "sync")
            def _(sy, ot=ot, th=th):
                P.w(sy, "cV")
                P.dma(sy, "dA",
                      L["zln_d"].ap()[th * 8:(th + 1) * 8, :, ot * 128:ot * 128 + 128]
                          .rearrange("c p n -> p c n"),
                      zr.rearrange("p (c n) -> p c n", n=128))

    if L.get("stage") == "z":
        return
    kst = [bv(SLOT_B, 2048), bv(SLOT_B + 2048, 2048)]
    vst = [bv(SLOT_B + 4096, 512), bv(SLOT_B + 4608, 512)]

    for wname, kind in [("aWk", "k"), ("aWv", "v"), ("aWr", "r")]:
        for q in range(4):
            slot, pw = next_slot(wname if q == 0 else None)
            dbc = wt_quarter_load(P, slot, L["wbf"][wname], q, pre_wait=pw)

            if kind in ("k", "r"):
                for oo in range(4):
                    ot = q * 4 + oo
                    stg = kst[oo % 2]
                    for th in range(2):
                        half = th

                        @sec(P, "tensor")
                        def _(t, slot=slot, dbc=dbc, oo=oo, th=th, half=half, q=q):
                            P.w(t, "dA", dbc)
                            if q == 0 and oo == 0 and th == 0:
                                P.w(t, "cV")
                                P.w(t, "cS")
                            if half in ev_cnt:
                                P.w(t, *ev_cnt[half])
                            pa = PA[:, half * 1024:half * 1024 + 1024]
                            for n in range(2):
                                for k in range(NT):
                                    P.inc(t.matmul(pa[:, n * 512:(n + 1) * 512],
                                                   wq(slot, k)[:, oo * 128:oo * 128 + 128],
                                                   bv(SLOT_A + k * 2048 + th * 1024 + n * 512, 512),
                                                   start=(k == 0), stop=(k == NT - 1)),
                                          "cT", 1 if (k == NT - 1 and n == 1) else 0)

                        if kind == "k":
                            @sec(P, "vector")
                            def _(v, th=th, half=half, stg=stg):
                                P.w(v, "cT")
                                P.w(v, "dA")
                                P.inc(v.tensor_copy(out=stg[:, th * 1024:(th + 1) * 1024],
                                                    in_=PA[:, half * 1024:half * 1024 + 1024]), "cV")
                            ev_cnt[half] = ("cV", P.C["cV"])
                        else:
                            @sec(P, "scalar")
                            def _(s, th=th, half=half, stg=stg):
                                P.w(s, "cT")
                                P.w(s, "dA")
                                P.inc(s.activation(out=stg[:, th * 1024:(th + 1) * 1024],
                                                   in_=PA[:, half * 1024:half * 1024 + 1024],
                                                   func=AF.Sigmoid), "cS")
                            ev_cnt[half] = ("cS", P.C["cS"])

                    dstd = L["kT_d"] if kind == "k" else L["rT_d"]

                    @sec(P, "sync")
                    def _(sy, ot=ot, dstd=dstd, kind=kind, stg=stg):
                        P.w(sy, "cV" if kind == "k" else "cS")
                        P.dma(sy, "dA",
                              dstd.ap()[:, :, ot * 128:ot * 128 + 128]
                                  .rearrange("c p n -> p c n"),
                              stg.rearrange("p (c n) -> p c n", n=128))
            else:
                for tt in range(NT):
                    pq = tt % 2
                    stg = vst[tt % 2]

                    @sec(P, "tensor")
                    def _(t, slot=slot, dbc=dbc, tt=tt, pq=pq, q=q):
                        P.w(t, "dA", dbc)
                        if q == 0 and tt == 0:
                            P.w(t, "cV")
                            P.w(t, "cS")
                        key = ("vq", pq)
                        if key in ev_cnt:
                            P.w(t, *ev_cnt[key])
                        pa = PA[:, 1024 + pq * 512:1024 + pq * 512 + 512]
                        for k in range(NT):
                            P.inc(t.matmul(pa, bv(SLOT_A + k * 2048 + tt * 128, 128),
                                           wq(slot, k), start=(k == 0), stop=(k == NT - 1)),
                                  "cT", 1 if k == NT - 1 else 0)

                    @sec(P, "vector")
                    def _(v, tt=tt, pq=pq, stg=stg):
                        P.w(v, "cT")
                        P.w(v, "dA")
                        P.inc(v.tensor_copy(out=stg, in_=PA[:, 1024 + pq * 512:1024 + pq * 512 + 512]), "cV")
                    ev_cnt[("vq", pq)] = ("cV", P.C["cV"])

                    @sec(P, "sync")
                    def _(sy, tt=tt, q=q, stg=stg):
                        P.w(sy, "cV")
                        P.dma(sy, "dA", L["vN_d"].ap()[tt, :, q * 512:q * 512 + 512], stg)

    if L.get("stage") == "kvr":
        return
    emit_attn_a_rec(P)
    if L.get("stage") == "rec":
        return
    emit_attn_a_out(P)


RB_LNA = SLOT_B
RB_CT = SLOT_B + 4096
RB_KT = SLOT_B + 8192
RB_RT = SLOT_B + 10240
RB_VN = SLOT_B + 12288
RB_RT2 = SLOT_B + 14336
RB_KHT = SLOT_B + 16384
RB_KHN = SLOT_B + 18432
RB_MARG = SLOT_B + 20480
RB_EB = SLOT_B + 24576
RB_KB = SLOT_B + 26624
RB_RH = SLOT_B + 28672
RB_ASB = SLOT_B + 28928
RB_ZERO = SLOT_B + 30976


def emit_attn_a_rec(P):
    L = P.L
    lvl = L.get("rec_lvl", 7)
    nch = L.get("rec_nch", NCH)
    nsub = L.get("rec_nsub", NSUB)
    nhf = L.get("rec_nhf", 2)
    bv, fv = L["bv"], L["fv"]
    PA, PB = L["PA"], L["PB"]
    lna = fv(RB_LNA, 2048)
    cT = fv(RB_CT, 2048)
    cT3 = cT.rearrange("p (j s) -> p j s", s=128)
    kT = bv(RB_KT, 2048)
    rT = bv(RB_RT, 2048)
    rT3 = rT.rearrange("p (j s) -> p j s", s=128)
    vN = bv(RB_VN, 2048)
    rt2 = bv(RB_RT2, 2048)
    khT = bv(RB_KHT, 2048)
    khN = bv(RB_KHN, 2048)
    marg = fv(RB_MARG, 2048)
    marg3 = marg.rearrange("p (j s) -> p j s", s=128)
    Eb = bv(RB_EB, 2048)
    kb = bv(RB_KB, 2048)
    rh = bv(RB_RH, 256)
    Asb = bv(RB_ASB, 2048)
    zero = fv(RB_ZERO, 128)
    cL_b = cT3[:, :, 127:128].broadcast_to((128, NT, 128))

    @sec(P, "vector")
    def _(v):
        P.w(v, "cS")
        P.w(v, "dA")
        P.inc(v.memset(zero, 0.0), "cV")

    for c in range(nch):
        @sec(P, "sync")
        def _(sy, c=c):
            P.w(sy, "cT")
            P.dma(sy, "dA", lna, L["zln_d"].ap()[c])
            P.dma(sy, "dA", kT, L["kT_d"].ap()[c])
            P.dma(sy, "dA", rT, L["rT_d"].ap()[c])
            P.dma(sy, "dA", vN, L["vN_d"].ap()[c])

        @sec(P, "vector")
        def _(v):
            P.w(v, "dA")
            for j in range(NT):
                P.inc(v.tensor_tensor_scan(out=cT[:, j * 128:(j + 1) * 128],
                                           data0=lna[:, j * 128:(j + 1) * 128],
                                           data1=zero, initial=0.0,
                                           op0=OP.add, op1=OP.add),
                      "cV", 1 if j == NT - 1 else 0)

        if lvl < 2:
            continue
        @sec(P, "scalar")
        def _(s):
            P.w(s, "cV")
            P.inc(s.activation(out=Eb, in_=cT, func=AF.Exp), "cS")
            P.inc(s.activation(out=L["eL_sb"][:].rearrange("p (j o) -> p j o", o=1),
                               in_=cT3[:, :, 127:128], func=AF.Exp), "cS")

        @sec(P, "vector")
        def _(v):
            P.w(v, "cS")
            v.tensor_tensor(out=rt2, in0=rT, in1=Eb, op=OP.mult)
            P.inc(v.tensor_tensor(out=marg3, in0=cL_b, in1=cT3, op=OP.subtract), "cV")

        @sec(P, "scalar")
        def _(s):
            P.w(s, "cV")
            P.inc(s.activation(out=Eb, in_=marg, func=AF.Exp), "cS")

        @sec(P, "vector")
        def _(v):
            P.w(v, "cS")
            P.inc(v.tensor_tensor(out=khT, in0=kT, in1=Eb, op=OP.mult), "cV")

        transpose16(P, lambda jj: khT[:, jj * 128:jj * 128 + 128],
                    lambda jj: khN[:, jj * 128:jj * 128 + 128])
        if lvl < 3:
            continue

        for hf in range(nhf):
            j0 = hf * 8
            for i in range(nsub):
                @sec(P, "vector")
                def _(v, i=i, j0=j0):
                    P.w(v, "cS")
                    mh = marg[:, j0 * 128:(j0 + 8) * 128]
                    mh3 = marg3[:, j0:j0 + 8, :]
                    ch = cT[:, j0 * 128:(j0 + 8) * 128]
                    ch3 = cT3[:, j0:j0 + 8, :]
                    if i == 0:
                        P.inc(v.tensor_scalar(out=mh, in0=ch, scalar1=-80.0, scalar2=None,
                                              op0=OP.max), "cV")
                    else:
                        bview = cT3[:, j0:j0 + 8, 16 * i - 1:16 * i].broadcast_to((128, 8, 128))
                        v.tensor_tensor(out=mh3, in0=ch3, in1=bview, op=OP.subtract)
                        P.inc(v.tensor_scalar(out=mh, in0=mh, scalar1=-80.0, scalar2=None,
                                              op0=OP.max), "cV")

                @sec(P, "scalar")
                def _(s, i=i, j0=j0):
                    P.w(s, "cV")
                    P.inc(s.activation(out=Eb[:, j0 * 128:(j0 + 8) * 128],
                                       in_=marg[:, j0 * 128:(j0 + 8) * 128],
                                       func=AF.Exp, scale=-1.0), "cS")
                    P.inc(s.activation(
                        out=rh[:, 0:128].rearrange("p (j s) -> p j s", s=16),
                        in_=marg3[:, j0:j0 + 8, 16 * i:16 * i + 16],
                        func=AF.Exp), "cS")

                @sec(P, "vector")
                def _(v, i=i, j0=j0):
                    P.w(v, "cS")
                    P.w(v, "cT")
                    v.tensor_tensor(out=kb[:, j0 * 128:(j0 + 8) * 128],
                                    in0=kT[:, j0 * 128:(j0 + 8) * 128],
                                    in1=Eb[:, j0 * 128:(j0 + 8) * 128], op=OP.mult)
                    P.inc(v.tensor_tensor(
                        out=rh[:, 128:256].rearrange("p (j s) -> p j s", s=16),
                        in0=rT3[:, j0:j0 + 8, 16 * i:16 * i + 16],
                        in1=rh[:, 0:128].rearrange("p (j s) -> p j s", s=16),
                        op=OP.mult), "cV")

                if lvl < 4:
                    continue

                @sec(P, "tensor")
                def _(t, i=i, j0=j0):
                    P.w(t, "cV")
                    for jj in range(8):
                        for hp in range(2):
                            hh = 2 * jj + hp
                            P.w(t, "cT")
                            P.inc(t.matmul(
                                PA[:, hh * 128 + 16 * i: hh * 128 + 16 * (i + 1)],
                                kb[hp * 64:hp * 64 + 64,
                                   (j0 + jj) * 128:(j0 + jj) * 128 + 128],
                                rh[hp * 64:hp * 64 + 64, 128 + jj * 16:128 + jj * 16 + 16],
                                start=True, stop=True), "cT", 1)

            if lvl < 5:
                continue
            @sec(P, "vector")
            def _(v, j0=j0):
                P.w(v, "cT")
                for hh in range(16):
                    P.inc(v.tensor_tensor(out=Asb[:, hh * 128:(hh + 1) * 128],
                                          in0=PA[:, hh * 128:(hh + 1) * 128],
                                          in1=L["triu_sb"][:], op=OP.mult),
                          "cV", 1 if hh == 15 else 0)

            if lvl < 6:
                continue
            @sec(P, "tensor")
            def _(t, j0=j0, hf=hf, c=c):
                P.w(t, "cV")
                if c == 0 and hf == 0:
                    P.w(t, "cG")
                for pr in range(8):
                    jj = j0 + pr
                    for hp in range(2):
                        h = 2 * jj + hp
                        hh = 2 * pr + hp
                        ps = PB[hp * 64:hp * 64 + 64, pr * 128:pr * 128 + 128]
                        P.w(t, "cT")
                        P.inc(t.matmul(ps, vN[:, h * 64:h * 64 + 64],
                                       Asb[:, hh * 128:(hh + 1) * 128],
                                       start=True, stop=False), "cT", 1)
                        P.w(t, "cT")
                        P.inc(t.matmul(ps,
                                       L["state_bf"][hp * 64:hp * 64 + 64, jj * 64:jj * 64 + 64],
                                       rt2[hp * 64:hp * 64 + 64, jj * 128:jj * 128 + 128],
                                       start=False, stop=True), "cT", 1)

            @sec(P, "vector")
            def _(v, j0=j0, c=c):
                P.w(v, "cT")
                for pr in range(8):
                    jj = j0 + pr
                    P.inc(v.tensor_copy(out=bv(SLOT_A + jj * 2048 + c * 128, 128),
                                        in_=PB[:, pr * 128:pr * 128 + 128]),
                          "cV", 1 if pr == 7 else 0)

        if lvl < 7:
            continue
        @sec(P, "tensor")
        def _(t, c=c):
            P.w(t, "cV")
            for pr in range(8):
                for hp in range(2):
                    h = 2 * pr + hp
                    P.w(t, "cT")
                    P.inc(t.matmul(PB[hp * 64:hp * 64 + 64, pr * 64:pr * 64 + 64],
                                   khN[:, h * 64:h * 64 + 64],
                                   vN[:, h * 64:h * 64 + 64],
                                   start=True, stop=True), "cT", 1)
            for pr in range(8, 16):
                for hp in range(2):
                    h = 2 * pr + hp
                    P.w(t, "cT")
                    P.inc(t.matmul(PA[hp * 64:hp * 64 + 64, (pr - 8) * 64:(pr - 8) * 64 + 64],
                                   khN[:, h * 64:h * 64 + 64],
                                   vN[:, h * 64:h * 64 + 64],
                                   start=True, stop=True), "cT", 1)

        @sec(P, "vector")
        def _(v, c=c):
            P.w(v, "cT")
            for j in range(NT):
                v.tensor_scalar(out=L["state_sb"][:, j * 64:(j + 1) * 64],
                                in0=L["state_sb"][:, j * 64:(j + 1) * 64],
                                scalar1=L["eL_sb"][:, j:j + 1], scalar2=None, op0=OP.mult)
                src = PB[:, j * 64:(j + 1) * 64] if j < 8 else PA[:, (j - 8) * 64:(j - 8) * 64 + 64]
                v.tensor_tensor(out=L["state_sb"][:, j * 64:(j + 1) * 64],
                                in0=L["state_sb"][:, j * 64:(j + 1) * 64],
                                in1=src, op=OP.add)
                P.inc(v.tensor_copy(out=L["state_bf"][:, j * 64:(j + 1) * 64],
                                    in_=L["state_sb"][:, j * 64:(j + 1) * 64]),
                      "cV", 1 if j == NT - 1 else 0)


def emit_attn_a_out(P):
    L = P.L
    bv, fv = L["bv"], L["fv"]
    PA = L["PA"]

    xb = [fv(SLOT_B, 512), fv(SLOT_B + 1024, 512)]
    yb = [fv(SLOT_B + 2048, 512), fv(SLOT_B + 3072, 512)]

    ev = {}
    for o2q in range(4):
        pw = [("dW", P.wcount["aWo"]), ("cT", P.C["cT"])]
        slot = o2q % 3

        @sec(P, "sync")
        def _(sy, pw=pw):
            for args in pw:
                P.w(sy, *args)
        dbc = wt_quarter_load(P, slot, L["wbf"]["aWo"], o2q)
        for tt in range(NT):
            pq = tt % 2

            @sec(P, "sync")
            def _(sy, tt=tt, o2q=o2q, pq=pq):
                P.w(sy, "cV")
                P.dma(sy, "dA", xb[pq],
                      L["x_in"][tt * 128:(tt + 1) * 128, o2q * 512:o2q * 512 + 512])

            @sec(P, "tensor")
            def _(t, tt=tt, pq=pq, dbc=dbc, o2q=o2q, slot=slot):
                P.w(t, "dA", dbc)
                if o2q == 0 and tt == 0:
                    P.w(t, "cV")
                if pq in ev:
                    P.w(t, *ev[pq])
                pa = PA[:, pq * 512:pq * 512 + 512]
                for k in range(NT):
                    P.inc(t.matmul(pa, bv(SLOT_A + k * 2048 + tt * 128, 128),
                                   wq_slot(L, slot, k), start=(k == 0), stop=(k == NT - 1)),
                          "cT", 1 if k == NT - 1 else 0)

            @sec(P, "vector")
            def _(v, tt=tt, pq=pq, o2q=o2q):
                P.w(v, "cT")
                P.w(v, "dA")
                P.inc(v.tensor_tensor(out=yb[pq], in0=PA[:, pq * 512:pq * 512 + 512],
                                      in1=xb[pq], op=OP.add), "cV")
            ev[pq] = ("cV", P.C["cV"])

            @sec(P, "sync")
            def _(sy, tt=tt, pq=pq, o2q=o2q):
                P.w(sy, "cV")
                P.dma(sy, "dA", L["resid_d"].ap()[tt * 128:(tt + 1) * 128,
                                                  o2q * 512:o2q * 512 + 512], yb[pq])


def wq_slot(L, slot, j):
    return L["bv"](SLOT_C + slot * 8192 + j * 512, 512)


FC_XNT = 0
FC_RING = 16384
FC_G = 40960
FC_GU0 = 47104
FC_GU1 = 53248
FC_FW = 59392
FC_OUT = 63488
FC_RST = 96256
FL_XB = SLOT_C + 8192
FL_XN = SLOT_C + 16384
FL_XNT = SLOT_C + 18432
FL_GW = 16384
FL_FWT = 20480


def emit_ffn(P, call):
    L = P.L
    bv, fv = L["bv"], L["fv"]
    PA, PB = L["PA"], L["PB"]
    rt = L["rt_sb"]
    sc = L["sc_sb"]
    phase_fence(P)

    if call == 0:
        gwst = bv(FL_GW, 2048)

        @sec(P, "gpsimd")
        def _(g):
            P.w(g, "cT")
            P.dma(g, "dW", gwst[0:NEXP, :], L["gate_w"])
        gwc = P.C["dW"]

        @sec(P, "vector")
        def _(v):
            v.tensor_copy(out=L["identf_sb"][:], in_=L["identb_sb"][:])

        for h in range(2):
            pt = L["PT0"] if h % 2 == 0 else L["PT1"]

            @sec(P, "tensor")
            def _(t, h=h, pt=pt):
                P.w(t, "dW", gwc)
                P.w(t, "cV")
                for j in range(8):
                    jj = h * 8 + j
                    P.w(t, "cT")
                    P.inc(t.transpose(pt[:, j * 128:j * 128 + 32],
                                      gwst[0:NEXP, jj * 128:jj * 128 + 128],
                                      L["identb_sb"][0:NEXP, 0:NEXP]), "cT", 1)

            @sec(P, "vector")
            def _(v, h=h, pt=pt):
                P.w(v, "cT")
                for j in range(8):
                    jj = h * 8 + j
                    v.tensor_copy(out=L["gwT_sb"][:, jj * 32:(jj + 1) * 32],
                                  in_=pt[:, j * 128:j * 128 + 32])

    xbuf = fv(FL_XB, 2048)
    xnst = bv(FL_XN, 2048)
    xntst = bv(FL_XNT, 2048)
    for i in range(NT):
        @sec(P, "sync")
        def _(sy, i=i):
            P.w(sy, "cV")
            P.w(sy, "cT")
            P.dma(sy, "dA", xbuf, L["resid_d"].ap()[i * 128:(i + 1) * 128, :])
        ln_tile(P, xbuf, False, xnst)
        transpose16(P, lambda jj: xnst[:, jj * 128:jj * 128 + 128],
                    lambda jj: xntst[:, jj * 128:jj * 128 + 128])

        @sec(P, "sync")
        def _(sy, i=i):
            P.w(sy, "cV")
            P.dma(sy, "dA", L["xnT_d"].ap()[i], xntst)

        @sec(P, "tensor")
        def _(t, i=i):
            P.w(t, "cV")
            for k in range(NT):
                P.w(t, "cT")
                P.inc(t.matmul(PB[:, 0:32], xntst[:, k * 128:k * 128 + 128],
                               L["gwT_sb"][:, k * 32:(k + 1) * 32],
                               start=(k == 0), stop=(k == NT - 1)), "cT", 1)

        @sec(P, "vector")
        def _(v, i=i):
            P.w(v, "cT")
            v.tensor_copy(out=rt[:, 0:32], in_=PB[:, 0:32])
            v.tensor_reduce(out=sc[:, 8:9], in_=rt[:, 0:32], axis=AX.X, op=OP.max)
            v.tensor_scalar(out=sc[:, 8:9], in0=sc[:, 8:9], scalar1=-1.0, scalar2=None, op0=OP.mult)

        @sec(P, "scalar")
        def _(s, i=i):
            P.w(s, "cV")
            s.activation(out=rt[:, 32:64], in_=rt[:, 0:32], func=AF.Exp,
                         bias=sc[:, 8:9], accum_out=sc[:, 9:10])

        @sec(P, "vector")
        def _(v, i=i):
            P.w(v, "cS")
            v.reciprocal(out=sc[:, 10:11], in_=sc[:, 9:10])
            v.tensor_scalar(out=rt[:, 0:32], in0=rt[:, 32:64], scalar1=sc[:, 10:11],
                            scalar2=None, op0=OP.mult)
            v.tensor_copy(out=rt[:, 32:64], in_=rt[:, 0:32])
            for kk in range(TOPK):
                v.tensor_reduce(out=sc[:, 11:12], in_=rt[:, 32:64], axis=AX.X, op=OP.max)
                if kk < TOPK - 1:
                    v.tensor_scalar(out=rt[:, 64:96], in0=rt[:, 32:64], scalar1=sc[:, 11:12],
                                    scalar2=None, op0=OP.is_ge)
                    v.scalar_tensor_tensor(out=rt[:, 32:64], in0=rt[:, 64:96], scalar=-2.0,
                                           in1=rt[:, 32:64], op0=OP.mult, op1=OP.add)
            v.tensor_scalar(out=rt[:, 64:96], in0=rt[:, 0:32], scalar1=sc[:, 11:12],
                            scalar2=None, op0=OP.is_ge)
            v.tensor_tensor(out=rt[:, 32:64], in0=rt[:, 0:32], in1=rt[:, 64:96], op=OP.mult)
            v.tensor_reduce(out=sc[:, 12:13], in_=rt[:, 32:64], axis=AX.X, op=OP.add)
            v.tensor_scalar(out=sc[:, 12:13], in0=sc[:, 12:13], scalar1=1e-9, scalar2=None, op0=OP.max)
            v.reciprocal(out=sc[:, 13:14], in_=sc[:, 12:13])
            v.tensor_scalar(out=L["fw_sb"][:, i * 32:(i + 1) * 32], in0=rt[:, 32:64],
                            scalar1=sc[:, 13:14], scalar2=None, op0=OP.mult)

    fwtst = fv(FL_FWT, 2048)
    for i in range(NT):
        @sec(P, "tensor")
        def _(t, i=i):
            P.w(t, "cV")
            P.w(t, "cT")
            P.inc(t.transpose(PB[0:NEXP, 128:256],
                              L["fw_sb"][:, i * 32:(i + 1) * 32], L["identf_sb"][:]), "cT", 1)

        @sec(P, "vector")
        def _(v, i=i):
            P.w(v, "cT")
            v.tensor_copy(out=fwtst[0:NEXP, i * 128:(i + 1) * 128], in_=PB[0:NEXP, 128:256])

    @sec(P, "sync")
    def _(sy):
        P.w(sy, "cV")
        P.dma(sy, "dA", L["fwT_d"].ap()[:, :], fwtst[0:NEXP, :])
    fwt_stored = P.C["dA"]

    ring_c = {"i": 0, "busy": {}}
    WSTRIDE = {"wg": 768, "wu": 768, "wd": 2048}

    def wload(name, e, first=False):
        slot = ring_c["i"] % 2
        ring_c["i"] += 1
        busy = ring_c["busy"].get(slot)

        @sec(P, "gpsimd")
        def _(g, name=name, e=e, slot=slot, busy=busy, first=first):
            if first:
                P.w(g, "dA", fwt_stored)
                P.w(g, "cV")
                P.w(g, "cT")
            P.w(g, "dW", P.wcount[f"ffn{e}"])
            if busy is not None:
                P.w(g, "cT", busy)
            wi = {"wg": 768, "wu": 768, "wd": 2048}[name]
            P.dma(g, "dW",
                  bv(FC_RING + slot * 12288, 12288).rearrange("p (s i) -> p s i", i=wi),
                  L["wbf"][name].ap()[e].rearrange("(s p) i -> p s i", p=128))
        return slot, P.C["dW"]

    ev = {}
    for ch in range(2):
        @sec(P, "sync")
        def _(sy, ch=ch):
            P.w(sy, "cT")
            for j in range(NT):
                P.dma(sy, "dA", bv(FC_XNT + j * 1024, 1024),
                      L["xnT_d"].ap()[ch * 8:(ch + 1) * 8, :, j * 128:(j + 1) * 128]
                          .rearrange("c p n -> p c n"), gate=(j == 0))
        xnt_done = P.C["dA"]

        for pair in range(16):
            for sub, guoff in ((0, FC_GU0), (1, FC_GU1)):
                e = 2 * pair + sub
                sg, dg = wload("wg", e, first=(ch == 0 and pair == 0 and sub == 0))

                @sec(P, "sync")
                def _(sy, e=e, ch=ch, sub=sub):
                    P.w(sy, "cV")
                    P.dma(sy, "dA", fv(FC_FW + sub * 2048, 1024),
                          bass.AP(L["fwT_d"], e * T + ch * 1024, [[0, 128], [1, 1024]]))
                dfw = P.C["dA"]

                for it in range(NIT):
                    @sec(P, "tensor")
                    def _(t, it=it, dg=dg, sg=sg):
                        if it == 0:
                            P.w(t, "dW", dg)
                            P.w(t, "dA", xnt_done)
                        if "pa01" in ev:
                            P.w(t, *ev["pa01"])
                        for n in range(2):
                            for k in range(NT):
                                P.inc(t.matmul(PA[:, n * 512:(n + 1) * 512],
                                               bv(FC_RING + sg * 12288 + k * 768 + it * 128, 128),
                                               bv(FC_XNT + k * 1024 + n * 512, 512),
                                               start=(k == 0), stop=(k == NT - 1)),
                                      "cT", 1 if (k == NT - 1 and n == 1) else 0)

                    @sec(P, "scalar")
                    def _(s, it=it):
                        P.w(s, "cT")
                        P.w(s, "cV")
                        s.activation(out=bv(FC_G + it * 1024, 1024), in_=PA[:, 0:1024],
                                     func=AF.Silu)
                    ev["pa01"] = ("cS", P.C["cS"])
                ring_c["busy"][sg] = P.C["cT"]

                su, du = wload("wu", e)
                for it in range(NIT):
                    @sec(P, "tensor")
                    def _(t, it=it, du=du, su=su):
                        if it == 0:
                            P.w(t, "dW", du)
                        if "pa23" in ev:
                            P.w(t, *ev["pa23"])
                        for n in range(2):
                            for k in range(NT):
                                P.inc(t.matmul(PA[:, 1024 + n * 512:1024 + (n + 1) * 512],
                                               bv(FC_RING + su * 12288 + k * 768 + it * 128, 128),
                                               bv(FC_XNT + k * 1024 + n * 512, 512),
                                               start=(k == 0), stop=(k == NT - 1)),
                                      "cT", 1 if (k == NT - 1 and n == 1) else 0)

                    @sec(P, "vector")
                    def _(v, it=it, sub=sub, guoff=guoff, dfw=dfw):
                        P.w(v, "cT")
                        P.w(v, "dA", dfw)
                        P.w(v, "cS")
                        gu = bv(guoff + it * 1024, 1024)
                        v.tensor_tensor(out=gu, in0=PA[:, 1024:2048],
                                        in1=fv(FC_FW + sub * 2048, 1024), op=OP.mult)
                        v.tensor_tensor(out=gu, in0=gu, in1=bv(FC_G + it * 1024, 1024),
                                        op=OP.mult)
                    ev["pa23"] = ("cV", P.C["cV"])
                ring_c["busy"][su] = P.C["cT"]

            dslots = [wload("wd", 2 * pair)[0:2], wload("wd", 2 * pair + 1)[0:2]]

            for tt in range(8):
                for hh2 in range(2):
                    use_pb = (tt * 2 + hh2) % 2 == 0
                    ps = PB[:, 0:1024] if use_pb else PA[:, 0:1024]
                    key = "pb" if use_pb else "pa01"

                    @sec(P, "tensor")
                    def _(t, tt=tt, hh2=hh2, ps=ps, key=key, dslots=dslots):
                        if tt == 0 and hh2 == 0:
                            P.w(t, "dW", dslots[1][1])
                            P.w(t, "cV")
                        if key in ev:
                            P.w(t, *ev[key])
                        for sub, (sd, _dc) in enumerate(dslots):
                            guoff = FC_GU0 if sub == 0 else FC_GU1
                            for it in range(NIT):
                                for n in range(2):
                                    P.inc(t.matmul(
                                        ps[:, n * 512:(n + 1) * 512],
                                        bv(guoff + it * 1024 + tt * 128, 128),
                                        bv(FC_RING + sd * 12288 + it * 2048
                                           + hh2 * 1024 + n * 512, 512),
                                        start=(sub == 0 and it == 0),
                                        stop=(sub == 1 and it == NIT - 1)),
                                        "cT", 1 if (sub == 1 and it == NIT - 1 and n == 1) else 0)

                    @sec(P, "vector")
                    def _(v, tt=tt, hh2=hh2, ps=ps, pair=pair):
                        P.w(v, "cT")
                        dst = fv(FC_OUT + tt * 4096 + hh2 * 2048, 1024)
                        if pair == 0:
                            v.tensor_copy(out=dst, in_=ps)
                        else:
                            v.tensor_tensor(out=dst, in0=dst, in1=ps, op=OP.add)
                    ev[key] = ("cV", P.C["cV"])
            for sd, _dc in dslots:
                ring_c["busy"][sd] = P.C["cT"]

        for tt in range(8):
            gt = ch * 8 + tt

            @sec(P, "sync")
            def _(sy, gt=gt):
                P.w(sy, "cV")
                P.dma(sy, "dA", fv(FC_RST, 2048), L["resid_d"].ap()[gt * 128:(gt + 1) * 128, :])
            rld = P.C["dA"]

            @sec(P, "vector")
            def _(v, tt=tt, rld=rld):
                P.w(v, "dA", rld)
                dst = fv(FC_OUT + tt * 4096, 2048)
                v.tensor_tensor(out=dst, in0=dst, in1=fv(FC_RST, 2048), op=OP.add)

            @sec(P, "sync")
            def _(sy, gt=gt, tt=tt):
                P.w(sy, "cV")
                P.dma(sy, "dA", L["resid_d"].ap()[gt * 128:(gt + 1) * 128, :],
                      fv(FC_OUT + tt * 4096, 2048))


BB_HST = 0
BB_QT = 32768
BB_KT = 65536
BB_VNO = 73728
BB_RING = 82048
BB_PT = 0
BB_CTXT = 16384
BB_STST = 82048
BB_CTXST = 86144
BB_RST = 86656


def emit_attn_b(P):
    L = P.L
    bv, fv = L["bv"], L["fv"]
    PA, PB = L["PA"], L["PB"]
    sc = L["sc_sb"]
    phase_fence(P)

    xbuf = fv(BB_QT, 2048)
    xnst = bv(BB_QT + 8192, 2048)
    for i in range(NT):
        @sec(P, "sync")
        def _(sy, i=i):
            P.w(sy, "cV")
            P.w(sy, "cT")
            P.dma(sy, "dA", xbuf, L["resid_d"].ap()[i * 128:(i + 1) * 128, :])
        ln_tile(P, xbuf, False, xnst)
        transpose16(P, lambda jj: xnst[:, jj * 128:jj * 128 + 128],
                    lambda jj, i=i: bv(BB_HST + jj * 2048 + i * 128, 128))

    ring_c = {"i": 0, "busy": {}}

    def bload(name, colrange, first=False):
        slot = ring_c["i"] % 2
        ring_c["i"] += 1
        busy = ring_c["busy"].get(slot)
        c0, cn = colrange

        @sec(P, "gpsimd")
        def _(g, name=name, slot=slot, busy=busy, first=first, c0=c0, cn=cn):
            if first:
                P.w(g, "cV")
                P.w(g, "cT")
            P.w(g, "dW", P.wcount[name])
            if busy is not None:
                P.w(g, "cT", busy)
            P.dma(g, "dW",
                  bv(BB_RING + slot * 8192, 16 * cn).rearrange("p (s o) -> p s o", o=cn),
                  L["wbf"][name].ap()[:, c0:c0 + cn].rearrange("(s p) o -> p s o", p=128))
        return slot, P.C["dW"]

    def rsl(slot, k, cn=512):
        return bv(BB_RING + slot * 8192 + k * cn, cn)

    @sec(P, "gpsimd")
    def _(g):
        P.w(g, "cT")
        P.w(g, "cV")
        g.memset(bv(BB_VNO, 8256), 1.0)
    vno_ones = P.C["cG"]

    ev = {}
    for q in range(4):
        slot, dq = bload("bWq", (q * 512, 512), first=(q == 0))
        for oo in range(4):
            ot = q * 4 + oo
            for th in range(2):
                half = th

                @sec(P, "tensor")
                def _(t, slot=slot, dq=dq, oo=oo, th=th, half=half, q=q):
                    P.w(t, "dW", dq)
                    if q == 0 and oo == 0 and th == 0:
                        P.w(t, "cV")
                        P.w(t, "cS")
                    if ("q", half) in ev:
                        P.w(t, *ev[("q", half)])
                    pa = PA[:, half * 1024:half * 1024 + 1024]
                    for n in range(2):
                        for k in range(NT):
                            P.inc(t.matmul(pa[:, n * 512:(n + 1) * 512],
                                           rsl(slot, k)[:, oo * 128:oo * 128 + 128],
                                           bv(BB_HST + k * 2048 + th * 1024 + n * 512, 512),
                                           start=(k == 0), stop=(k == NT - 1)),
                                  "cT", 1 if (k == NT - 1 and n == 1) else 0)

                @sec(P, "vector")
                def _(v, ot=ot, th=th, half=half):
                    P.w(v, "cT")
                    v.tensor_scalar(out=bv(BB_QT + ot * 2048 + th * 1024, 1024),
                                    in0=PA[:, half * 1024:half * 1024 + 1024],
                                    scalar1=SQRT_HD, scalar2=None, op0=OP.mult)
                ev[("q", half)] = ("cV", P.C["cV"])
        ring_c["busy"][slot] = P.C["cT"]

    slot, dk = bload("bWk", (0, 512))
    for oo in range(4):
        for th in range(2):
            half = th

            @sec(P, "tensor")
            def _(t, slot=slot, dk=dk, oo=oo, th=th, half=half):
                P.w(t, "dW", dk)
                if ("q", half) in ev:
                    P.w(t, *ev[("q", half)])
                pa = PA[:, half * 1024:half * 1024 + 1024]
                for n in range(2):
                    for k in range(NT):
                        P.inc(t.matmul(pa[:, n * 512:(n + 1) * 512],
                                       rsl(slot, k)[:, oo * 128:oo * 128 + 128],
                                       bv(BB_HST + k * 2048 + th * 1024 + n * 512, 512),
                                       start=(k == 0), stop=(k == NT - 1)),
                              "cT", 1 if (k == NT - 1 and n == 1) else 0)

            @sec(P, "vector")
            def _(v, oo=oo, th=th, half=half):
                P.w(v, "cT")
                v.tensor_copy(out=bv(BB_KT + oo * 2048 + th * 1024, 1024),
                              in_=PA[:, half * 1024:half * 1024 + 1024])
            ev[("q", half)] = ("cV", P.C["cV"])
    ring_c["busy"][slot] = P.C["cT"]

    slot, dv = bload("bWv", (0, 512))
    for tt in range(NT):
        pq = tt % 2

        @sec(P, "tensor")
        def _(t, slot=slot, dv=dv, tt=tt, pq=pq):
            P.w(t, "dW", dv)
            if tt == 0:
                P.w(t, "cV")
                P.w(t, "cS")
                P.w(t, "cG", vno_ones)
            if ("v", pq) in ev:
                P.w(t, *ev[("v", pq)])
            pa = PA[:, pq * 512:pq * 512 + 512]
            for k in range(NT):
                P.inc(t.matmul(pa, bv(BB_HST + k * 2048 + tt * 128, 128),
                               rsl(slot, k), start=(k == 0), stop=(k == NT - 1)),
                      "cT", 1 if k == NT - 1 else 0)

        @sec(P, "vector")
        def _(v, tt=tt, pq=pq):
            P.w(v, "cT")
            P.w(v, "cG", vno_ones)
            for kv in range(NKV):
                v.tensor_copy(out=bv(BB_VNO + tt * 516 + kv * 129, 128),
                              in_=PA[:, pq * 512 + kv * 128:pq * 512 + kv * 128 + 128])
        ev[("v", pq)] = ("cV", P.C["cV"])
    ring_c["busy"][slot] = P.C["cT"]

    STst = [fv(BB_STST, 1024), fv(BB_STST + 2048, 1024)]
    e50 = L["e50_sb"]
    for qh in range(2):
        nkt = 8 * (qh + 1)
        for H in range(NQH):
            kv = H // (NQH // NKV)
            for kc in range(nkt):
                half = kc % 2

                @sec(P, "tensor")
                def _(t, qh=qh, H=H, kc=kc, half=half, kv=kv):
                    if H == 0 and kc == 0:
                        P.w(t, "cV")
                        P.w(t, "cS")
                    if ("st", half) in ev:
                        P.w(t, *ev[("st", half)])
                    pa = PA[:, half * 1024:half * 1024 + 1024]
                    for n in range(2):
                        P.w(t, "cT")
                        P.inc(t.matmul(pa[:, n * 512:(n + 1) * 512],
                                       bv(BB_KT + kv * 2048 + kc * 128, 128),
                                       bv(BB_QT + H * 2048 + qh * 1024 + n * 512, 512),
                                       start=True, stop=True), "cT", 1)

                dl_ = 128 * (kc - 8 * qh)

                @sec(P, "vector")
                def _(v, kc=kc, half=half, dl_=dl_):
                    P.w(v, "cT")
                    P.w(v, "cS")
                    pa = PA[:, half * 1024:half * 1024 + 1024]
                    st = STst[half]
                    lo = max(dl_, 0)
                    if lo < 1024:
                        v.tensor_scalar(out=st[:, lo:1024], in0=pa[:, lo:1024],
                                        scalar1=-50.0, scalar2=50.0, op0=OP.max, op1=OP.min)
                    if dl_ >= 0:
                        v.scalar_tensor_tensor(out=st[:, dl_:dl_ + 128],
                                               in0=st[:, dl_:dl_ + 128], scalar=50.0,
                                               in1=L["triu_sb"][:],
                                               op0=OP.add, op1=OP.mult)
                        v.tensor_scalar(out=st[:, dl_:dl_ + 128], in0=st[:, dl_:dl_ + 128],
                                        scalar1=-50.0, scalar2=None, op0=OP.add)

                @sec(P, "scalar")
                def _(s, kc=kc, half=half, dl_=dl_, H=H):
                    P.w(s, "cV")
                    if H == 0 and kc == 0:
                        P.w(s, "dW")
                        P.w(s, "cT")
                    st = STst[half]
                    pt = bv(BB_PT + kc * 1024, 1024)
                    if dl_ >= 0:
                        if dl_ + 128 < 1024:
                            s.activation(out=pt[:, dl_ + 128:1024], in_=st[:, dl_ + 128:1024],
                                         func=AF.Exp)
                        s.activation(out=pt[:, dl_:dl_ + 128], in_=st[:, dl_:dl_ + 128],
                                     func=AF.Exp)
                    else:
                        s.activation(out=pt, in_=st, func=AF.Exp)
                ev[("st", half)] = ("cS", P.C["cS"])

            for qtl in range(8):
                gq = qh * 8 + qtl
                half = qtl % 2
                psb = PB[:, half * 512:half * 512 + 129]

                @sec(P, "tensor")
                def _(t, qtl=qtl, gq=gq, half=half, psb=psb, kv=kv, nkt=nkt, qh=qh):
                    P.w(t, "cS")
                    if ("ctx", half) in ev:
                        P.w(t, *ev[("ctx", half)])
                    for kc in range(16):
                        if kc < nkt:
                            lhs = (bv(BB_PT + kc * 1024 + qtl * 128, 128) if kc <= gq
                                   else e50[:, 0:128])
                        else:
                            lhs = e50[:, 0:128]
                        P.w(t, "cT")
                        P.inc(t.matmul(psb, lhs, bv(BB_VNO + kc * 516 + kv * 129, 129),
                                       start=(kc == 0), stop=(kc == 15)), "cT", 1)

                @sec(P, "vector")
                def _(v, qtl=qtl, half=half, psb=psb):
                    P.w(v, "cT")
                    v.reciprocal(out=sc[:, 14:15], in_=PB[:, half * 512 + 128:half * 512 + 129])
                    v.tensor_scalar(out=bv(BB_CTXST + (qtl % 2) * 256, 128),
                                    in0=PB[:, half * 512:half * 512 + 128],
                                    scalar1=sc[:, 14:15], scalar2=None, op0=OP.mult)
                ev[("ctx", half)] = ("cV", P.C["cV"])

                pt_bank = L["PT0"] if qtl % 2 == 0 else L["PT1"]

                @sec(P, "tensor")
                def _(t, qtl=qtl, pt_bank=pt_bank):
                    P.w(t, "cV")
                    P.w(t, "cT")
                    P.inc(t.transpose(pt_bank[:, 0:128], bv(BB_CTXST + (qtl % 2) * 256, 128),
                                      L["identb_sb"][:]), "cT", 1)

                @sec(P, "vector")
                def _(v, qtl=qtl, H=H, qh=qh, pt_bank=pt_bank):
                    P.w(v, "cT")
                    v.tensor_copy(out=bv(BB_CTXT + H * 1024 + qtl * 128, 128),
                                  in_=pt_bank[:, 0:128])

        for o2q in range(4):
            slot = o2q % 2
            busy = ring_c["busy"].get(("wo", slot))

            @sec(P, "gpsimd")
            def _(g, o2q=o2q, slot=slot, busy=busy, qh=qh):
                P.w(g, "dW", P.wcount["bWo"])
                P.w(g, "cS")
                P.w(g, "cT")
                if busy is not None:
                    P.w(g, "cT", busy)
                P.dma(g, "dW",
                      bv(BB_PT + slot * 8192, 8192).rearrange("p (s o) -> p s o", o=512),
                      L["wbf"]["bWo"].ap()[:, o2q * 512:(o2q + 1) * 512]
                          .rearrange("(s p) o -> p s o", p=128))
            dwo = P.C["dW"]

            for qtl in range(8):
                gq = qh * 8 + qtl
                half = qtl % 2

                @sec(P, "sync")
                def _(sy, gq=gq, o2q=o2q, half=half):
                    P.w(sy, "cV")
                    P.dma(sy, "dA", fv(BB_RST + half * 1024, 512),
                          L["resid_d"].ap()[gq * 128:(gq + 1) * 128, o2q * 512:(o2q + 1) * 512])
                rld = P.C["dA"]

                @sec(P, "tensor")
                def _(t, qtl=qtl, half=half, dwo=dwo, slot=slot, o2q=o2q):
                    P.w(t, "dW", dwo)
                    if o2q == 0 and qtl == 0:
                        P.w(t, "cV")
                        P.w(t, "cS")
                    if ("wo", half) in ev:
                        P.w(t, *ev[("wo", half)])
                    pa = PA[:, half * 512:half * 512 + 512]
                    for k in range(NT):
                        P.inc(t.matmul(pa, bv(BB_CTXT + k * 1024 + qtl * 128, 128),
                                       bv(BB_PT + slot * 8192 + k * 512, 512),
                                       start=(k == 0), stop=(k == NT - 1)),
                              "cT", 1 if k == NT - 1 else 0)

                @sec(P, "vector")
                def _(v, half=half, rld=rld):
                    P.w(v, "cT")
                    P.w(v, "dA", rld)
                    v.tensor_tensor(out=fv(BB_RST + half * 1024, 512),
                                    in0=PA[:, half * 512:half * 512 + 512],
                                    in1=fv(BB_RST + half * 1024, 512), op=OP.add)
                ev[("wo", half)] = ("cV", P.C["cV"])

                @sec(P, "sync")
                def _(sy, gq=gq, o2q=o2q, half=half):
                    P.w(sy, "cV")
                    P.dma(sy, "dA", L["resid_d"].ap()[gq * 128:(gq + 1) * 128,
                                                      o2q * 512:(o2q + 1) * 512],
                          fv(BB_RST + half * 1024, 512))
            ring_c["busy"][("wo", slot)] = P.C["cT"]


def emit_finish(P):
    L = P.L

    @sec(P, "sync")
    def _(sy):
        P.w(sy, "dW")
        P.w(sy, "cV")
        P.w(sy, "cS")
        P.w(sy, "cT")
        P.dma(sy, "dA", L["out"], L["resid_d"].ap()[:, :])
        P.w(sy, "dA")


def kernel(**inputs):
    x = np.ascontiguousarray(inputs["L0_attn_a_x"])
    nc = build(4)
    consts = {
        "c_ident_bf": np.eye(128, dtype=ml_dtypes.bfloat16),
        "c_triu_f32": np.triu(np.ones((128, 128), dtype=np.float32)),
    }
    base = {k: np.ascontiguousarray(v) for k, v in inputs.items() if k != "L0_attn_a_x"}
    in_maps = []
    for c in range(8):
        m = dict(base)
        m.update(consts)
        m["L0_attn_a_x"] = x[c]
        in_maps.append(m)
    res = run_bass_kernel_spmd(nc, in_maps, core_ids=list(range(8)))
    return np.stack([r["out"] for r in res.results], axis=0)


# version 04
# speedup vs baseline: 1.2508x; 1.2508x over previous
import sys
sys.path.insert(0, "/opt/trn_rl_repo")
import math
from contextlib import ExitStack

import numpy as np
import ml_dtypes

import concourse.bass as bass
import concourse.mybir as mybir
from concourse.alu_op_type import AluOpType as OP
from concourse.bass_utils import run_bass_kernel_spmd

F32 = mybir.dt.float32
BF16 = mybir.dt.bfloat16
AF = mybir.ActivationFunctionType
AX = mybir.AxisListType

D = 2048
T = 2048
NT = 16
LCH = 128
NCH = T // LCH
SUB = 16
NSUB = LCH // SUB
HS = 64
NHEAD = 32
HD = 128
NQH = 16
NKV = 4
DKV = NKV * HD
NEXP = 32
IFF = 768
NIT = IFF // 128
TOPK = 8
LN_EPS = 1e-5
SQRT_HD = math.sqrt(128.0)
E50 = float(np.exp(np.float32(-50.0)))

ARN = 100352

SLOT_A = 0
SLOT_B = 32768
SLOT_C = 65536
TAIL = 92160


class Prog:
    def __init__(self, nc, block, sems, L):
        self.nc = nc
        self.block = block
        self.sem = sems
        self.C = {k: 0 for k in sems}
        self.dmaset = {"dW", "dA"}
        self.L = L
        self.wcount = {}

    def inc(self, instr, name, n=1):
        if name in ("cV", "cS", "cG"):
            return instr
        instr.then_inc(self.sem[name], n * (16 if name in self.dmaset else 1))
        self.C[name] += n
        return instr

    def w(self, eng, name, count=None):
        if count is None:
            count = self.C[name]
        assert count <= self.C[name], f"wait on future count {name} {count} > {self.C[name]}"
        eng.wait_ge(self.sem[name], count * (16 if name in self.dmaset else 1))

    def dma(self, eng, q, out, in_, transpose=False, gate=True, **kw):
        if gate and self.C[q] > 0:
            eng.wait_ge(self.sem[q], self.C[q] * 16)
        if transpose:
            instr = eng.dma_start_transpose(out, in_)
        else:
            instr = eng.dma_start(out=out, in_=in_, **kw)
        self.inc(instr, q)
        return self.C[q]


CHAIN_OPS = {
    "tensor_add", "tensor_mul", "tensor_sub", "tensor_max", "tensor_copy",
    "tensor_tensor", "tensor_scalar", "tensor_scalar_add", "tensor_scalar_mul",
    "tensor_scalar_max", "tensor_scalar_min", "tensor_scalar_sub",
    "tensor_single_scalar", "tensor_reduce", "reduce_max", "reduce_sum",
    "reciprocal", "memset", "memzero", "tensor_tensor_scan",
    "scalar_tensor_tensor", "select", "copy_predicated", "activation",
    "add", "mul", "copy", "sqrt", "square", "sign", "iota",
    "partition_broadcast", "partition_all_reduce", "bn_stats", "bn_aggr",
    "stream_shuffle", "affine_select",
}


class Chained:

    def __init__(self, P, eng, ctr):
        object.__setattr__(self, "_P", P)
        object.__setattr__(self, "_e", eng)
        object.__setattr__(self, "_c", ctr)

    def __getattr__(self, name):
        attr = getattr(self._e, name)
        if name not in CHAIN_OPS:
            return attr
        P, eng, ctr = self._P, self._e, self._c

        def wrap(*a, **kw):
            if P.C[ctr] > 0:
                eng.wait_ge(P.sem[ctr], P.C[ctr])
            instr = attr(*a, **kw)
            instr.then_inc(P.sem[ctr], 1)
            P.C[ctr] += 1
            return instr
        return wrap


_CHAIN_CTR = {"vector": "cV", "scalar": "cS", "gpsimd": "cG"}


def phase_fence(P):
    for eng in ("gpsimd", "vector", "scalar", "tensor", "sync"):
        @sec(P, eng)
        def _(e):
            for s_ in ("dA", "dW", "cT", "cV", "cS", "cG"):
                P.w(e, s_)


def sec(P, engname):
    deco = getattr(P.block, engname)
    ctr = _CHAIN_CTR.get(engname)
    if ctr is None:
        return deco

    def outer(fn):
        def inner(eng):
            return fn(Chained(P, eng, ctr))
        return deco(inner)
    return outer


def ln_tile(P, xt, affine, xn_out):
    L = P.L
    sc = L["sc_sb"]
    uu = L["ln_u"]
    npass = 2 if affine else 1
    src = xt
    for p in range(npass):
        @sec(P, "vector")
        def _(v, src=src):
            P.w(v, "dA")
            P.w(v, "cS")
            v.tensor_tensor(out=L["ln_sq"], in0=src, in1=src, op=OP.mult)
            v.tensor_reduce(out=sc[:, 0:1], in_=src, axis=AX.X, op=OP.add)
            v.tensor_scalar(out=sc[:, 0:1], in0=sc[:, 0:1], scalar1=1.0 / D, scalar2=None, op0=OP.mult)
            v.tensor_reduce(out=sc[:, 1:2], in_=L["ln_sq"], axis=AX.X, op=OP.add)
            v.tensor_scalar(out=sc[:, 1:2], in0=sc[:, 1:2], scalar1=1.0 / D, scalar2=None, op0=OP.mult)
            v.tensor_tensor(out=sc[:, 2:3], in0=sc[:, 0:1], in1=sc[:, 0:1], op=OP.mult)
            v.tensor_tensor(out=sc[:, 1:2], in0=sc[:, 1:2], in1=sc[:, 2:3], op=OP.subtract)
            v.tensor_scalar(out=sc[:, 1:2], in0=sc[:, 1:2], scalar1=LN_EPS, scalar2=None, op0=OP.add)

        @sec(P, "scalar")
        def _(s):
            P.w(s, "cV")
            s.activation(out=sc[:, 3:4], in_=sc[:, 1:2], func=AF.Ln)

        last = (p == npass - 1)

        @sec(P, "vector")
        def _(v, p=p):
            P.w(v, "cS")
            v.tensor_scalar(out=sc[:, 3:4], in0=sc[:, 3:4], scalar1=-0.5, scalar2=None, op0=OP.mult)

        @sec(P, "scalar")
        def _(s):
            P.w(s, "cV")
            s.activation(out=sc[:, 4:5], in_=sc[:, 3:4], func=AF.Exp)

        @sec(P, "vector")
        def _(v, src=src, last=last, p=p):
            P.w(v, "cS")
            if last:
                P.w(v, "cT")
            if not affine:
                P.inc(v.tensor_scalar(out=xn_out, in0=src, scalar1=sc[:, 0:1], scalar2=sc[:, 4:5],
                                      op0=OP.subtract, op1=OP.mult), "cV")
            elif p == 0:
                P.inc(v.tensor_scalar(out=uu, in0=src, scalar1=sc[:, 0:1], scalar2=sc[:, 4:5],
                                      op0=OP.subtract, op1=OP.mult), "cV")
            else:
                v.tensor_scalar(out=uu, in0=uu, scalar1=sc[:, 0:1], scalar2=sc[:, 4:5],
                                op0=OP.subtract, op1=OP.mult)
                v.tensor_tensor(out=uu, in0=uu, in1=L["gB"], op=OP.mult)
                P.inc(v.tensor_tensor(out=xn_out, in0=uu, in1=L["bB"], op=OP.add), "cV")
        src = uu


def transpose16(P, src_fn, dst_fn, n=16):
    L = P.L
    nh = (n + 7) // 8
    for h in range(nh):
        cnt = min(8, n - h * 8)
        pt = L["PT0"] if h % 2 == 0 else L["PT1"]

        @sec(P, "tensor")
        def _(t, h=h, cnt=cnt, pt=pt):
            P.w(t, "cV")
            for j in range(cnt):
                P.inc(t.transpose(pt[:, j * 128:j * 128 + 128], src_fn(h * 8 + j),
                                  L["identb_sb"][:]), "cT", 1 if j == cnt - 1 else 0)

        @sec(P, "vector")
        def _(v, h=h, cnt=cnt, pt=pt):
            P.w(v, "cT")
            for j in range(cnt):
                P.inc(v.tensor_copy(out=dst_fn(h * 8 + j), in_=pt[:, j * 128:j * 128 + 128]),
                      "cV", 1 if j == cnt - 1 else 0)




def build(n_sublayers=4, stage=None, rec_lvl=7, rec_nch=NCH, rec_nsub=NSUB, rec_nhf=2):
    nc = bass.Bass("TRN2", target_bir_lowering=False)

    def din(name, shape):
        return nc.dram_tensor(name, list(shape), F32, kind="ExternalInput").ap()

    L = {"n_sublayers": n_sublayers}
    for nm, shp in [
        ("L0_attn_a_x", (T, D)), ("L0_attn_a_ln_scale", (D,)), ("L0_attn_a_ln_bias", (D,)),
        ("L0_attn_a_Wx", (D, D)), ("L0_attn_a_Ww", (D, D)), ("L0_attn_a_bw", (D,)),
        ("L0_attn_a_Wk", (D, D)), ("L0_attn_a_Wv", (D, D)), ("L0_attn_a_Wr", (D, D)),
        ("L0_attn_a_Wo", (D, D)), ("L0_attn_b_Wq", (D, D)), ("L0_attn_b_Wk", (D, DKV)),
        ("L0_attn_b_Wv", (D, DKV)), ("L0_attn_b_Wo", (D, D)), ("L0_ffn_gate_w", (NEXP, D)),
        ("L0_ffn_w_gate", (NEXP, D, IFF)), ("L0_ffn_w_up", (NEXP, D, IFF)),
        ("L0_ffn_w_down", (NEXP, IFF, D)),
    ]:
        L[nm] = din(nm, shp)
    L["x_in"], L["ln_scale"], L["ln_bias"] = L["L0_attn_a_x"], L["L0_attn_a_ln_scale"], L["L0_attn_a_ln_bias"]
    L["aWx"], L["aWw"], L["a_bw"] = L["L0_attn_a_Wx"], L["L0_attn_a_Ww"], L["L0_attn_a_bw"]
    L["aWk"], L["aWv"], L["aWr"], L["aWo"] = L["L0_attn_a_Wk"], L["L0_attn_a_Wv"], L["L0_attn_a_Wr"], L["L0_attn_a_Wo"]
    L["bWq"], L["bWk"], L["bWv"], L["bWo"] = L["L0_attn_b_Wq"], L["L0_attn_b_Wk"], L["L0_attn_b_Wv"], L["L0_attn_b_Wo"]
    L["gate_w"], L["w_gate"], L["w_up"], L["w_down"] = L["L0_ffn_gate_w"], L["L0_ffn_w_gate"], L["L0_ffn_w_up"], L["L0_ffn_w_down"]

    L["ident_bf"] = nc.dram_tensor("c_ident_bf", [128, 128], BF16, kind="ExternalInput").ap()
    L["triu_f32"] = nc.dram_tensor("c_triu_f32", [128, 128], F32, kind="ExternalInput").ap()
    L["out"] = nc.dram_tensor("out", [T, D], F32, kind="ExternalOutput").ap()

    def scr(name, shape, dt):
        return nc.dram_tensor(name, list(shape), dt)

    L["resid_d"] = scr("resid_d", (T, D), F32)
    wbf = {}
    for nm in ["aWw", "aWk", "aWv", "aWr", "aWo", "bWq", "bWo"]:
        wbf[nm] = scr("wbf_" + nm, (D, D), BF16)
    wbf["bWk"] = scr("wbf_bWk", (D, DKV), BF16)
    wbf["bWv"] = scr("wbf_bWv", (D, DKV), BF16)
    wbf["wg"] = scr("wbf_wg", (NEXP, D, IFF), BF16)
    wbf["wu"] = scr("wbf_wu", (NEXP, D, IFF), BF16)
    wbf["wd"] = scr("wbf_wd", (NEXP, IFF, D), BF16)
    L["wbf"] = wbf
    L["zln_d"] = scr("zln_d", (NCH, 128, D), F32)
    L["kT_d"] = scr("kT_d", (NCH, 128, D), BF16)
    L["rT_d"] = scr("rT_d", (NCH, 128, D), BF16)
    L["vN_d"] = scr("vN_d", (NCH, 128, D), BF16)
    L["xnT_d"] = scr("xnT_d", (NT, 128, D), BF16)
    L["fwT_d"] = scr("fwT_d", (NEXP, T), F32)

    st = ExitStack()
    with st:
        e = st.enter_context
        AR = e(nc.sbuf_tensor("AR", [128, ARN], BF16))
        L["AR"] = AR
        L["state_sb"] = e(nc.sbuf_tensor("state_sb", [128, NT * HS], F32))
        L["state_bf"] = e(nc.sbuf_tensor("state_bf", [128, NT * HS], BF16))
        L["identb_sb"] = e(nc.sbuf_tensor("identb_sb", [128, 128], BF16))
        L["triu_sb"] = e(nc.sbuf_tensor("triu_sb", [128, 128], F32))
        L["e50_sb"] = e(nc.sbuf_tensor("e50_sb", [128, 128], BF16))
        L["gwT_sb"] = e(nc.sbuf_tensor("gwT_sb", [128, NT * NEXP], BF16))
        L["fw_sb"] = e(nc.sbuf_tensor("fw_sb", [128, NT * NEXP], F32))
        L["sc_sb"] = e(nc.sbuf_tensor("sc_sb", [128, 64], F32))
        L["bw_sb"] = e(nc.sbuf_tensor("bw_sb", [128, NT], F32))
        L["eL_sb"] = e(nc.sbuf_tensor("eL_sb", [128, NT], F32))
        L["rt_sb"] = e(nc.sbuf_tensor("rt_sb", [128, 96], F32))
        L["identf_sb"] = e(nc.sbuf_tensor("identf_sb", [128, 128], F32))
        L["PA"] = e(nc.psum_tensor("PA", [128, 2048], F32))
        L["PB"] = e(nc.psum_tensor("PB", [128, 1024], F32))
        L["PT0"] = e(nc.psum_tensor("PT0", [128, 1024], BF16))
        L["PT1"] = e(nc.psum_tensor("PT1", [128, 1024], BF16))
        sems = {}
        for s_ in ["dW", "dA", "cT", "cV", "cS", "cG"]:
            sems[s_] = e(nc.semaphore(s_))
        block = e(nc.Block())

        P = Prog(nc, block, sems, L)
        ARf = AR.ap().bitcast(F32)

        def bv(off, n):
            assert off + n <= ARN
            return AR[:, off:off + n]

        def fv(off, n):
            assert off % 2 == 0 and off + 2 * n <= ARN
            return ARf[:, off // 2: off // 2 + n]

        L["bv"] = bv
        L["fv"] = fv
        L["ln_sq"] = fv(TAIL, 2048)
        L["ln_u"] = fv(TAIL + 4096, 2048)

        L["stage"] = stage
        L["rec_lvl"] = rec_lvl
        L["rec_nch"] = rec_nch
        L["rec_nsub"] = rec_nsub
        L["rec_nhf"] = rec_nhf
        emit_consts_and_w0(P)
        if stage != "w0":
            emit_attn_a(P)
        if n_sublayers >= 2:
            emit_ffn(P, 0)
        if n_sublayers >= 3:
            emit_attn_b(P)
        if n_sublayers >= 4:
            emit_ffn(P, 1)
        emit_finish(P)

    return nc


def emit_consts_and_w0(P):
    L = P.L
    wbf = L["wbf"]

    @sec(P, "sync")
    def _(sy):
        P.dma(sy, "dA", L["identb_sb"][:], L["ident_bf"])
        P.dma(sy, "dA", L["triu_sb"][:], L["triu_f32"])
        P.dma(sy, "dA", L["bw_sb"][:],
              bass.AP(L["a_bw"].tensor, 0, [[1, 128], [128, NT]]),
              allow_slow_non_contiguous=True)

    @sec(P, "gpsimd")
    def _(g):
        g.memset(L["e50_sb"][:], E50)
        g.memset(L["state_sb"][:], 0.0)
        P.inc(g.memset(L["state_bf"][:], 0.0), "cG")
        for nm in ["aWw", "aWk", "aWv", "aWr", "aWo"]:
            P.wcount[nm] = P.dma(g, "dW", wbf[nm].ap(), L[nm])
        for ei in range(NEXP):
            P.dma(g, "dW", wbf["wg"].ap()[ei], L["w_gate"][ei])
            P.dma(g, "dW", wbf["wu"].ap()[ei], L["w_up"][ei])
            P.wcount[f"ffn{ei}"] = P.dma(g, "dW", wbf["wd"].ap()[ei], L["w_down"][ei])
        for nm in ["bWq", "bWk", "bWv", "bWo"]:
            P.wcount[nm] = P.dma(g, "dW", wbf[nm].ap(), L[nm])




def wt_quarter_load(P, slot, wdram, q, pre_wait=None):
    L = P.L
    bv = L["bv"]

    @sec(P, "sync")
    def _(sy):
        if pre_wait:
            for args in pre_wait:
                P.w(sy, *args)
        for j in range(NT):
            P.dma(sy, "dA", bv(SLOT_C + slot * 8192 + j * 512, 512),
                  wdram.ap()[q * 512:(q + 1) * 512, j * 128:j * 128 + 128],
                  transpose=True, gate=(j == 0))
    return P.C["dA"]


def emit_attn_a(P):
    L = P.L
    bv, fv = L["bv"], L["fv"]
    PA, PB = L["PA"], L["PB"]

    def wq(slot, j):
        return bv(SLOT_C + slot * 8192 + j * 512, 512)

    ring_n = {"i": 0}

    def next_slot(wname=None):
        pw = []
        if wname is not None:
            pw.append(("dW", P.wcount[wname]))
        if ring_n["i"] >= 3:
            pw.append(("cT", P.C["cT"]))
        slot = ring_n["i"] % 3
        ring_n["i"] += 1
        return slot, pw

    @sec(P, "gpsimd")
    def _(g):
        for j in range(NT):
            P.dma(g, "dW", bv(SLOT_A + j * 2048, 2048),
                  L["aWx"][j * 128:(j + 1) * 128, :], gate=(j == 0))
    wx_done = P.C["dW"]

    ev_cnt = {}
    for q in range(4):
        slot, pw = next_slot("aWw" if q == 0 else None)
        dbc = wt_quarter_load(P, slot, L["wbf"]["aWw"], q, pre_wait=pw)
        for dt in range(NT):
            pq = dt % 2

            @sec(P, "tensor")
            def _(t, q=q, dt=dt, pq=pq, slot=slot, dbc=dbc):
                if q == 0 and dt == 0:
                    P.w(t, "cV")
                    P.w(t, "dW", wx_done)
                if dt == 0:
                    P.w(t, "dA", dbc)
                if pq in ev_cnt:
                    P.w(t, *ev_cnt[pq])
                pa = PA[:, pq * 512:pq * 512 + 512]
                for k in range(NT):
                    P.inc(t.matmul(pa, bv(SLOT_A + k * 2048 + dt * 128, 128),
                                   wq(slot, k), start=(k == 0), stop=(k == NT - 1)),
                          "cT", 1 if k == NT - 1 else 0)

            @sec(P, "vector")
            def _(v, q=q, dt=dt, pq=pq):
                P.w(v, "cT")
                P.inc(v.tensor_copy(out=bv(SLOT_B + dt * 2048 + q * 512, 512),
                                    in_=PA[:, pq * 512:pq * 512 + 512]), "cV")
            ev_cnt[pq] = ("cV", P.C["cV"])

    if L.get("stage") == "wz":
        return
    L["gB"] = fv(SLOT_C, 2048)
    L["bB"] = fv(SLOT_C + 4096, 2048)
    xbuf = fv(SLOT_C + 8192, 2048)
    xnst = bv(SLOT_C + 16384, 2048)

    @sec(P, "sync")
    def _(sy):
        P.w(sy, "cV")
        P.dma(sy, "dA", L["gB"], bass.AP(L["ln_scale"].tensor, 0, [[0, 128], [1, 2048]]))
        P.dma(sy, "dA", L["bB"], bass.AP(L["ln_bias"].tensor, 0, [[0, 128], [1, 2048]]))

    for i in range(NT):
        @sec(P, "sync")
        def _(sy, i=i):
            P.w(sy, "cV")
            P.dma(sy, "dA", xbuf, L["x_in"][i * 128:(i + 1) * 128, :])
        ln_tile(P, xbuf, True, xnst)
        transpose16(P, lambda jj: xnst[:, jj * 128:jj * 128 + 128],
                    lambda jj, i=i: bv(SLOT_A + jj * 2048 + i * 128, 128))

    if L.get("stage") == "ln":
        return
    zlB = [fv(TAIL, 1024), fv(TAIL + 4096, 1024)]
    zrB = [fv(TAIL + 2048, 1024), fv(TAIL + 6144, 1024)]
    for ot in range(NT):
        for th in range(2):
            half = (2 * ot + th) % 2

            @sec(P, "tensor")
            def _(t, ot=ot, th=th, half=half):
                if ot == 0 and th == 0:
                    P.w(t, "cV")
                if half in ev_cnt:
                    P.w(t, *ev_cnt[half])
                pa = PA[:, half * 1024:half * 1024 + 1024]
                for n in range(2):
                    for k in range(NT):
                        P.inc(t.matmul(pa[:, n * 512:(n + 1) * 512],
                                       bv(SLOT_B + k * 2048 + ot * 128, 128),
                                       bv(SLOT_A + k * 2048 + th * 1024 + n * 512, 512),
                                       start=(k == 0), stop=(k == NT - 1)),
                              "cT", 1 if (k == NT - 1 and n == 1) else 0)

            zl = zlB[half]
            zr = zrB[half]

            @sec(P, "scalar")
            def _(s, ot=ot, half=half, zl=zl):
                P.w(s, "cT")
                P.w(s, "cV")
                pa = PA[:, half * 1024:half * 1024 + 1024]
                s.activation(out=zl, in_=pa, func=AF.Abs, bias=L["bw_sb"][:, ot:ot + 1])
                s.activation(out=zl, in_=zl, func=AF.Exp, scale=-1.0)
                P.inc(s.activation(out=zl, in_=zl, func=AF.Ln, bias=1.0), "cS")

            @sec(P, "vector")
            def _(v, ot=ot, half=half, zl=zl, zr=zr):
                P.w(v, "cS")
                P.w(v, "dA")
                pa = PA[:, half * 1024:half * 1024 + 1024]
                v.tensor_scalar(out=zr, in0=pa, scalar1=L["bw_sb"][:, ot:ot + 1],
                                scalar2=0.0, op0=OP.add, op1=OP.max)
                P.inc(v.scalar_tensor_tensor(out=zr, in0=zr, scalar=-1.0, in1=zl,
                                             op0=OP.mult, op1=OP.subtract), "cV")
            ev_cnt[half] = ("cV", P.C["cV"])

            @sec(P, "sync")
            def _(sy, ot=ot, th=th, zr=zr):
                P.w(sy, "cV")
                P.dma(sy, "dA",
                      L["zln_d"].ap()[th * 8:(th + 1) * 8, :, ot * 128:ot * 128 + 128]
                          .rearrange("c p n -> p c n"),
                      zr.rearrange("p (c n) -> p c n", n=128))

    if L.get("stage") == "z":
        return
    kst = [bv(SLOT_B, 2048), bv(SLOT_B + 2048, 2048)]
    vst = [bv(SLOT_B + 4096, 512), bv(SLOT_B + 4608, 512)]

    for wname, kind in [("aWk", "k"), ("aWv", "v"), ("aWr", "r")]:
        for q in range(4):
            slot, pw = next_slot(wname if q == 0 else None)
            dbc = wt_quarter_load(P, slot, L["wbf"][wname], q, pre_wait=pw)

            if kind in ("k", "r"):
                for oo in range(4):
                    ot = q * 4 + oo
                    stg = kst[oo % 2]
                    for th in range(2):
                        half = th

                        @sec(P, "tensor")
                        def _(t, slot=slot, dbc=dbc, oo=oo, th=th, half=half, q=q):
                            P.w(t, "dA", dbc)
                            if q == 0 and oo == 0 and th == 0:
                                P.w(t, "cV")
                                P.w(t, "cS")
                            if half in ev_cnt:
                                P.w(t, *ev_cnt[half])
                            pa = PA[:, half * 1024:half * 1024 + 1024]
                            for n in range(2):
                                for k in range(NT):
                                    P.inc(t.matmul(pa[:, n * 512:(n + 1) * 512],
                                                   wq(slot, k)[:, oo * 128:oo * 128 + 128],
                                                   bv(SLOT_A + k * 2048 + th * 1024 + n * 512, 512),
                                                   start=(k == 0), stop=(k == NT - 1)),
                                          "cT", 1 if (k == NT - 1 and n == 1) else 0)

                        if kind == "k":
                            @sec(P, "vector")
                            def _(v, th=th, half=half, stg=stg):
                                P.w(v, "cT")
                                P.w(v, "dA")
                                P.inc(v.tensor_copy(out=stg[:, th * 1024:(th + 1) * 1024],
                                                    in_=PA[:, half * 1024:half * 1024 + 1024]), "cV")
                            ev_cnt[half] = ("cV", P.C["cV"])
                        else:
                            @sec(P, "scalar")
                            def _(s, th=th, half=half, stg=stg):
                                P.w(s, "cT")
                                P.w(s, "dA")
                                P.inc(s.activation(out=stg[:, th * 1024:(th + 1) * 1024],
                                                   in_=PA[:, half * 1024:half * 1024 + 1024],
                                                   func=AF.Sigmoid), "cS")
                            ev_cnt[half] = ("cS", P.C["cS"])

                    dstd = L["kT_d"] if kind == "k" else L["rT_d"]

                    @sec(P, "sync")
                    def _(sy, ot=ot, dstd=dstd, kind=kind, stg=stg):
                        P.w(sy, "cV" if kind == "k" else "cS")
                        P.dma(sy, "dA",
                              dstd.ap()[:, :, ot * 128:ot * 128 + 128]
                                  .rearrange("c p n -> p c n"),
                              stg.rearrange("p (c n) -> p c n", n=128))
            else:
                for tt in range(NT):
                    pq = tt % 2
                    stg = vst[tt % 2]

                    @sec(P, "tensor")
                    def _(t, slot=slot, dbc=dbc, tt=tt, pq=pq, q=q):
                        P.w(t, "dA", dbc)
                        if q == 0 and tt == 0:
                            P.w(t, "cV")
                            P.w(t, "cS")
                        key = ("vq", pq)
                        if key in ev_cnt:
                            P.w(t, *ev_cnt[key])
                        pa = PA[:, 1024 + pq * 512:1024 + pq * 512 + 512]
                        for k in range(NT):
                            P.inc(t.matmul(pa, bv(SLOT_A + k * 2048 + tt * 128, 128),
                                           wq(slot, k), start=(k == 0), stop=(k == NT - 1)),
                                  "cT", 1 if k == NT - 1 else 0)

                    @sec(P, "vector")
                    def _(v, tt=tt, pq=pq, stg=stg):
                        P.w(v, "cT")
                        P.w(v, "dA")
                        P.inc(v.tensor_copy(out=stg, in_=PA[:, 1024 + pq * 512:1024 + pq * 512 + 512]), "cV")
                    ev_cnt[("vq", pq)] = ("cV", P.C["cV"])

                    @sec(P, "sync")
                    def _(sy, tt=tt, q=q, stg=stg):
                        P.w(sy, "cV")
                        P.dma(sy, "dA", L["vN_d"].ap()[tt, :, q * 512:q * 512 + 512], stg)

    if L.get("stage") == "kvr":
        return
    emit_attn_a_rec(P)
    if L.get("stage") == "rec":
        return
    emit_attn_a_out(P)


RB_LNA = SLOT_B
RB_CT = SLOT_B + 4096
RB_KT = SLOT_B + 8192
RB_RT = SLOT_B + 10240
RB_VN = SLOT_B + 12288
RB_RT2 = SLOT_B + 14336
RB_KHT = SLOT_B + 16384
RB_KHN = SLOT_B + 18432
RB_MARG = SLOT_B + 20480
RB_EB = SLOT_B + 24576
RB_KB = SLOT_B + 26624
RB_RH = SLOT_B + 28672
RB_ASB = SLOT_B + 28928
RB_ZERO = SLOT_B + 30976


def emit_attn_a_rec(P):
    L = P.L
    lvl = L.get("rec_lvl", 7)
    nch = L.get("rec_nch", NCH)
    nsub = L.get("rec_nsub", NSUB)
    nhf = L.get("rec_nhf", 2)
    bv, fv = L["bv"], L["fv"]
    PA, PB = L["PA"], L["PB"]
    lnaB = [fv(RB_LNA, 2048), fv(SLOT_C, 2048)]
    cT = fv(RB_CT, 2048)
    cT3 = cT.rearrange("p (j s) -> p j s", s=128)
    kTB = [bv(RB_KT, 2048), bv(SLOT_C + 4096, 2048)]
    rTB = [bv(RB_RT, 2048), bv(SLOT_C + 6144, 2048)]
    vNB = [bv(RB_VN, 2048), bv(SLOT_C + 8192, 2048)]
    rt2 = bv(RB_RT2, 2048)
    khT = bv(RB_KHT, 2048)
    khN = bv(RB_KHN, 2048)
    marg = fv(RB_MARG, 2048)
    marg3 = marg.rearrange("p (j s) -> p j s", s=128)
    Eb = bv(RB_EB, 2048)
    kb = bv(RB_KB, 2048)
    rh = bv(RB_RH, 256)
    Asb = bv(RB_ASB, 2048)
    zero = fv(RB_ZERO, 128)
    cL_b = cT3[:, :, 127:128].broadcast_to((128, NT, 128))

    @sec(P, "vector")
    def _(v):
        P.w(v, "cS")
        P.w(v, "dA")
        P.inc(v.memset(zero, 0.0), "cV")

    chunk_ld = {}

    def load_chunk(cc):
        @sec(P, "sync")
        def _(sy, cc=cc):
            P.w(sy, "cT")
            P.dma(sy, "dA", lnaB[cc % 2], L["zln_d"].ap()[cc])
            P.dma(sy, "dA", kTB[cc % 2], L["kT_d"].ap()[cc], gate=False)
            P.dma(sy, "dA", rTB[cc % 2], L["rT_d"].ap()[cc], gate=False)
            P.dma(sy, "dA", vNB[cc % 2], L["vN_d"].ap()[cc], gate=False)
        chunk_ld[cc] = P.C["dA"]

    load_chunk(0)
    for c in range(nch):
        lna = lnaB[c % 2]
        kT = kTB[c % 2]
        rT = rTB[c % 2]
        rT3 = rT.rearrange("p (j s) -> p j s", s=128)
        vN = vNB[c % 2]
        if c + 1 < nch:
            load_chunk(c + 1)

        @sec(P, "vector")
        def _(v, c=c):
            P.w(v, "dA", chunk_ld[c])
            for j in range(NT):
                P.inc(v.tensor_tensor_scan(out=cT[:, j * 128:(j + 1) * 128],
                                           data0=lna[:, j * 128:(j + 1) * 128],
                                           data1=zero, initial=0.0,
                                           op0=OP.add, op1=OP.add),
                      "cV", 1 if j == NT - 1 else 0)

        if lvl < 2:
            continue
        @sec(P, "scalar")
        def _(s):
            P.w(s, "cV")
            P.inc(s.activation(out=Eb, in_=cT, func=AF.Exp), "cS")
            P.inc(s.activation(out=L["eL_sb"][:].rearrange("p (j o) -> p j o", o=1),
                               in_=cT3[:, :, 127:128], func=AF.Exp), "cS")

        @sec(P, "vector")
        def _(v):
            P.w(v, "cS")
            v.tensor_tensor(out=rt2, in0=rT, in1=Eb, op=OP.mult)
            P.inc(v.tensor_tensor(out=marg3, in0=cL_b, in1=cT3, op=OP.subtract), "cV")

        @sec(P, "scalar")
        def _(s):
            P.w(s, "cV")
            P.inc(s.activation(out=Eb, in_=marg, func=AF.Exp), "cS")

        @sec(P, "vector")
        def _(v):
            P.w(v, "cS")
            P.inc(v.tensor_tensor(out=khT, in0=kT, in1=Eb, op=OP.mult), "cV")

        transpose16(P, lambda jj: khT[:, jj * 128:jj * 128 + 128],
                    lambda jj: khN[:, jj * 128:jj * 128 + 128])
        if lvl < 3:
            continue

        for hf in range(nhf):
            j0 = hf * 8
            for i in range(nsub):
                @sec(P, "vector")
                def _(v, i=i, j0=j0):
                    P.w(v, "cS")
                    mh = marg[:, j0 * 128:(j0 + 8) * 128]
                    mh3 = marg3[:, j0:j0 + 8, :]
                    ch = cT[:, j0 * 128:(j0 + 8) * 128]
                    ch3 = cT3[:, j0:j0 + 8, :]
                    if i == 0:
                        P.inc(v.tensor_scalar(out=mh, in0=ch, scalar1=-80.0, scalar2=None,
                                              op0=OP.max), "cV")
                    else:
                        bview = cT3[:, j0:j0 + 8, 16 * i - 1:16 * i].broadcast_to((128, 8, 128))
                        v.tensor_tensor(out=mh3, in0=ch3, in1=bview, op=OP.subtract)
                        P.inc(v.tensor_scalar(out=mh, in0=mh, scalar1=-80.0, scalar2=None,
                                              op0=OP.max), "cV")

                @sec(P, "scalar")
                def _(s, i=i, j0=j0):
                    P.w(s, "cV")
                    P.inc(s.activation(out=Eb[:, j0 * 128:(j0 + 8) * 128],
                                       in_=marg[:, j0 * 128:(j0 + 8) * 128],
                                       func=AF.Exp, scale=-1.0), "cS")
                    P.inc(s.activation(
                        out=rh[:, 0:128].rearrange("p (j s) -> p j s", s=16),
                        in_=marg3[:, j0:j0 + 8, 16 * i:16 * i + 16],
                        func=AF.Exp), "cS")

                @sec(P, "vector")
                def _(v, i=i, j0=j0):
                    P.w(v, "cS")
                    P.w(v, "cT")
                    v.tensor_tensor(out=kb[:, j0 * 128:(j0 + 8) * 128],
                                    in0=kT[:, j0 * 128:(j0 + 8) * 128],
                                    in1=Eb[:, j0 * 128:(j0 + 8) * 128], op=OP.mult)
                    P.inc(v.tensor_tensor(
                        out=rh[:, 128:256].rearrange("p (j s) -> p j s", s=16),
                        in0=rT3[:, j0:j0 + 8, 16 * i:16 * i + 16],
                        in1=rh[:, 0:128].rearrange("p (j s) -> p j s", s=16),
                        op=OP.mult), "cV")

                if lvl < 4:
                    continue

                @sec(P, "tensor")
                def _(t, i=i, j0=j0):
                    P.w(t, "cV")
                    for jj in range(8):
                        for hp in range(2):
                            hh = 2 * jj + hp
                            P.w(t, "cT")
                            P.inc(t.matmul(
                                PA[:, hh * 128 + 16 * i: hh * 128 + 16 * (i + 1)],
                                kb[hp * 64:hp * 64 + 64,
                                   (j0 + jj) * 128:(j0 + jj) * 128 + 128],
                                rh[hp * 64:hp * 64 + 64, 128 + jj * 16:128 + jj * 16 + 16],
                                start=True, stop=True), "cT", 1)

            if lvl < 5:
                continue
            @sec(P, "vector")
            def _(v, j0=j0):
                P.w(v, "cT")
                for hh in range(16):
                    P.inc(v.tensor_tensor(out=Asb[:, hh * 128:(hh + 1) * 128],
                                          in0=PA[:, hh * 128:(hh + 1) * 128],
                                          in1=L["triu_sb"][:], op=OP.mult),
                          "cV", 1 if hh == 15 else 0)

            if lvl < 6:
                continue
            @sec(P, "tensor")
            def _(t, j0=j0, hf=hf, c=c):
                P.w(t, "cV")
                if c == 0 and hf == 0:
                    P.w(t, "cG")
                for pr in range(8):
                    jj = j0 + pr
                    for hp in range(2):
                        h = 2 * jj + hp
                        hh = 2 * pr + hp
                        ps = PB[hp * 64:hp * 64 + 64, pr * 128:pr * 128 + 128]
                        P.w(t, "cT")
                        P.inc(t.matmul(ps, vN[:, h * 64:h * 64 + 64],
                                       Asb[:, hh * 128:(hh + 1) * 128],
                                       start=True, stop=False), "cT", 1)
                        P.w(t, "cT")
                        P.inc(t.matmul(ps,
                                       L["state_bf"][hp * 64:hp * 64 + 64, jj * 64:jj * 64 + 64],
                                       rt2[hp * 64:hp * 64 + 64, jj * 128:jj * 128 + 128],
                                       start=False, stop=True), "cT", 1)

            @sec(P, "vector")
            def _(v, j0=j0, c=c):
                P.w(v, "cT")
                for pr in range(8):
                    jj = j0 + pr
                    P.inc(v.tensor_copy(out=bv(SLOT_A + jj * 2048 + c * 128, 128),
                                        in_=PB[:, pr * 128:pr * 128 + 128]),
                          "cV", 1 if pr == 7 else 0)

        if lvl < 7:
            continue
        @sec(P, "tensor")
        def _(t, c=c):
            P.w(t, "cV")
            for pr in range(8):
                for hp in range(2):
                    h = 2 * pr + hp
                    P.w(t, "cT")
                    P.inc(t.matmul(PB[hp * 64:hp * 64 + 64, pr * 64:pr * 64 + 64],
                                   khN[:, h * 64:h * 64 + 64],
                                   vN[:, h * 64:h * 64 + 64],
                                   start=True, stop=True), "cT", 1)
            for pr in range(8, 16):
                for hp in range(2):
                    h = 2 * pr + hp
                    P.w(t, "cT")
                    P.inc(t.matmul(PA[hp * 64:hp * 64 + 64, (pr - 8) * 64:(pr - 8) * 64 + 64],
                                   khN[:, h * 64:h * 64 + 64],
                                   vN[:, h * 64:h * 64 + 64],
                                   start=True, stop=True), "cT", 1)

        @sec(P, "vector")
        def _(v, c=c):
            P.w(v, "cT")
            for j in range(NT):
                v.tensor_scalar(out=L["state_sb"][:, j * 64:(j + 1) * 64],
                                in0=L["state_sb"][:, j * 64:(j + 1) * 64],
                                scalar1=L["eL_sb"][:, j:j + 1], scalar2=None, op0=OP.mult)
                src = PB[:, j * 64:(j + 1) * 64] if j < 8 else PA[:, (j - 8) * 64:(j - 8) * 64 + 64]
                v.tensor_tensor(out=L["state_sb"][:, j * 64:(j + 1) * 64],
                                in0=L["state_sb"][:, j * 64:(j + 1) * 64],
                                in1=src, op=OP.add)
                P.inc(v.tensor_copy(out=L["state_bf"][:, j * 64:(j + 1) * 64],
                                    in_=L["state_sb"][:, j * 64:(j + 1) * 64]),
                      "cV", 1 if j == NT - 1 else 0)


def emit_attn_a_out(P):
    L = P.L
    bv, fv = L["bv"], L["fv"]
    PA = L["PA"]

    xb = [fv(SLOT_B, 512), fv(SLOT_B + 1024, 512)]
    yb = [fv(SLOT_B + 2048, 512), fv(SLOT_B + 3072, 512)]

    ev = {}
    for o2q in range(4):
        pw = [("dW", P.wcount["aWo"]), ("cT", P.C["cT"])]
        slot = o2q % 3

        @sec(P, "sync")
        def _(sy, pw=pw):
            for args in pw:
                P.w(sy, *args)
        dbc = wt_quarter_load(P, slot, L["wbf"]["aWo"], o2q)
        for tt in range(NT):
            pq = tt % 2

            @sec(P, "sync")
            def _(sy, tt=tt, o2q=o2q, pq=pq):
                P.w(sy, "cV")
                P.dma(sy, "dA", xb[pq],
                      L["x_in"][tt * 128:(tt + 1) * 128, o2q * 512:o2q * 512 + 512])

            @sec(P, "tensor")
            def _(t, tt=tt, pq=pq, dbc=dbc, o2q=o2q, slot=slot):
                P.w(t, "dA", dbc)
                if o2q == 0 and tt == 0:
                    P.w(t, "cV")
                if pq in ev:
                    P.w(t, *ev[pq])
                pa = PA[:, pq * 512:pq * 512 + 512]
                for k in range(NT):
                    P.inc(t.matmul(pa, bv(SLOT_A + k * 2048 + tt * 128, 128),
                                   wq_slot(L, slot, k), start=(k == 0), stop=(k == NT - 1)),
                          "cT", 1 if k == NT - 1 else 0)

            @sec(P, "vector")
            def _(v, tt=tt, pq=pq, o2q=o2q):
                P.w(v, "cT")
                P.w(v, "dA")
                P.inc(v.tensor_tensor(out=yb[pq], in0=PA[:, pq * 512:pq * 512 + 512],
                                      in1=xb[pq], op=OP.add), "cV")
            ev[pq] = ("cV", P.C["cV"])

            @sec(P, "sync")
            def _(sy, tt=tt, pq=pq, o2q=o2q):
                P.w(sy, "cV")
                P.dma(sy, "dA", L["resid_d"].ap()[tt * 128:(tt + 1) * 128,
                                                  o2q * 512:o2q * 512 + 512], yb[pq])


def wq_slot(L, slot, j):
    return L["bv"](SLOT_C + slot * 8192 + j * 512, 512)


FC_XNT = 0
FC_RING = 16384
FC_G = 40960
FC_GU0 = 47104
FC_GU1 = 53248
FC_FW = 59392
FC_OUT = 63488
FC_RST = 96256
FL_XB = SLOT_C + 8192
FL_XN = SLOT_C + 16384
FL_XNT = SLOT_C + 18432
FL_GW = 16384
FL_FWT = 20480


def emit_ffn(P, call):
    L = P.L
    bv, fv = L["bv"], L["fv"]
    PA, PB = L["PA"], L["PB"]
    rt = L["rt_sb"]
    sc = L["sc_sb"]
    phase_fence(P)

    if call == 0:
        gwst = bv(FL_GW, 2048)

        @sec(P, "gpsimd")
        def _(g):
            P.w(g, "cT")
            P.dma(g, "dW", gwst[0:NEXP, :], L["gate_w"])
        gwc = P.C["dW"]

        @sec(P, "vector")
        def _(v):
            v.tensor_copy(out=L["identf_sb"][:], in_=L["identb_sb"][:])

        for h in range(2):
            pt = L["PT0"] if h % 2 == 0 else L["PT1"]

            @sec(P, "tensor")
            def _(t, h=h, pt=pt):
                P.w(t, "dW", gwc)
                P.w(t, "cV")
                for j in range(8):
                    jj = h * 8 + j
                    P.w(t, "cT")
                    P.inc(t.transpose(pt[:, j * 128:j * 128 + 32],
                                      gwst[0:NEXP, jj * 128:jj * 128 + 128],
                                      L["identb_sb"][0:NEXP, 0:NEXP]), "cT", 1)

            @sec(P, "vector")
            def _(v, h=h, pt=pt):
                P.w(v, "cT")
                for j in range(8):
                    jj = h * 8 + j
                    v.tensor_copy(out=L["gwT_sb"][:, jj * 32:(jj + 1) * 32],
                                  in_=pt[:, j * 128:j * 128 + 32])

    xbuf = fv(FL_XB, 2048)
    xnst = bv(FL_XN, 2048)
    xntst = bv(FL_XNT, 2048)
    for i in range(NT):
        @sec(P, "sync")
        def _(sy, i=i):
            P.w(sy, "cV")
            P.w(sy, "cT")
            P.dma(sy, "dA", xbuf, L["resid_d"].ap()[i * 128:(i + 1) * 128, :])
        ln_tile(P, xbuf, False, xnst)
        transpose16(P, lambda jj: xnst[:, jj * 128:jj * 128 + 128],
                    lambda jj: xntst[:, jj * 128:jj * 128 + 128])

        @sec(P, "sync")
        def _(sy, i=i):
            P.w(sy, "cV")
            P.dma(sy, "dA", L["xnT_d"].ap()[i], xntst)

        @sec(P, "tensor")
        def _(t, i=i):
            P.w(t, "cV")
            for k in range(NT):
                P.inc(t.matmul(PB[:, 0:32], xntst[:, k * 128:k * 128 + 128],
                               L["gwT_sb"][:, k * 32:(k + 1) * 32],
                               start=(k == 0), stop=(k == NT - 1)),
                      "cT", 1 if k == NT - 1 else 0)

        @sec(P, "vector")
        def _(v, i=i):
            P.w(v, "cT")
            v.tensor_copy(out=rt[:, 0:32], in_=PB[:, 0:32])
            v.tensor_reduce(out=sc[:, 8:9], in_=rt[:, 0:32], axis=AX.X, op=OP.max)
            v.tensor_scalar(out=sc[:, 8:9], in0=sc[:, 8:9], scalar1=-1.0, scalar2=None, op0=OP.mult)

        @sec(P, "scalar")
        def _(s, i=i):
            P.w(s, "cV")
            s.activation(out=rt[:, 32:64], in_=rt[:, 0:32], func=AF.Exp,
                         bias=sc[:, 8:9], accum_out=sc[:, 9:10])

        @sec(P, "vector")
        def _(v, i=i):
            P.w(v, "cS")
            v.reciprocal(out=sc[:, 10:11], in_=sc[:, 9:10])
            v.tensor_scalar(out=rt[:, 0:32], in0=rt[:, 32:64], scalar1=sc[:, 10:11],
                            scalar2=None, op0=OP.mult)
            v.tensor_copy(out=rt[:, 32:64], in_=rt[:, 0:32])
            for kk in range(TOPK):
                v.tensor_reduce(out=sc[:, 11:12], in_=rt[:, 32:64], axis=AX.X, op=OP.max)
                if kk < TOPK - 1:
                    v.tensor_scalar(out=rt[:, 64:96], in0=rt[:, 32:64], scalar1=sc[:, 11:12],
                                    scalar2=None, op0=OP.is_ge)
                    v.scalar_tensor_tensor(out=rt[:, 32:64], in0=rt[:, 64:96], scalar=-2.0,
                                           in1=rt[:, 32:64], op0=OP.mult, op1=OP.add)
            v.tensor_scalar(out=rt[:, 64:96], in0=rt[:, 0:32], scalar1=sc[:, 11:12],
                            scalar2=None, op0=OP.is_ge)
            v.tensor_tensor(out=rt[:, 32:64], in0=rt[:, 0:32], in1=rt[:, 64:96], op=OP.mult)
            v.tensor_reduce(out=sc[:, 12:13], in_=rt[:, 32:64], axis=AX.X, op=OP.add)
            v.tensor_scalar(out=sc[:, 12:13], in0=sc[:, 12:13], scalar1=1e-9, scalar2=None, op0=OP.max)
            v.reciprocal(out=sc[:, 13:14], in_=sc[:, 12:13])
            v.tensor_scalar(out=L["fw_sb"][:, i * 32:(i + 1) * 32], in0=rt[:, 32:64],
                            scalar1=sc[:, 13:14], scalar2=None, op0=OP.mult)

    fwtst = fv(FL_FWT, 2048)
    for i in range(NT):
        @sec(P, "tensor")
        def _(t, i=i):
            P.w(t, "cV")
            P.w(t, "cT")
            P.inc(t.transpose(PB[0:NEXP, 128:256],
                              L["fw_sb"][:, i * 32:(i + 1) * 32], L["identf_sb"][:]), "cT", 1)

        @sec(P, "vector")
        def _(v, i=i):
            P.w(v, "cT")
            v.tensor_copy(out=fwtst[0:NEXP, i * 128:(i + 1) * 128], in_=PB[0:NEXP, 128:256])

    @sec(P, "sync")
    def _(sy):
        P.w(sy, "cV")
        P.dma(sy, "dA", L["fwT_d"].ap()[:, :], fwtst[0:NEXP, :])
    fwt_stored = P.C["dA"]

    ring_c = {"i": 0, "busy": {}}
    WSTRIDE = {"wg": 768, "wu": 768, "wd": 2048}

    def wload(name, e, first=False):
        slot = ring_c["i"] % 2
        ring_c["i"] += 1
        busy = ring_c["busy"].get(slot)

        @sec(P, "sync")
        def _(g, name=name, e=e, slot=slot, busy=busy, first=first):
            if first:
                P.w(g, "cV")
                P.w(g, "cT")
            P.w(g, "dW", P.wcount[f"ffn{e}"])
            if busy is not None:
                P.w(g, "cT", busy)
            wi = {"wg": 768, "wu": 768, "wd": 2048}[name]
            P.dma(g, "dA",
                  bv(FC_RING + slot * 12288, 12288).rearrange("p (s i) -> p s i", i=wi),
                  L["wbf"][name].ap()[e].rearrange("(s p) i -> p s i", p=128))
        return slot, P.C["dA"]

    ev = {}
    for ch in range(2):
        @sec(P, "sync")
        def _(sy, ch=ch):
            P.w(sy, "cT")
            for j in range(NT):
                P.dma(sy, "dA", bv(FC_XNT + j * 1024, 1024),
                      L["xnT_d"].ap()[ch * 8:(ch + 1) * 8, :, j * 128:(j + 1) * 128]
                          .rearrange("c p n -> p c n"), gate=(j == 0))
        xnt_done = P.C["dA"]

        for pair in range(16):
            for sub, guoff in ((0, FC_GU0), (1, FC_GU1)):
                e = 2 * pair + sub
                sg, dg = wload("wg", e, first=(ch == 0 and pair == 0 and sub == 0))
                su, du = wload("wu", e)

                @sec(P, "sync")
                def _(sy, e=e, ch=ch, sub=sub):
                    P.w(sy, "cV")
                    P.dma(sy, "dA", fv(FC_FW + sub * 2048, 1024),
                          bass.AP(L["fwT_d"], e * T + ch * 1024, [[0, 128], [1, 1024]]))
                dfw = P.C["dA"]

                for it in range(NIT):
                    @sec(P, "tensor")
                    def _(t, it=it, dg=dg, sg=sg):
                        if it == 0:
                            P.w(t, "dA", dg)
                            P.w(t, "dA", xnt_done)
                        if "pa01" in ev:
                            P.w(t, *ev["pa01"])
                        for n in range(2):
                            for k in range(NT):
                                P.inc(t.matmul(PA[:, n * 512:(n + 1) * 512],
                                               bv(FC_RING + sg * 12288 + k * 768 + it * 128, 128),
                                               bv(FC_XNT + k * 1024 + n * 512, 512),
                                               start=(k == 0), stop=(k == NT - 1)),
                                      "cT", 1 if (k == NT - 1 and n == 1) else 0)

                    @sec(P, "scalar")
                    def _(s, it=it):
                        P.w(s, "cT")
                        P.w(s, "cV")
                        s.activation(out=bv(FC_G + it * 1024, 1024), in_=PA[:, 0:1024],
                                     func=AF.Silu)
                    ev["pa01"] = ("cS", P.C["cS"])

                    @sec(P, "tensor")
                    def _(t, it=it, du=du, su=su):
                        if it == 0:
                            P.w(t, "dA", du)
                        if "pa23" in ev:
                            P.w(t, *ev["pa23"])
                        for n in range(2):
                            for k in range(NT):
                                P.inc(t.matmul(PA[:, 1024 + n * 512:1024 + (n + 1) * 512],
                                               bv(FC_RING + su * 12288 + k * 768 + it * 128, 128),
                                               bv(FC_XNT + k * 1024 + n * 512, 512),
                                               start=(k == 0), stop=(k == NT - 1)),
                                      "cT", 1 if (k == NT - 1 and n == 1) else 0)

                    @sec(P, "vector")
                    def _(v, it=it, sub=sub, guoff=guoff, dfw=dfw):
                        P.w(v, "cT")
                        P.w(v, "dA", dfw)
                        P.w(v, "cS")
                        gu = bv(guoff + it * 1024, 1024)
                        v.tensor_tensor(out=gu, in0=PA[:, 1024:2048],
                                        in1=fv(FC_FW + sub * 2048, 1024), op=OP.mult)
                        v.tensor_tensor(out=gu, in0=gu, in1=bv(FC_G + it * 1024, 1024),
                                        op=OP.mult)
                    ev["pa23"] = ("cV", P.C["cV"])
                ring_c["busy"][sg] = P.C["cT"]
                ring_c["busy"][su] = P.C["cT"]

            dslots = [wload("wd", 2 * pair)[0:2], wload("wd", 2 * pair + 1)[0:2]]

            for tt in range(8):
                for hh2 in range(2):
                    use_pb = (tt * 2 + hh2) % 2 == 0
                    ps = PB[:, 0:1024] if use_pb else PA[:, 0:1024]
                    key = "pb" if use_pb else "pa01"

                    @sec(P, "tensor")
                    def _(t, tt=tt, hh2=hh2, ps=ps, key=key, dslots=dslots):
                        if tt == 0 and hh2 == 0:
                            P.w(t, "dA", dslots[1][1])
                            P.w(t, "cV")
                        if key in ev:
                            P.w(t, *ev[key])
                        for sub, (sd, _dc) in enumerate(dslots):
                            guoff = FC_GU0 if sub == 0 else FC_GU1
                            for it in range(NIT):
                                for n in range(2):
                                    P.inc(t.matmul(
                                        ps[:, n * 512:(n + 1) * 512],
                                        bv(guoff + it * 1024 + tt * 128, 128),
                                        bv(FC_RING + sd * 12288 + it * 2048
                                           + hh2 * 1024 + n * 512, 512),
                                        start=(sub == 0 and it == 0),
                                        stop=(sub == 1 and it == NIT - 1)),
                                        "cT", 1 if (sub == 1 and it == NIT - 1 and n == 1) else 0)

                    @sec(P, "vector")
                    def _(v, tt=tt, hh2=hh2, ps=ps, pair=pair):
                        P.w(v, "cT")
                        dst = fv(FC_OUT + tt * 4096 + hh2 * 2048, 1024)
                        if pair == 0:
                            v.tensor_copy(out=dst, in_=ps)
                        else:
                            v.tensor_tensor(out=dst, in0=dst, in1=ps, op=OP.add)
                    ev[key] = ("cV", P.C["cV"])
            for sd, _dc in dslots:
                ring_c["busy"][sd] = P.C["cT"]

        for tt in range(8):
            gt = ch * 8 + tt

            @sec(P, "sync")
            def _(sy, gt=gt):
                P.w(sy, "cV")
                P.dma(sy, "dA", fv(FC_RST, 2048), L["resid_d"].ap()[gt * 128:(gt + 1) * 128, :])
            rld = P.C["dA"]

            @sec(P, "vector")
            def _(v, tt=tt, rld=rld):
                P.w(v, "dA", rld)
                dst = fv(FC_OUT + tt * 4096, 2048)
                v.tensor_tensor(out=dst, in0=dst, in1=fv(FC_RST, 2048), op=OP.add)

            @sec(P, "sync")
            def _(sy, gt=gt, tt=tt):
                P.w(sy, "cV")
                P.dma(sy, "dA", L["resid_d"].ap()[gt * 128:(gt + 1) * 128, :],
                      fv(FC_OUT + tt * 4096, 2048))


BB_HST = 0
BB_QT = 32768
BB_KT = 65536
BB_VNO = 73728
BB_RING = 82048
BB_PT = 0
BB_CTXT = 16384
BB_STST = 82048
BB_CTXST = 86144
BB_RST = 87296


def emit_attn_b(P):
    L = P.L
    bv, fv = L["bv"], L["fv"]
    PA, PB = L["PA"], L["PB"]
    sc = L["sc_sb"]
    phase_fence(P)

    xbuf = fv(BB_QT, 2048)
    xnst = bv(BB_QT + 8192, 2048)
    for i in range(NT):
        @sec(P, "sync")
        def _(sy, i=i):
            P.w(sy, "cV")
            P.w(sy, "cT")
            P.dma(sy, "dA", xbuf, L["resid_d"].ap()[i * 128:(i + 1) * 128, :])
        ln_tile(P, xbuf, False, xnst)
        transpose16(P, lambda jj: xnst[:, jj * 128:jj * 128 + 128],
                    lambda jj, i=i: bv(BB_HST + jj * 2048 + i * 128, 128))

    ring_c = {"i": 0, "busy": {}}

    def bload(name, colrange, first=False):
        slot = ring_c["i"] % 2
        ring_c["i"] += 1
        busy = ring_c["busy"].get(slot)
        c0, cn = colrange

        @sec(P, "sync")
        def _(g, name=name, slot=slot, busy=busy, first=first, c0=c0, cn=cn):
            if first:
                P.w(g, "cV")
                P.w(g, "cT")
            P.w(g, "dW", P.wcount[name])
            if busy is not None:
                P.w(g, "cT", busy)
            P.dma(g, "dA",
                  bv(BB_RING + slot * 8192, 16 * cn).rearrange("p (s o) -> p s o", o=cn),
                  L["wbf"][name].ap()[:, c0:c0 + cn].rearrange("(s p) o -> p s o", p=128))
        return slot, P.C["dA"]

    def rsl(slot, k, cn=512):
        return bv(BB_RING + slot * 8192 + k * cn, cn)

    @sec(P, "gpsimd")
    def _(g):
        P.w(g, "cT")
        P.w(g, "cV")
        g.memset(bv(BB_VNO, 8256), 1.0)
    vno_ones = P.C["cG"]

    ev = {}
    for q in range(4):
        slot, dq = bload("bWq", (q * 512, 512), first=(q == 0))
        for oo in range(4):
            ot = q * 4 + oo
            for th in range(2):
                half = th

                @sec(P, "tensor")
                def _(t, slot=slot, dq=dq, oo=oo, th=th, half=half, q=q):
                    P.w(t, "dA", dq)
                    if q == 0 and oo == 0 and th == 0:
                        P.w(t, "cV")
                        P.w(t, "cS")
                    if ("q", half) in ev:
                        P.w(t, *ev[("q", half)])
                    pa = PA[:, half * 1024:half * 1024 + 1024]
                    for n in range(2):
                        for k in range(NT):
                            P.inc(t.matmul(pa[:, n * 512:(n + 1) * 512],
                                           rsl(slot, k)[:, oo * 128:oo * 128 + 128],
                                           bv(BB_HST + k * 2048 + th * 1024 + n * 512, 512),
                                           start=(k == 0), stop=(k == NT - 1)),
                                  "cT", 1 if (k == NT - 1 and n == 1) else 0)

                @sec(P, "vector")
                def _(v, ot=ot, th=th, half=half):
                    P.w(v, "cT")
                    v.tensor_scalar(out=bv(BB_QT + ot * 2048 + th * 1024, 1024),
                                    in0=PA[:, half * 1024:half * 1024 + 1024],
                                    scalar1=SQRT_HD, scalar2=None, op0=OP.mult)
                ev[("q", half)] = ("cV", P.C["cV"])
        ring_c["busy"][slot] = P.C["cT"]

    slot, dk = bload("bWk", (0, 512))
    for oo in range(4):
        for th in range(2):
            half = th

            @sec(P, "tensor")
            def _(t, slot=slot, dk=dk, oo=oo, th=th, half=half):
                P.w(t, "dA", dk)
                if ("q", half) in ev:
                    P.w(t, *ev[("q", half)])
                pa = PA[:, half * 1024:half * 1024 + 1024]
                for n in range(2):
                    for k in range(NT):
                        P.inc(t.matmul(pa[:, n * 512:(n + 1) * 512],
                                       rsl(slot, k)[:, oo * 128:oo * 128 + 128],
                                       bv(BB_HST + k * 2048 + th * 1024 + n * 512, 512),
                                       start=(k == 0), stop=(k == NT - 1)),
                              "cT", 1 if (k == NT - 1 and n == 1) else 0)

            @sec(P, "vector")
            def _(v, oo=oo, th=th, half=half):
                P.w(v, "cT")
                v.tensor_copy(out=bv(BB_KT + oo * 2048 + th * 1024, 1024),
                              in_=PA[:, half * 1024:half * 1024 + 1024])
            ev[("q", half)] = ("cV", P.C["cV"])
    ring_c["busy"][slot] = P.C["cT"]

    slot, dv = bload("bWv", (0, 512))
    for tt in range(NT):
        pq = tt % 2

        @sec(P, "tensor")
        def _(t, slot=slot, dv=dv, tt=tt, pq=pq):
            P.w(t, "dA", dv)
            if tt == 0:
                P.w(t, "cV")
                P.w(t, "cS")
                P.w(t, "cG", vno_ones)
            if ("v", pq) in ev:
                P.w(t, *ev[("v", pq)])
            pa = PA[:, pq * 512:pq * 512 + 512]
            for k in range(NT):
                P.inc(t.matmul(pa, bv(BB_HST + k * 2048 + tt * 128, 128),
                               rsl(slot, k), start=(k == 0), stop=(k == NT - 1)),
                      "cT", 1 if k == NT - 1 else 0)

        @sec(P, "vector")
        def _(v, tt=tt, pq=pq):
            P.w(v, "cT")
            P.w(v, "cG", vno_ones)
            for kv in range(NKV):
                v.tensor_copy(out=bv(BB_VNO + tt * 516 + kv * 129, 128),
                              in_=PA[:, pq * 512 + kv * 128:pq * 512 + kv * 128 + 128])
        ev[("v", pq)] = ("cV", P.C["cV"])
    ring_c["busy"][slot] = P.C["cT"]

    STst = [fv(BB_STST, 1024), fv(BB_STST + 2048, 1024)]
    e50 = L["e50_sb"]
    for qh in range(2):
        nkt = 8 * (qh + 1)
        for H in range(NQH):
            kv = H // (NQH // NKV)
            for kc in range(nkt):
                half = kc % 2

                @sec(P, "tensor")
                def _(t, qh=qh, H=H, kc=kc, half=half, kv=kv):
                    if H == 0 and kc == 0:
                        P.w(t, "cV")
                        P.w(t, "cS")
                    if ("st", half) in ev:
                        P.w(t, *ev[("st", half)])
                    pa = PA[:, half * 1024:half * 1024 + 1024]
                    for n in range(2):
                        P.inc(t.matmul(pa[:, n * 512:(n + 1) * 512],
                                       bv(BB_KT + kv * 2048 + kc * 128, 128),
                                       bv(BB_QT + H * 2048 + qh * 1024 + n * 512, 512),
                                       start=True, stop=True),
                              "cT", 1 if n == 1 else 0)

                dl_ = 128 * (kc - 8 * qh)

                @sec(P, "vector")
                def _(v, kc=kc, half=half, dl_=dl_):
                    P.w(v, "cT")
                    P.w(v, "cS")
                    pa = PA[:, half * 1024:half * 1024 + 1024]
                    st = STst[half]
                    lo = max(dl_, 0)
                    if lo < 1024:
                        v.tensor_scalar(out=st[:, lo:1024], in0=pa[:, lo:1024],
                                        scalar1=-50.0, scalar2=50.0, op0=OP.max, op1=OP.min)
                    if dl_ >= 0:
                        v.scalar_tensor_tensor(out=st[:, dl_:dl_ + 128],
                                               in0=st[:, dl_:dl_ + 128], scalar=50.0,
                                               in1=L["triu_sb"][:],
                                               op0=OP.add, op1=OP.mult)
                        v.tensor_scalar(out=st[:, dl_:dl_ + 128], in0=st[:, dl_:dl_ + 128],
                                        scalar1=-50.0, scalar2=None, op0=OP.add)

                @sec(P, "scalar")
                def _(s, kc=kc, half=half, dl_=dl_, H=H):
                    P.w(s, "cV")
                    if H == 0 and kc == 0:
                        P.w(s, "dA")
                        P.w(s, "cT")
                    st = STst[half]
                    pt = bv(BB_PT + kc * 1024, 1024)
                    if dl_ >= 0:
                        if dl_ + 128 < 1024:
                            s.activation(out=pt[:, dl_ + 128:1024], in_=st[:, dl_ + 128:1024],
                                         func=AF.Exp)
                        s.activation(out=pt[:, dl_:dl_ + 128], in_=st[:, dl_:dl_ + 128],
                                     func=AF.Exp)
                    else:
                        s.activation(out=pt, in_=st, func=AF.Exp)
                ev[("st", half)] = ("cS", P.C["cS"])

            for qtl in range(8):
                gq = qh * 8 + qtl
                half = qtl % 2
                psb = PB[:, half * 512:half * 512 + 129]

                @sec(P, "tensor")
                def _(t, qtl=qtl, gq=gq, half=half, psb=psb, kv=kv, nkt=nkt, qh=qh):
                    P.w(t, "cS")
                    if ("ctx", half) in ev:
                        P.w(t, *ev[("ctx", half)])
                    for kc in range(16):
                        if kc < nkt:
                            lhs = (bv(BB_PT + kc * 1024 + qtl * 128, 128) if kc <= gq
                                   else e50[:, 0:128])
                        else:
                            lhs = e50[:, 0:128]
                        P.inc(t.matmul(psb, lhs, bv(BB_VNO + kc * 516 + kv * 129, 129),
                                       start=(kc == 0), stop=(kc == 15)),
                              "cT", 1 if kc == 15 else 0)

                @sec(P, "vector")
                def _(v, qtl=qtl, half=half, psb=psb):
                    P.w(v, "cT")
                    v.reciprocal(out=sc[:, 14:15], in_=PB[:, half * 512 + 128:half * 512 + 129])
                    v.tensor_scalar(out=bv(BB_CTXST + qtl * 128, 128),
                                    in0=PB[:, half * 512:half * 512 + 128],
                                    scalar1=sc[:, 14:15], scalar2=None, op0=OP.mult)
                ev[("ctx", half)] = ("cV", P.C["cV"])

            transpose16(P, lambda jj: bv(BB_CTXST + jj * 128, 128),
                        lambda jj, H=H: bv(BB_CTXT + H * 1024 + jj * 128, 128), n=8)

        for o2q in range(4):
            slot = o2q % 2
            busy = ring_c["busy"].get(("wo", slot))

            @sec(P, "sync")
            def _(g, o2q=o2q, slot=slot, busy=busy, qh=qh):
                P.w(g, "dW", P.wcount["bWo"])
                P.w(g, "cS")
                P.w(g, "cT")
                if busy is not None:
                    P.w(g, "cT", busy)
                P.dma(g, "dA",
                      bv(BB_PT + slot * 8192, 8192).rearrange("p (s o) -> p s o", o=512),
                      L["wbf"]["bWo"].ap()[:, o2q * 512:(o2q + 1) * 512]
                          .rearrange("(s p) o -> p s o", p=128))
            dwo = P.C["dA"]

            for qtl in range(8):
                gq = qh * 8 + qtl
                half = qtl % 2

                @sec(P, "sync")
                def _(sy, gq=gq, o2q=o2q, half=half):
                    P.w(sy, "cV")
                    P.dma(sy, "dA", fv(BB_RST + half * 1024, 512),
                          L["resid_d"].ap()[gq * 128:(gq + 1) * 128, o2q * 512:(o2q + 1) * 512])
                rld = P.C["dA"]

                @sec(P, "tensor")
                def _(t, qtl=qtl, half=half, dwo=dwo, slot=slot, o2q=o2q):
                    P.w(t, "dA", dwo)
                    if o2q == 0 and qtl == 0:
                        P.w(t, "cV")
                        P.w(t, "cS")
                    if ("wo", half) in ev:
                        P.w(t, *ev[("wo", half)])
                    pa = PA[:, half * 512:half * 512 + 512]
                    for k in range(NT):
                        P.inc(t.matmul(pa, bv(BB_CTXT + k * 1024 + qtl * 128, 128),
                                       bv(BB_PT + slot * 8192 + k * 512, 512),
                                       start=(k == 0), stop=(k == NT - 1)),
                              "cT", 1 if k == NT - 1 else 0)

                @sec(P, "vector")
                def _(v, half=half, rld=rld):
                    P.w(v, "cT")
                    P.w(v, "dA", rld)
                    v.tensor_tensor(out=fv(BB_RST + half * 1024, 512),
                                    in0=PA[:, half * 512:half * 512 + 512],
                                    in1=fv(BB_RST + half * 1024, 512), op=OP.add)
                ev[("wo", half)] = ("cV", P.C["cV"])

                @sec(P, "sync")
                def _(sy, gq=gq, o2q=o2q, half=half):
                    P.w(sy, "cV")
                    P.dma(sy, "dA", L["resid_d"].ap()[gq * 128:(gq + 1) * 128,
                                                      o2q * 512:(o2q + 1) * 512],
                          fv(BB_RST + half * 1024, 512))
            ring_c["busy"][("wo", slot)] = P.C["cT"]


def emit_finish(P):
    L = P.L

    @sec(P, "sync")
    def _(sy):
        P.w(sy, "dW")
        P.w(sy, "cV")
        P.w(sy, "cS")
        P.w(sy, "cT")
        P.dma(sy, "dA", L["out"], L["resid_d"].ap()[:, :])
        P.w(sy, "dA")


def kernel(**inputs):
    x = np.ascontiguousarray(inputs["L0_attn_a_x"])
    nc = build(4)
    consts = {
        "c_ident_bf": np.eye(128, dtype=ml_dtypes.bfloat16),
        "c_triu_f32": np.triu(np.ones((128, 128), dtype=np.float32)),
    }
    base = {k: np.ascontiguousarray(v) for k, v in inputs.items() if k != "L0_attn_a_x"}
    in_maps = []
    for c in range(8):
        m = dict(base)
        m.update(consts)
        m["L0_attn_a_x"] = x[c]
        in_maps.append(m)
    res = run_bass_kernel_spmd(nc, in_maps, core_ids=list(range(8)))
    return np.stack([r["out"] for r in res.results], axis=0)


# version 05 (final)
# speedup vs baseline: 1.3324x; 1.0652x over previous
import sys
sys.path.insert(0, "/opt/trn_rl_repo")
import math
from contextlib import ExitStack

import numpy as np
import ml_dtypes

import concourse.bass as bass
import concourse.mybir as mybir
from concourse.alu_op_type import AluOpType as OP
from concourse.bass_utils import run_bass_kernel_spmd

F32 = mybir.dt.float32
BF16 = mybir.dt.bfloat16
AF = mybir.ActivationFunctionType
AX = mybir.AxisListType

D = 2048
T = 2048
NT = 16
LCH = 128
NCH = T // LCH
SUB = 16
NSUB = LCH // SUB
HS = 64
NHEAD = 32
HD = 128
NQH = 16
NKV = 4
DKV = NKV * HD
NEXP = 32
IFF = 768
NIT = IFF // 128
TOPK = 8
LN_EPS = 1e-5
SQRT_HD = math.sqrt(128.0)
E50 = float(np.exp(np.float32(-50.0)))

ARN = 100352

SLOT_A = 0
SLOT_B = 32768
SLOT_C = 65536
TAIL = 92160


class Prog:
    def __init__(self, nc, block, sems, L):
        self.nc = nc
        self.block = block
        self.sem = sems
        self.C = {k: 0 for k in sems}
        self.dmaset = {"dW", "dA"}
        self.L = L
        self.wcount = {}

    def inc(self, instr, name, n=1):
        if name in ("cV", "cS", "cG"):
            return instr
        instr.then_inc(self.sem[name], n * (16 if name in self.dmaset else 1))
        self.C[name] += n
        return instr

    def w(self, eng, name, count=None):
        if count is None:
            count = self.C[name]
        assert count <= self.C[name], f"wait on future count {name} {count} > {self.C[name]}"
        eng.wait_ge(self.sem[name], count * (16 if name in self.dmaset else 1))

    def dma(self, eng, q, out, in_, transpose=False, gate=True, **kw):
        if gate and self.C[q] > 0:
            eng.wait_ge(self.sem[q], self.C[q] * 16)
        if transpose:
            instr = eng.dma_start_transpose(out, in_)
        else:
            instr = eng.dma_start(out=out, in_=in_, **kw)
        self.inc(instr, q)
        return self.C[q]


CHAIN_OPS = {
    "tensor_add", "tensor_mul", "tensor_sub", "tensor_max", "tensor_copy",
    "tensor_tensor", "tensor_scalar", "tensor_scalar_add", "tensor_scalar_mul",
    "tensor_scalar_max", "tensor_scalar_min", "tensor_scalar_sub",
    "tensor_single_scalar", "tensor_reduce", "reduce_max", "reduce_sum",
    "reciprocal", "memset", "memzero", "tensor_tensor_scan",
    "scalar_tensor_tensor", "select", "copy_predicated", "activation",
    "add", "mul", "copy", "sqrt", "square", "sign", "iota",
    "partition_broadcast", "partition_all_reduce", "bn_stats", "bn_aggr",
    "stream_shuffle", "affine_select",
}


class Chained:

    def __init__(self, P, eng, ctr):
        object.__setattr__(self, "_P", P)
        object.__setattr__(self, "_e", eng)
        object.__setattr__(self, "_c", ctr)

    def __getattr__(self, name):
        attr = getattr(self._e, name)
        if name not in CHAIN_OPS:
            return attr
        P, eng, ctr = self._P, self._e, self._c

        def wrap(*a, **kw):
            if P.C[ctr] > 0:
                eng.wait_ge(P.sem[ctr], P.C[ctr])
            instr = attr(*a, **kw)
            instr.then_inc(P.sem[ctr], 1)
            P.C[ctr] += 1
            return instr
        return wrap


_CHAIN_CTR = {"vector": "cV", "scalar": "cS", "gpsimd": "cG"}


def phase_fence(P):
    for eng in ("gpsimd", "vector", "scalar", "tensor", "sync"):
        @sec(P, eng)
        def _(e):
            for s_ in ("dA", "dW", "cT", "cV", "cS", "cG"):
                P.w(e, s_)


def sec(P, engname):
    deco = getattr(P.block, engname)
    ctr = _CHAIN_CTR.get(engname)
    if ctr is None:
        return deco

    def outer(fn):
        def inner(eng):
            return fn(Chained(P, eng, ctr))
        return deco(inner)
    return outer


def ln_tile(P, xt, affine, xn_out):
    L = P.L
    sc = L["sc_sb"]
    uu = L["ln_u"]
    npass = 2 if affine else 1
    src = xt
    for p in range(npass):
        @sec(P, "vector")
        def _(v, src=src):
            P.w(v, "dA")
            P.w(v, "cS")
            v.tensor_tensor(out=L["ln_sq"], in0=src, in1=src, op=OP.mult)
            v.tensor_reduce(out=sc[:, 0:1], in_=src, axis=AX.X, op=OP.add)
            v.tensor_scalar(out=sc[:, 0:1], in0=sc[:, 0:1], scalar1=1.0 / D, scalar2=None, op0=OP.mult)
            v.tensor_reduce(out=sc[:, 1:2], in_=L["ln_sq"], axis=AX.X, op=OP.add)
            v.tensor_scalar(out=sc[:, 1:2], in0=sc[:, 1:2], scalar1=1.0 / D, scalar2=None, op0=OP.mult)
            v.tensor_tensor(out=sc[:, 2:3], in0=sc[:, 0:1], in1=sc[:, 0:1], op=OP.mult)
            v.tensor_tensor(out=sc[:, 1:2], in0=sc[:, 1:2], in1=sc[:, 2:3], op=OP.subtract)
            v.tensor_scalar(out=sc[:, 1:2], in0=sc[:, 1:2], scalar1=LN_EPS, scalar2=None, op0=OP.add)

        @sec(P, "scalar")
        def _(s):
            P.w(s, "cV")
            s.activation(out=sc[:, 3:4], in_=sc[:, 1:2], func=AF.Ln)

        last = (p == npass - 1)

        @sec(P, "vector")
        def _(v, p=p):
            P.w(v, "cS")
            v.tensor_scalar(out=sc[:, 3:4], in0=sc[:, 3:4], scalar1=-0.5, scalar2=None, op0=OP.mult)

        @sec(P, "scalar")
        def _(s):
            P.w(s, "cV")
            s.activation(out=sc[:, 4:5], in_=sc[:, 3:4], func=AF.Exp)

        @sec(P, "vector")
        def _(v, src=src, last=last, p=p):
            P.w(v, "cS")
            if last:
                P.w(v, "cT")
            if not affine:
                P.inc(v.tensor_scalar(out=xn_out, in0=src, scalar1=sc[:, 0:1], scalar2=sc[:, 4:5],
                                      op0=OP.subtract, op1=OP.mult), "cV")
            elif p == 0:
                P.inc(v.tensor_scalar(out=uu, in0=src, scalar1=sc[:, 0:1], scalar2=sc[:, 4:5],
                                      op0=OP.subtract, op1=OP.mult), "cV")
            else:
                v.tensor_scalar(out=uu, in0=uu, scalar1=sc[:, 0:1], scalar2=sc[:, 4:5],
                                op0=OP.subtract, op1=OP.mult)
                v.tensor_tensor(out=uu, in0=uu, in1=L["gB"], op=OP.mult)
                P.inc(v.tensor_tensor(out=xn_out, in0=uu, in1=L["bB"], op=OP.add), "cV")
        src = uu


def transpose16(P, src_fn, dst_fn, n=16):
    L = P.L
    nh = (n + 7) // 8
    for h in range(nh):
        cnt = min(8, n - h * 8)
        pt = L["PT0"] if h % 2 == 0 else L["PT1"]

        @sec(P, "tensor")
        def _(t, h=h, cnt=cnt, pt=pt):
            P.w(t, "cV")
            for j in range(cnt):
                P.inc(t.transpose(pt[:, j * 128:j * 128 + 128], src_fn(h * 8 + j),
                                  L["identb_sb"][:]), "cT", 1 if j == cnt - 1 else 0)

        @sec(P, "vector")
        def _(v, h=h, cnt=cnt, pt=pt):
            P.w(v, "cT")
            for j in range(cnt):
                P.inc(v.tensor_copy(out=dst_fn(h * 8 + j), in_=pt[:, j * 128:j * 128 + 128]),
                      "cV", 1 if j == cnt - 1 else 0)




def build(n_sublayers=4, stage=None, rec_lvl=7, rec_nch=NCH, rec_nsub=NSUB, rec_nhf=2):
    nc = bass.Bass("TRN2", target_bir_lowering=False)

    def din(name, shape):
        return nc.dram_tensor(name, list(shape), F32, kind="ExternalInput").ap()

    L = {"n_sublayers": n_sublayers}
    for nm, shp in [
        ("L0_attn_a_x", (T, D)), ("L0_attn_a_ln_scale", (D,)), ("L0_attn_a_ln_bias", (D,)),
        ("L0_attn_a_Wx", (D, D)), ("L0_attn_a_Ww", (D, D)), ("L0_attn_a_bw", (D,)),
        ("L0_attn_a_Wk", (D, D)), ("L0_attn_a_Wv", (D, D)), ("L0_attn_a_Wr", (D, D)),
        ("L0_attn_a_Wo", (D, D)), ("L0_attn_b_Wq", (D, D)), ("L0_attn_b_Wk", (D, DKV)),
        ("L0_attn_b_Wv", (D, DKV)), ("L0_attn_b_Wo", (D, D)), ("L0_ffn_gate_w", (NEXP, D)),
        ("L0_ffn_w_gate", (NEXP, D, IFF)), ("L0_ffn_w_up", (NEXP, D, IFF)),
        ("L0_ffn_w_down", (NEXP, IFF, D)),
    ]:
        L[nm] = din(nm, shp)
    L["x_in"], L["ln_scale"], L["ln_bias"] = L["L0_attn_a_x"], L["L0_attn_a_ln_scale"], L["L0_attn_a_ln_bias"]
    L["aWx"], L["aWw"], L["a_bw"] = L["L0_attn_a_Wx"], L["L0_attn_a_Ww"], L["L0_attn_a_bw"]
    L["aWk"], L["aWv"], L["aWr"], L["aWo"] = L["L0_attn_a_Wk"], L["L0_attn_a_Wv"], L["L0_attn_a_Wr"], L["L0_attn_a_Wo"]
    L["bWq"], L["bWk"], L["bWv"], L["bWo"] = L["L0_attn_b_Wq"], L["L0_attn_b_Wk"], L["L0_attn_b_Wv"], L["L0_attn_b_Wo"]
    L["gate_w"], L["w_gate"], L["w_up"], L["w_down"] = L["L0_ffn_gate_w"], L["L0_ffn_w_gate"], L["L0_ffn_w_up"], L["L0_ffn_w_down"]

    L["ident_bf"] = nc.dram_tensor("c_ident_bf", [128, 128], BF16, kind="ExternalInput").ap()
    L["triu_f32"] = nc.dram_tensor("c_triu_f32", [128, 128], F32, kind="ExternalInput").ap()
    L["out"] = nc.dram_tensor("out", [T, D], F32, kind="ExternalOutput").ap()

    def scr(name, shape, dt):
        return nc.dram_tensor(name, list(shape), dt)

    L["resid_d"] = scr("resid_d", (T, D), F32)
    wbf = {}
    for nm in ["aWw", "aWk", "aWv", "aWr", "aWo", "bWq", "bWo"]:
        wbf[nm] = scr("wbf_" + nm, (D, D), BF16)
    wbf["bWk"] = scr("wbf_bWk", (D, DKV), BF16)
    wbf["bWv"] = scr("wbf_bWv", (D, DKV), BF16)
    wbf["wg"] = scr("wbf_wg", (NEXP, D, IFF), BF16)
    wbf["wu"] = scr("wbf_wu", (NEXP, D, IFF), BF16)
    wbf["wd"] = scr("wbf_wd", (NEXP, IFF, D), BF16)
    L["wbf"] = wbf
    L["zln_d"] = scr("zln_d", (NCH, 128, D), F32)
    L["kT_d"] = scr("kT_d", (NCH, 128, D), BF16)
    L["rT_d"] = scr("rT_d", (NCH, 128, D), BF16)
    L["vN_d"] = scr("vN_d", (NCH, 128, D), BF16)
    L["xnT_d"] = scr("xnT_d", (NT, 128, D), BF16)
    L["fwT_d"] = scr("fwT_d", (NEXP, T), F32)

    st = ExitStack()
    with st:
        e = st.enter_context
        AR = e(nc.sbuf_tensor("AR", [128, ARN], BF16))
        L["AR"] = AR
        L["state_sb"] = e(nc.sbuf_tensor("state_sb", [128, NT * HS], F32))
        L["state_bf"] = e(nc.sbuf_tensor("state_bf", [128, NT * HS], BF16))
        L["identb_sb"] = e(nc.sbuf_tensor("identb_sb", [128, 128], BF16))
        L["triu_sb"] = e(nc.sbuf_tensor("triu_sb", [128, 128], F32))
        L["e50_sb"] = e(nc.sbuf_tensor("e50_sb", [128, 128], BF16))
        L["gwT_sb"] = e(nc.sbuf_tensor("gwT_sb", [128, NT * NEXP], BF16))
        L["fw_sb"] = e(nc.sbuf_tensor("fw_sb", [128, NT * NEXP], F32))
        L["sc_sb"] = e(nc.sbuf_tensor("sc_sb", [128, 64], F32))
        L["bw_sb"] = e(nc.sbuf_tensor("bw_sb", [128, NT], F32))
        L["eL_sb"] = e(nc.sbuf_tensor("eL_sb", [128, NT], F32))
        L["rt_sb"] = e(nc.sbuf_tensor("rt_sb", [128, 96], F32))
        L["identf_sb"] = e(nc.sbuf_tensor("identf_sb", [128, 128], F32))
        L["PA"] = e(nc.psum_tensor("PA", [128, 2048], F32))
        L["PB"] = e(nc.psum_tensor("PB", [128, 1024], F32))
        L["PT0"] = e(nc.psum_tensor("PT0", [128, 1024], BF16))
        L["PT1"] = e(nc.psum_tensor("PT1", [128, 1024], BF16))
        sems = {}
        for s_ in ["dW", "dA", "cT", "cV", "cS", "cG"]:
            sems[s_] = e(nc.semaphore(s_))
        block = e(nc.Block())

        P = Prog(nc, block, sems, L)
        ARf = AR.ap().bitcast(F32)

        def bv(off, n):
            assert off + n <= ARN
            return AR[:, off:off + n]

        def fv(off, n):
            assert off % 2 == 0 and off + 2 * n <= ARN
            return ARf[:, off // 2: off // 2 + n]

        L["bv"] = bv
        L["fv"] = fv
        L["ln_sq"] = fv(TAIL, 2048)
        L["ln_u"] = fv(TAIL + 4096, 2048)

        L["stage"] = stage
        L["rec_lvl"] = rec_lvl
        L["rec_nch"] = rec_nch
        L["rec_nsub"] = rec_nsub
        L["rec_nhf"] = rec_nhf
        emit_consts_and_w0(P)
        if stage != "w0":
            emit_attn_a(P)
        if n_sublayers >= 2:
            emit_ffn(P, 0)
        if n_sublayers >= 3:
            emit_attn_b(P)
        if n_sublayers >= 4:
            emit_ffn(P, 1)
        emit_finish(P)

    return nc


def emit_consts_and_w0(P):
    L = P.L
    wbf = L["wbf"]

    @sec(P, "sync")
    def _(sy):
        P.dma(sy, "dA", L["identb_sb"][:], L["ident_bf"])
        P.dma(sy, "dA", L["triu_sb"][:], L["triu_f32"])
        P.dma(sy, "dA", L["bw_sb"][:],
              bass.AP(L["a_bw"].tensor, 0, [[1, 128], [128, NT]]),
              allow_slow_non_contiguous=True)

    @sec(P, "gpsimd")
    def _(g):
        g.memset(L["e50_sb"][:], E50)
        g.memset(L["state_sb"][:], 0.0)
        P.inc(g.memset(L["state_bf"][:], 0.0), "cG")
        for nm in ["aWw", "aWk", "aWv", "aWr", "aWo"]:
            P.wcount[nm] = P.dma(g, "dW", wbf[nm].ap(), L[nm])
        for ei in range(NEXP):
            P.dma(g, "dW", wbf["wg"].ap()[ei], L["w_gate"][ei])
            P.dma(g, "dW", wbf["wu"].ap()[ei], L["w_up"][ei])
            P.wcount[f"ffn{ei}"] = P.dma(g, "dW", wbf["wd"].ap()[ei], L["w_down"][ei])
        for nm in ["bWq", "bWk", "bWv", "bWo"]:
            P.wcount[nm] = P.dma(g, "dW", wbf[nm].ap(), L[nm])




def wt_quarter_load(P, slot, wdram, q, pre_wait=None):
    L = P.L
    bv = L["bv"]

    @sec(P, "sync")
    def _(sy):
        if pre_wait:
            for args in pre_wait:
                P.w(sy, *args)
        for j in range(NT):
            P.dma(sy, "dA", bv(SLOT_C + slot * 8192 + j * 512, 512),
                  wdram.ap()[q * 512:(q + 1) * 512, j * 128:j * 128 + 128],
                  transpose=True, gate=(j == 0))
    return P.C["dA"]


def emit_attn_a(P):
    L = P.L
    bv, fv = L["bv"], L["fv"]
    PA, PB = L["PA"], L["PB"]

    def wq(slot, j):
        return bv(SLOT_C + slot * 8192 + j * 512, 512)

    ring_n = {"i": 0}

    def next_slot(wname=None):
        pw = []
        if wname is not None:
            pw.append(("dW", P.wcount[wname]))
        if ring_n["i"] >= 3:
            pw.append(("cT", P.C["cT"]))
        slot = ring_n["i"] % 3
        ring_n["i"] += 1
        return slot, pw

    @sec(P, "gpsimd")
    def _(g):
        for j in range(NT):
            P.dma(g, "dW", bv(SLOT_A + j * 2048, 2048),
                  L["aWx"][j * 128:(j + 1) * 128, :], gate=(j == 0))
    wx_done = P.C["dW"]

    ev_cnt = {}
    for q in range(4):
        slot, pw = next_slot("aWw" if q == 0 else None)
        dbc = wt_quarter_load(P, slot, L["wbf"]["aWw"], q, pre_wait=pw)
        for dt in range(NT):
            pq = dt % 2

            @sec(P, "tensor")
            def _(t, q=q, dt=dt, pq=pq, slot=slot, dbc=dbc):
                if q == 0 and dt == 0:
                    P.w(t, "cV")
                    P.w(t, "dW", wx_done)
                if dt == 0:
                    P.w(t, "dA", dbc)
                if pq in ev_cnt:
                    P.w(t, *ev_cnt[pq])
                pa = PA[:, pq * 512:pq * 512 + 512]
                for k in range(NT):
                    P.inc(t.matmul(pa, bv(SLOT_A + k * 2048 + dt * 128, 128),
                                   wq(slot, k), start=(k == 0), stop=(k == NT - 1)),
                          "cT", 1 if k == NT - 1 else 0)

            @sec(P, "vector")
            def _(v, q=q, dt=dt, pq=pq):
                P.w(v, "cT")
                P.inc(v.tensor_copy(out=bv(SLOT_B + dt * 2048 + q * 512, 512),
                                    in_=PA[:, pq * 512:pq * 512 + 512]), "cV")
            ev_cnt[pq] = ("cV", P.C["cV"])

    if L.get("stage") == "wz":
        return
    L["gB"] = fv(SLOT_C, 2048)
    L["bB"] = fv(SLOT_C + 4096, 2048)
    xbuf = fv(SLOT_C + 8192, 2048)
    xnst = bv(SLOT_C + 16384, 2048)

    @sec(P, "sync")
    def _(sy):
        P.w(sy, "cV")
        P.dma(sy, "dA", L["gB"], bass.AP(L["ln_scale"].tensor, 0, [[0, 128], [1, 2048]]))
        P.dma(sy, "dA", L["bB"], bass.AP(L["ln_bias"].tensor, 0, [[0, 128], [1, 2048]]))

    for i in range(NT):
        @sec(P, "sync")
        def _(sy, i=i):
            P.w(sy, "cV")
            P.dma(sy, "dA", xbuf, L["x_in"][i * 128:(i + 1) * 128, :])
        ln_tile(P, xbuf, True, xnst)
        transpose16(P, lambda jj: xnst[:, jj * 128:jj * 128 + 128],
                    lambda jj, i=i: bv(SLOT_A + jj * 2048 + i * 128, 128))

    if L.get("stage") == "ln":
        return
    zlB = [fv(TAIL, 1024), fv(TAIL + 4096, 1024)]
    zrB = [fv(TAIL + 2048, 1024), fv(TAIL + 6144, 1024)]
    for ot in range(NT):
        for th in range(2):
            half = (2 * ot + th) % 2

            @sec(P, "tensor")
            def _(t, ot=ot, th=th, half=half):
                if ot == 0 and th == 0:
                    P.w(t, "cV")
                if half in ev_cnt:
                    P.w(t, *ev_cnt[half])
                pa = PA[:, half * 1024:half * 1024 + 1024]
                for n in range(2):
                    for k in range(NT):
                        P.inc(t.matmul(pa[:, n * 512:(n + 1) * 512],
                                       bv(SLOT_B + k * 2048 + ot * 128, 128),
                                       bv(SLOT_A + k * 2048 + th * 1024 + n * 512, 512),
                                       start=(k == 0), stop=(k == NT - 1)),
                              "cT", 1 if (k == NT - 1 and n == 1) else 0)

            zl = zlB[half]
            zr = zrB[half]

            @sec(P, "scalar")
            def _(s, ot=ot, half=half, zl=zl):
                P.w(s, "cT")
                P.w(s, "cV")
                pa = PA[:, half * 1024:half * 1024 + 1024]
                s.activation(out=zl, in_=pa, func=AF.Abs, bias=L["bw_sb"][:, ot:ot + 1])
                s.activation(out=zl, in_=zl, func=AF.Exp, scale=-1.0)
                P.inc(s.activation(out=zl, in_=zl, func=AF.Ln, bias=1.0), "cS")

            @sec(P, "vector")
            def _(v, ot=ot, half=half, zl=zl, zr=zr):
                P.w(v, "cS")
                P.w(v, "dA")
                pa = PA[:, half * 1024:half * 1024 + 1024]
                v.tensor_scalar(out=zr, in0=pa, scalar1=L["bw_sb"][:, ot:ot + 1],
                                scalar2=0.0, op0=OP.add, op1=OP.max)
                P.inc(v.scalar_tensor_tensor(out=zr, in0=zr, scalar=-1.0, in1=zl,
                                             op0=OP.mult, op1=OP.subtract), "cV")
            ev_cnt[half] = ("cV", P.C["cV"])

            @sec(P, "sync")
            def _(sy, ot=ot, th=th, zr=zr):
                P.w(sy, "cV")
                P.dma(sy, "dA",
                      L["zln_d"].ap()[th * 8:(th + 1) * 8, :, ot * 128:ot * 128 + 128]
                          .rearrange("c p n -> p c n"),
                      zr.rearrange("p (c n) -> p c n", n=128))

    if L.get("stage") == "z":
        return
    kst = [bv(SLOT_B, 2048), bv(SLOT_B + 2048, 2048)]
    vst = [bv(SLOT_B + 4096, 512), bv(SLOT_B + 4608, 512)]

    for wname, kind in [("aWk", "k"), ("aWv", "v"), ("aWr", "r")]:
        for q in range(4):
            slot, pw = next_slot(wname if q == 0 else None)
            dbc = wt_quarter_load(P, slot, L["wbf"][wname], q, pre_wait=pw)

            if kind in ("k", "r"):
                for oo in range(4):
                    ot = q * 4 + oo
                    stg = kst[oo % 2]
                    for th in range(2):
                        half = th

                        @sec(P, "tensor")
                        def _(t, slot=slot, dbc=dbc, oo=oo, th=th, half=half, q=q):
                            P.w(t, "dA", dbc)
                            if q == 0 and oo == 0 and th == 0:
                                P.w(t, "cV")
                                P.w(t, "cS")
                            if half in ev_cnt:
                                P.w(t, *ev_cnt[half])
                            pa = PA[:, half * 1024:half * 1024 + 1024]
                            for n in range(2):
                                for k in range(NT):
                                    P.inc(t.matmul(pa[:, n * 512:(n + 1) * 512],
                                                   wq(slot, k)[:, oo * 128:oo * 128 + 128],
                                                   bv(SLOT_A + k * 2048 + th * 1024 + n * 512, 512),
                                                   start=(k == 0), stop=(k == NT - 1)),
                                          "cT", 1 if (k == NT - 1 and n == 1) else 0)

                        if kind == "k":
                            @sec(P, "vector")
                            def _(v, th=th, half=half, stg=stg):
                                P.w(v, "cT")
                                P.w(v, "dA")
                                P.inc(v.tensor_copy(out=stg[:, th * 1024:(th + 1) * 1024],
                                                    in_=PA[:, half * 1024:half * 1024 + 1024]), "cV")
                            ev_cnt[half] = ("cV", P.C["cV"])
                        else:
                            @sec(P, "scalar")
                            def _(s, th=th, half=half, stg=stg):
                                P.w(s, "cT")
                                P.w(s, "dA")
                                P.inc(s.activation(out=stg[:, th * 1024:(th + 1) * 1024],
                                                   in_=PA[:, half * 1024:half * 1024 + 1024],
                                                   func=AF.Sigmoid), "cS")
                            ev_cnt[half] = ("cS", P.C["cS"])

                    dstd = L["kT_d"] if kind == "k" else L["rT_d"]

                    @sec(P, "sync")
                    def _(sy, ot=ot, dstd=dstd, kind=kind, stg=stg):
                        P.w(sy, "cV" if kind == "k" else "cS")
                        P.dma(sy, "dA",
                              dstd.ap()[:, :, ot * 128:ot * 128 + 128]
                                  .rearrange("c p n -> p c n"),
                              stg.rearrange("p (c n) -> p c n", n=128))
            else:
                for tt in range(NT):
                    pq = tt % 2
                    stg = vst[tt % 2]

                    @sec(P, "tensor")
                    def _(t, slot=slot, dbc=dbc, tt=tt, pq=pq, q=q):
                        P.w(t, "dA", dbc)
                        if q == 0 and tt == 0:
                            P.w(t, "cV")
                            P.w(t, "cS")
                        key = ("vq", pq)
                        if key in ev_cnt:
                            P.w(t, *ev_cnt[key])
                        pa = PA[:, 1024 + pq * 512:1024 + pq * 512 + 512]
                        for k in range(NT):
                            P.inc(t.matmul(pa, bv(SLOT_A + k * 2048 + tt * 128, 128),
                                           wq(slot, k), start=(k == 0), stop=(k == NT - 1)),
                                  "cT", 1 if k == NT - 1 else 0)

                    @sec(P, "vector")
                    def _(v, tt=tt, pq=pq, stg=stg):
                        P.w(v, "cT")
                        P.w(v, "dA")
                        P.inc(v.tensor_copy(out=stg, in_=PA[:, 1024 + pq * 512:1024 + pq * 512 + 512]), "cV")
                    ev_cnt[("vq", pq)] = ("cV", P.C["cV"])

                    @sec(P, "sync")
                    def _(sy, tt=tt, q=q, stg=stg):
                        P.w(sy, "cV")
                        P.dma(sy, "dA", L["vN_d"].ap()[tt, :, q * 512:q * 512 + 512], stg)

    if L.get("stage") == "kvr":
        return
    emit_attn_a_rec(P)
    if L.get("stage") == "rec":
        return
    emit_attn_a_out(P)


RB_LNA = SLOT_B
RB_CT = SLOT_B + 4096
RB_KT = SLOT_B + 8192
RB_RT = SLOT_B + 10240
RB_VN = SLOT_B + 12288
RB_RT2 = SLOT_B + 14336
RB_KHT = SLOT_B + 16384
RB_KHN = SLOT_B + 18432
RB_MARG = SLOT_B + 20480
RB_EB = SLOT_B + 24576
RB_KB = SLOT_B + 26624
RB_RH = SLOT_B + 28672
RB_ASB = SLOT_B + 28928
RB_ZERO = SLOT_B + 30976


def emit_attn_a_rec(P):
    L = P.L
    lvl = L.get("rec_lvl", 7)
    nch = L.get("rec_nch", NCH)
    nsub = L.get("rec_nsub", NSUB)
    nhf = L.get("rec_nhf", 2)
    bv, fv = L["bv"], L["fv"]
    PA, PB = L["PA"], L["PB"]
    lnaB = [fv(RB_LNA, 2048), fv(SLOT_C, 2048)]
    cT = fv(RB_CT, 2048)
    cT3 = cT.rearrange("p (j s) -> p j s", s=128)
    kTB = [bv(RB_KT, 2048), bv(SLOT_C + 4096, 2048)]
    rTB = [bv(RB_RT, 2048), bv(SLOT_C + 6144, 2048)]
    vNB = [bv(RB_VN, 2048), bv(SLOT_C + 8192, 2048)]
    rt2 = bv(RB_RT2, 2048)
    khT = bv(RB_KHT, 2048)
    khN = bv(RB_KHN, 2048)
    marg = fv(RB_MARG, 2048)
    marg3 = marg.rearrange("p (j s) -> p j s", s=128)
    Eb = bv(RB_EB, 2048)
    kb = bv(RB_KB, 2048)
    rh = bv(RB_RH, 256)
    Asb = bv(RB_ASB, 2048)
    zero = fv(RB_ZERO, 128)
    cL_b = cT3[:, :, 127:128].broadcast_to((128, NT, 128))

    @sec(P, "vector")
    def _(v):
        P.w(v, "cS")
        P.w(v, "dA")
        P.inc(v.memset(zero, 0.0), "cV")

    chunk_ld = {}

    def load_chunk(cc):
        @sec(P, "sync")
        def _(sy, cc=cc):
            P.w(sy, "cT")
            P.dma(sy, "dA", lnaB[cc % 2], L["zln_d"].ap()[cc])
            P.dma(sy, "dA", kTB[cc % 2], L["kT_d"].ap()[cc], gate=False)
            P.dma(sy, "dA", rTB[cc % 2], L["rT_d"].ap()[cc], gate=False)
            P.dma(sy, "dA", vNB[cc % 2], L["vN_d"].ap()[cc], gate=False)
        chunk_ld[cc] = P.C["dA"]

    load_chunk(0)
    for c in range(nch):
        lna = lnaB[c % 2]
        kT = kTB[c % 2]
        rT = rTB[c % 2]
        rT3 = rT.rearrange("p (j s) -> p j s", s=128)
        vN = vNB[c % 2]
        if c + 1 < nch:
            load_chunk(c + 1)

        @sec(P, "vector")
        def _(v, c=c):
            P.w(v, "dA", chunk_ld[c])
            for j in range(NT):
                P.inc(v.tensor_tensor_scan(out=cT[:, j * 128:(j + 1) * 128],
                                           data0=lna[:, j * 128:(j + 1) * 128],
                                           data1=zero, initial=0.0,
                                           op0=OP.add, op1=OP.add),
                      "cV", 1 if j == NT - 1 else 0)

        if lvl < 2:
            continue
        @sec(P, "scalar")
        def _(s):
            P.w(s, "cV")
            P.inc(s.activation(out=Eb, in_=cT, func=AF.Exp), "cS")
            P.inc(s.activation(out=L["eL_sb"][:].rearrange("p (j o) -> p j o", o=1),
                               in_=cT3[:, :, 127:128], func=AF.Exp), "cS")

        @sec(P, "vector")
        def _(v):
            P.w(v, "cS")
            v.tensor_tensor(out=rt2, in0=rT, in1=Eb, op=OP.mult)
            P.inc(v.tensor_tensor(out=marg3, in0=cL_b, in1=cT3, op=OP.subtract), "cV")

        @sec(P, "scalar")
        def _(s):
            P.w(s, "cV")
            P.inc(s.activation(out=Eb, in_=marg, func=AF.Exp), "cS")

        @sec(P, "vector")
        def _(v):
            P.w(v, "cS")
            P.inc(v.tensor_tensor(out=khT, in0=kT, in1=Eb, op=OP.mult), "cV")

        transpose16(P, lambda jj: khT[:, jj * 128:jj * 128 + 128],
                    lambda jj: khN[:, jj * 128:jj * 128 + 128])
        if lvl < 3:
            continue

        for hf in range(nhf):
            j0 = hf * 8
            for i in range(nsub):
                @sec(P, "vector")
                def _(v, i=i, j0=j0):
                    P.w(v, "cS")
                    mh = marg[:, j0 * 128:(j0 + 8) * 128]
                    mh3 = marg3[:, j0:j0 + 8, :]
                    ch = cT[:, j0 * 128:(j0 + 8) * 128]
                    ch3 = cT3[:, j0:j0 + 8, :]
                    if i == 0:
                        P.inc(v.tensor_scalar(out=mh, in0=ch, scalar1=-80.0, scalar2=None,
                                              op0=OP.max), "cV")
                    else:
                        bview = cT3[:, j0:j0 + 8, 16 * i - 1:16 * i].broadcast_to((128, 8, 128))
                        v.tensor_tensor(out=mh3, in0=ch3, in1=bview, op=OP.subtract)
                        P.inc(v.tensor_scalar(out=mh, in0=mh, scalar1=-80.0, scalar2=None,
                                              op0=OP.max), "cV")

                @sec(P, "scalar")
                def _(s, i=i, j0=j0):
                    P.w(s, "cV")
                    P.inc(s.activation(out=Eb[:, j0 * 128:(j0 + 8) * 128],
                                       in_=marg[:, j0 * 128:(j0 + 8) * 128],
                                       func=AF.Exp, scale=-1.0), "cS")
                    P.inc(s.activation(
                        out=rh[:, 0:128].rearrange("p (j s) -> p j s", s=16),
                        in_=marg3[:, j0:j0 + 8, 16 * i:16 * i + 16],
                        func=AF.Exp), "cS")

                @sec(P, "vector")
                def _(v, i=i, j0=j0):
                    P.w(v, "cS")
                    P.w(v, "cT")
                    v.tensor_tensor(out=kb[:, j0 * 128:(j0 + 8) * 128],
                                    in0=kT[:, j0 * 128:(j0 + 8) * 128],
                                    in1=Eb[:, j0 * 128:(j0 + 8) * 128], op=OP.mult)
                    P.inc(v.tensor_tensor(
                        out=rh[:, 128:256].rearrange("p (j s) -> p j s", s=16),
                        in0=rT3[:, j0:j0 + 8, 16 * i:16 * i + 16],
                        in1=rh[:, 0:128].rearrange("p (j s) -> p j s", s=16),
                        op=OP.mult), "cV")

                if lvl < 4:
                    continue

                @sec(P, "tensor")
                def _(t, i=i, j0=j0):
                    P.w(t, "cV")
                    for jj in range(8):
                        for hp in range(2):
                            hh = 2 * jj + hp
                            P.w(t, "cT")
                            P.inc(t.matmul(
                                PA[:, hh * 128 + 16 * i: hh * 128 + 16 * (i + 1)],
                                kb[hp * 64:hp * 64 + 64,
                                   (j0 + jj) * 128:(j0 + jj) * 128 + 128],
                                rh[hp * 64:hp * 64 + 64, 128 + jj * 16:128 + jj * 16 + 16],
                                start=True, stop=True), "cT", 1)

            if lvl < 5:
                continue
            @sec(P, "vector")
            def _(v, j0=j0):
                P.w(v, "cT")
                for hh in range(16):
                    P.inc(v.tensor_tensor(out=Asb[:, hh * 128:(hh + 1) * 128],
                                          in0=PA[:, hh * 128:(hh + 1) * 128],
                                          in1=L["triu_sb"][:], op=OP.mult),
                          "cV", 1 if hh == 15 else 0)

            if lvl < 6:
                continue
            @sec(P, "tensor")
            def _(t, j0=j0, hf=hf, c=c):
                P.w(t, "cV")
                if c == 0 and hf == 0:
                    P.w(t, "cG")
                for pr in range(8):
                    jj = j0 + pr
                    for hp in range(2):
                        h = 2 * jj + hp
                        hh = 2 * pr + hp
                        ps = PB[hp * 64:hp * 64 + 64, pr * 128:pr * 128 + 128]
                        P.w(t, "cT")
                        P.inc(t.matmul(ps, vN[:, h * 64:h * 64 + 64],
                                       Asb[:, hh * 128:(hh + 1) * 128],
                                       start=True, stop=False), "cT", 1)
                        P.w(t, "cT")
                        P.inc(t.matmul(ps,
                                       L["state_bf"][hp * 64:hp * 64 + 64, jj * 64:jj * 64 + 64],
                                       rt2[hp * 64:hp * 64 + 64, jj * 128:jj * 128 + 128],
                                       start=False, stop=True), "cT", 1)

            @sec(P, "vector")
            def _(v, j0=j0, c=c):
                P.w(v, "cT")
                for pr in range(8):
                    jj = j0 + pr
                    P.inc(v.tensor_copy(out=bv(SLOT_A + jj * 2048 + c * 128, 128),
                                        in_=PB[:, pr * 128:pr * 128 + 128]),
                          "cV", 1 if pr == 7 else 0)

        if lvl < 7:
            continue
        @sec(P, "tensor")
        def _(t, c=c):
            P.w(t, "cV")
            for pr in range(8):
                for hp in range(2):
                    h = 2 * pr + hp
                    P.w(t, "cT")
                    P.inc(t.matmul(PB[hp * 64:hp * 64 + 64, pr * 64:pr * 64 + 64],
                                   khN[:, h * 64:h * 64 + 64],
                                   vN[:, h * 64:h * 64 + 64],
                                   start=True, stop=True), "cT", 1)
            for pr in range(8, 16):
                for hp in range(2):
                    h = 2 * pr + hp
                    P.w(t, "cT")
                    P.inc(t.matmul(PA[hp * 64:hp * 64 + 64, (pr - 8) * 64:(pr - 8) * 64 + 64],
                                   khN[:, h * 64:h * 64 + 64],
                                   vN[:, h * 64:h * 64 + 64],
                                   start=True, stop=True), "cT", 1)

        @sec(P, "vector")
        def _(v, c=c):
            P.w(v, "cT")
            for j in range(NT):
                v.tensor_scalar(out=L["state_sb"][:, j * 64:(j + 1) * 64],
                                in0=L["state_sb"][:, j * 64:(j + 1) * 64],
                                scalar1=L["eL_sb"][:, j:j + 1], scalar2=None, op0=OP.mult)
                src = PB[:, j * 64:(j + 1) * 64] if j < 8 else PA[:, (j - 8) * 64:(j - 8) * 64 + 64]
                v.tensor_tensor(out=L["state_sb"][:, j * 64:(j + 1) * 64],
                                in0=L["state_sb"][:, j * 64:(j + 1) * 64],
                                in1=src, op=OP.add)
                P.inc(v.tensor_copy(out=L["state_bf"][:, j * 64:(j + 1) * 64],
                                    in_=L["state_sb"][:, j * 64:(j + 1) * 64]),
                      "cV", 1 if j == NT - 1 else 0)


def emit_attn_a_out(P):
    L = P.L
    bv, fv = L["bv"], L["fv"]
    PA = L["PA"]

    xb = [fv(SLOT_B, 512), fv(SLOT_B + 1024, 512)]
    yb = [fv(SLOT_B + 2048, 512), fv(SLOT_B + 3072, 512)]

    ev = {}
    for o2q in range(4):
        pw = [("dW", P.wcount["aWo"]), ("cT", P.C["cT"])]
        slot = o2q % 3

        @sec(P, "sync")
        def _(sy, pw=pw):
            for args in pw:
                P.w(sy, *args)
        dbc = wt_quarter_load(P, slot, L["wbf"]["aWo"], o2q)
        for tt in range(NT):
            pq = tt % 2

            @sec(P, "sync")
            def _(sy, tt=tt, o2q=o2q, pq=pq):
                P.w(sy, "cV")
                P.dma(sy, "dA", xb[pq],
                      L["x_in"][tt * 128:(tt + 1) * 128, o2q * 512:o2q * 512 + 512])

            @sec(P, "tensor")
            def _(t, tt=tt, pq=pq, dbc=dbc, o2q=o2q, slot=slot):
                P.w(t, "dA", dbc)
                if o2q == 0 and tt == 0:
                    P.w(t, "cV")
                if pq in ev:
                    P.w(t, *ev[pq])
                pa = PA[:, pq * 512:pq * 512 + 512]
                for k in range(NT):
                    P.inc(t.matmul(pa, bv(SLOT_A + k * 2048 + tt * 128, 128),
                                   wq_slot(L, slot, k), start=(k == 0), stop=(k == NT - 1)),
                          "cT", 1 if k == NT - 1 else 0)

            @sec(P, "vector")
            def _(v, tt=tt, pq=pq, o2q=o2q):
                P.w(v, "cT")
                P.w(v, "dA")
                P.inc(v.tensor_tensor(out=yb[pq], in0=PA[:, pq * 512:pq * 512 + 512],
                                      in1=xb[pq], op=OP.add), "cV")
            ev[pq] = ("cV", P.C["cV"])

            @sec(P, "sync")
            def _(sy, tt=tt, pq=pq, o2q=o2q):
                P.w(sy, "cV")
                P.dma(sy, "dA", L["resid_d"].ap()[tt * 128:(tt + 1) * 128,
                                                  o2q * 512:o2q * 512 + 512], yb[pq])


def wq_slot(L, slot, j):
    return L["bv"](SLOT_C + slot * 8192 + j * 512, 512)


FC_XNT = 0
FC_RING = 16384
FC_G = 40960
FC_GU0 = 47104
FC_GU1 = 53248
FC_FW = 59392
FC_OUT = 63488
FC_RST = 96256
FL_XB = SLOT_C + 8192
FL_XN = SLOT_C + 16384
FL_XNT = SLOT_C + 18432
FL_GW = 16384
FL_FWT = 20480


def emit_ffn(P, call):
    L = P.L
    bv, fv = L["bv"], L["fv"]
    PA, PB = L["PA"], L["PB"]
    rt = L["rt_sb"]
    sc = L["sc_sb"]
    phase_fence(P)

    if call == 0:
        gwst = bv(FL_GW, 2048)

        @sec(P, "gpsimd")
        def _(g):
            P.w(g, "cT")
            P.dma(g, "dW", gwst[0:NEXP, :], L["gate_w"])
        gwc = P.C["dW"]

        @sec(P, "vector")
        def _(v):
            v.tensor_copy(out=L["identf_sb"][:], in_=L["identb_sb"][:])

        for h in range(2):
            pt = L["PT0"] if h % 2 == 0 else L["PT1"]

            @sec(P, "tensor")
            def _(t, h=h, pt=pt):
                P.w(t, "dW", gwc)
                P.w(t, "cV")
                for j in range(8):
                    jj = h * 8 + j
                    P.w(t, "cT")
                    P.inc(t.transpose(pt[:, j * 128:j * 128 + 32],
                                      gwst[0:NEXP, jj * 128:jj * 128 + 128],
                                      L["identb_sb"][0:NEXP, 0:NEXP]), "cT", 1)

            @sec(P, "vector")
            def _(v, h=h, pt=pt):
                P.w(v, "cT")
                for j in range(8):
                    jj = h * 8 + j
                    v.tensor_copy(out=L["gwT_sb"][:, jj * 32:(jj + 1) * 32],
                                  in_=pt[:, j * 128:j * 128 + 32])

    xbuf = fv(FL_XB, 2048)
    xnst = bv(FL_XN, 2048)
    xntst = bv(FL_XNT, 2048)
    for i in range(NT):
        @sec(P, "sync")
        def _(sy, i=i):
            P.w(sy, "cV")
            P.w(sy, "cT")
            P.dma(sy, "dA", xbuf, L["resid_d"].ap()[i * 128:(i + 1) * 128, :])
        ln_tile(P, xbuf, False, xnst)
        transpose16(P, lambda jj: xnst[:, jj * 128:jj * 128 + 128],
                    lambda jj: xntst[:, jj * 128:jj * 128 + 128])

        @sec(P, "sync")
        def _(sy, i=i):
            P.w(sy, "cV")
            P.dma(sy, "dA", L["xnT_d"].ap()[i], xntst)

        @sec(P, "tensor")
        def _(t, i=i):
            P.w(t, "cV")
            for k in range(NT):
                P.inc(t.matmul(PB[:, 0:32], xntst[:, k * 128:k * 128 + 128],
                               L["gwT_sb"][:, k * 32:(k + 1) * 32],
                               start=(k == 0), stop=(k == NT - 1)),
                      "cT", 1 if k == NT - 1 else 0)

        @sec(P, "vector")
        def _(v, i=i):
            P.w(v, "cT")
            v.tensor_copy(out=rt[:, 0:32], in_=PB[:, 0:32])
            v.tensor_reduce(out=sc[:, 8:9], in_=rt[:, 0:32], axis=AX.X, op=OP.max)
            v.tensor_scalar(out=sc[:, 8:9], in0=sc[:, 8:9], scalar1=-1.0, scalar2=None, op0=OP.mult)

        @sec(P, "scalar")
        def _(s, i=i):
            P.w(s, "cV")
            s.activation(out=rt[:, 32:64], in_=rt[:, 0:32], func=AF.Exp,
                         bias=sc[:, 8:9], accum_out=sc[:, 9:10])

        @sec(P, "vector")
        def _(v, i=i):
            P.w(v, "cS")
            v.reciprocal(out=sc[:, 10:11], in_=sc[:, 9:10])
            v.tensor_scalar(out=rt[:, 0:32], in0=rt[:, 32:64], scalar1=sc[:, 10:11],
                            scalar2=None, op0=OP.mult)
            v.tensor_copy(out=rt[:, 32:64], in_=rt[:, 0:32])
            for kk in range(TOPK):
                v.tensor_reduce(out=sc[:, 11:12], in_=rt[:, 32:64], axis=AX.X, op=OP.max)
                if kk < TOPK - 1:
                    v.tensor_scalar(out=rt[:, 64:96], in0=rt[:, 32:64], scalar1=sc[:, 11:12],
                                    scalar2=None, op0=OP.is_ge)
                    v.scalar_tensor_tensor(out=rt[:, 32:64], in0=rt[:, 64:96], scalar=-2.0,
                                           in1=rt[:, 32:64], op0=OP.mult, op1=OP.add)
            v.tensor_scalar(out=rt[:, 64:96], in0=rt[:, 0:32], scalar1=sc[:, 11:12],
                            scalar2=None, op0=OP.is_ge)
            v.tensor_tensor(out=rt[:, 32:64], in0=rt[:, 0:32], in1=rt[:, 64:96], op=OP.mult)
            v.tensor_reduce(out=sc[:, 12:13], in_=rt[:, 32:64], axis=AX.X, op=OP.add)
            v.tensor_scalar(out=sc[:, 12:13], in0=sc[:, 12:13], scalar1=1e-9, scalar2=None, op0=OP.max)
            v.reciprocal(out=sc[:, 13:14], in_=sc[:, 12:13])
            v.tensor_scalar(out=L["fw_sb"][:, i * 32:(i + 1) * 32], in0=rt[:, 32:64],
                            scalar1=sc[:, 13:14], scalar2=None, op0=OP.mult)

    fwtst = fv(FL_FWT, 2048)
    for i in range(NT):
        @sec(P, "tensor")
        def _(t, i=i):
            P.w(t, "cV")
            P.w(t, "cT")
            P.inc(t.transpose(PB[0:NEXP, 128:256],
                              L["fw_sb"][:, i * 32:(i + 1) * 32], L["identf_sb"][:]), "cT", 1)

        @sec(P, "vector")
        def _(v, i=i):
            P.w(v, "cT")
            v.tensor_copy(out=fwtst[0:NEXP, i * 128:(i + 1) * 128], in_=PB[0:NEXP, 128:256])

    @sec(P, "sync")
    def _(sy):
        P.w(sy, "cV")
        P.dma(sy, "dA", L["fwT_d"].ap()[:, :], fwtst[0:NEXP, :])
    fwt_stored = P.C["dA"]

    ring_c = {"i": 0, "busy": {}}
    WSTRIDE = {"wg": 768, "wu": 768, "wd": 2048}

    def wload(name, e, first=False):
        slot = ring_c["i"] % 2
        ring_c["i"] += 1
        busy = ring_c["busy"].get(slot)

        @sec(P, "sync")
        def _(g, name=name, e=e, slot=slot, busy=busy, first=first):
            if first:
                P.w(g, "cV")
                P.w(g, "cT")
            P.w(g, "dW", P.wcount[f"ffn{e}"])
            if busy is not None:
                P.w(g, "cT", busy)
            wi = {"wg": 768, "wu": 768, "wd": 2048}[name]
            P.dma(g, "dA",
                  bv(FC_RING + slot * 12288, 12288).rearrange("p (s i) -> p s i", i=wi),
                  L["wbf"][name].ap()[e].rearrange("(s p) i -> p s i", p=128))
        return slot, P.C["dA"]

    ev = {}
    for ch in range(2):
        @sec(P, "sync")
        def _(sy, ch=ch):
            P.w(sy, "cT")
            for j in range(NT):
                P.dma(sy, "dA", bv(FC_XNT + j * 1024, 1024),
                      L["xnT_d"].ap()[ch * 8:(ch + 1) * 8, :, j * 128:(j + 1) * 128]
                          .rearrange("c p n -> p c n"), gate=(j == 0))
        xnt_done = P.C["dA"]

        for pair in range(16):
            for sub, guoff in ((0, FC_GU0), (1, FC_GU1)):
                e = 2 * pair + sub
                sg, dg = wload("wg", e, first=(ch == 0 and pair == 0 and sub == 0))
                su, du = wload("wu", e)

                @sec(P, "sync")
                def _(sy, e=e, ch=ch, sub=sub):
                    P.w(sy, "cV")
                    P.dma(sy, "dA", fv(FC_FW + sub * 2048, 1024),
                          bass.AP(L["fwT_d"], e * T + ch * 1024, [[0, 128], [1, 1024]]))
                dfw = P.C["dA"]

                for it in range(NIT):
                    @sec(P, "tensor")
                    def _(t, it=it, dg=dg, sg=sg):
                        if it == 0:
                            P.w(t, "dA", dg)
                            P.w(t, "dA", xnt_done)
                        if "pa01" in ev:
                            P.w(t, *ev["pa01"])
                        for n in range(2):
                            for k in range(NT):
                                P.inc(t.matmul(PA[:, n * 512:(n + 1) * 512],
                                               bv(FC_RING + sg * 12288 + k * 768 + it * 128, 128),
                                               bv(FC_XNT + k * 1024 + n * 512, 512),
                                               start=(k == 0), stop=(k == NT - 1)),
                                      "cT", 1 if (k == NT - 1 and n == 1) else 0)

                    @sec(P, "scalar")
                    def _(s, it=it):
                        P.w(s, "cT")
                        P.w(s, "cV")
                        s.activation(out=bv(FC_G + it * 1024, 1024), in_=PA[:, 0:1024],
                                     func=AF.Silu)
                    ev["pa01"] = ("cS", P.C["cS"])

                    @sec(P, "tensor")
                    def _(t, it=it, du=du, su=su):
                        if it == 0:
                            P.w(t, "dA", du)
                        if "pa23" in ev:
                            P.w(t, *ev["pa23"])
                        for n in range(2):
                            for k in range(NT):
                                P.inc(t.matmul(PA[:, 1024 + n * 512:1024 + (n + 1) * 512],
                                               bv(FC_RING + su * 12288 + k * 768 + it * 128, 128),
                                               bv(FC_XNT + k * 1024 + n * 512, 512),
                                               start=(k == 0), stop=(k == NT - 1)),
                                      "cT", 1 if (k == NT - 1 and n == 1) else 0)

                    @sec(P, "vector")
                    def _(v, it=it, sub=sub, guoff=guoff, dfw=dfw):
                        P.w(v, "cT")
                        P.w(v, "dA", dfw)
                        P.w(v, "cS")
                        gu = bv(guoff + it * 1024, 1024)
                        v.tensor_tensor(out=gu, in0=PA[:, 1024:2048],
                                        in1=fv(FC_FW + sub * 2048, 1024), op=OP.mult)
                        v.tensor_tensor(out=gu, in0=gu, in1=bv(FC_G + it * 1024, 1024),
                                        op=OP.mult)
                    ev["pa23"] = ("cV", P.C["cV"])
                ring_c["busy"][sg] = P.C["cT"]
                ring_c["busy"][su] = P.C["cT"]

            dslots = [wload("wd", 2 * pair)[0:2], wload("wd", 2 * pair + 1)[0:2]]

            for tt in range(8):
                for hh2 in range(2):
                    use_pb = (tt * 2 + hh2) % 2 == 0
                    ps = PB[:, 0:1024] if use_pb else PA[:, 0:1024]
                    key = "pb" if use_pb else "pa01"

                    @sec(P, "tensor")
                    def _(t, tt=tt, hh2=hh2, ps=ps, key=key, dslots=dslots):
                        if tt == 0 and hh2 == 0:
                            P.w(t, "dA", dslots[1][1])
                            P.w(t, "cV")
                        if key in ev:
                            P.w(t, *ev[key])
                        for sub, (sd, _dc) in enumerate(dslots):
                            guoff = FC_GU0 if sub == 0 else FC_GU1
                            for it in range(NIT):
                                for n in range(2):
                                    P.inc(t.matmul(
                                        ps[:, n * 512:(n + 1) * 512],
                                        bv(guoff + it * 1024 + tt * 128, 128),
                                        bv(FC_RING + sd * 12288 + it * 2048
                                           + hh2 * 1024 + n * 512, 512),
                                        start=(sub == 0 and it == 0),
                                        stop=(sub == 1 and it == NIT - 1)),
                                        "cT", 1 if (sub == 1 and it == NIT - 1 and n == 1) else 0)

                    @sec(P, "vector")
                    def _(v, tt=tt, hh2=hh2, ps=ps, pair=pair):
                        P.w(v, "cT")
                        dst = fv(FC_OUT + tt * 4096 + hh2 * 2048, 1024)
                        if pair == 0:
                            v.tensor_copy(out=dst, in_=ps)
                        else:
                            v.tensor_tensor(out=dst, in0=dst, in1=ps, op=OP.add)
                    ev[key] = ("cV", P.C["cV"])
            for sd, _dc in dslots:
                ring_c["busy"][sd] = P.C["cT"]

        for tt in range(8):
            gt = ch * 8 + tt

            @sec(P, "sync")
            def _(sy, gt=gt):
                P.w(sy, "cV")
                P.dma(sy, "dA", fv(FC_RST, 2048), L["resid_d"].ap()[gt * 128:(gt + 1) * 128, :])
            rld = P.C["dA"]

            @sec(P, "vector")
            def _(v, tt=tt, rld=rld):
                P.w(v, "dA", rld)
                dst = fv(FC_OUT + tt * 4096, 2048)
                v.tensor_tensor(out=dst, in0=dst, in1=fv(FC_RST, 2048), op=OP.add)

            @sec(P, "sync")
            def _(sy, gt=gt, tt=tt, call=call):
                P.w(sy, "cV")
                dst = L["out"] if (call == 1 and L["n_sublayers"] >= 4) else L["resid_d"].ap()[:, :]
                P.dma(sy, "dA", dst[gt * 128:(gt + 1) * 128, :],
                      fv(FC_OUT + tt * 4096, 2048))


BB_HST = 0
BB_QT = 32768
BB_KT = 65536
BB_VNO = 73728
BB_RING = 82048
BB_PT = 0
BB_CTXT = 16384
BB_STST = 82048
BB_CTXST = 86144
BB_RST = 87296


def emit_attn_b(P):
    L = P.L
    bv, fv = L["bv"], L["fv"]
    PA, PB = L["PA"], L["PB"]
    sc = L["sc_sb"]
    phase_fence(P)

    xbuf = fv(BB_QT, 2048)
    xnst = bv(BB_QT + 8192, 2048)
    for i in range(NT):
        @sec(P, "sync")
        def _(sy, i=i):
            P.w(sy, "cV")
            P.w(sy, "cT")
            P.dma(sy, "dA", xbuf, L["resid_d"].ap()[i * 128:(i + 1) * 128, :])
        ln_tile(P, xbuf, False, xnst)
        transpose16(P, lambda jj: xnst[:, jj * 128:jj * 128 + 128],
                    lambda jj, i=i: bv(BB_HST + jj * 2048 + i * 128, 128))

    ring_c = {"i": 0, "busy": {}}

    def bload(name, colrange, first=False):
        slot = ring_c["i"] % 2
        ring_c["i"] += 1
        busy = ring_c["busy"].get(slot)
        c0, cn = colrange

        @sec(P, "sync")
        def _(g, name=name, slot=slot, busy=busy, first=first, c0=c0, cn=cn):
            if first:
                P.w(g, "cV")
                P.w(g, "cT")
            P.w(g, "dW", P.wcount[name])
            if busy is not None:
                P.w(g, "cT", busy)
            P.dma(g, "dA",
                  bv(BB_RING + slot * 8192, 16 * cn).rearrange("p (s o) -> p s o", o=cn),
                  L["wbf"][name].ap()[:, c0:c0 + cn].rearrange("(s p) o -> p s o", p=128))
        return slot, P.C["dA"]

    def rsl(slot, k, cn=512):
        return bv(BB_RING + slot * 8192 + k * cn, cn)

    @sec(P, "gpsimd")
    def _(g):
        P.w(g, "cT")
        P.w(g, "cV")
        g.memset(bv(BB_VNO, 8256), 1.0)
    vno_ones = P.C["cG"]

    ev = {}
    for q in range(4):
        slot, dq = bload("bWq", (q * 512, 512), first=(q == 0))
        for oo in range(4):
            ot = q * 4 + oo
            for th in range(2):
                half = th

                @sec(P, "tensor")
                def _(t, slot=slot, dq=dq, oo=oo, th=th, half=half, q=q):
                    P.w(t, "dA", dq)
                    if q == 0 and oo == 0 and th == 0:
                        P.w(t, "cV")
                        P.w(t, "cS")
                    if ("q", half) in ev:
                        P.w(t, *ev[("q", half)])
                    pa = PA[:, half * 1024:half * 1024 + 1024]
                    for n in range(2):
                        for k in range(NT):
                            P.inc(t.matmul(pa[:, n * 512:(n + 1) * 512],
                                           rsl(slot, k)[:, oo * 128:oo * 128 + 128],
                                           bv(BB_HST + k * 2048 + th * 1024 + n * 512, 512),
                                           start=(k == 0), stop=(k == NT - 1)),
                                  "cT", 1 if (k == NT - 1 and n == 1) else 0)

                @sec(P, "vector")
                def _(v, ot=ot, th=th, half=half):
                    P.w(v, "cT")
                    v.tensor_scalar(out=bv(BB_QT + ot * 2048 + th * 1024, 1024),
                                    in0=PA[:, half * 1024:half * 1024 + 1024],
                                    scalar1=SQRT_HD, scalar2=None, op0=OP.mult)
                ev[("q", half)] = ("cV", P.C["cV"])
        ring_c["busy"][slot] = P.C["cT"]

    slot, dk = bload("bWk", (0, 512))
    for oo in range(4):
        for th in range(2):
            half = th

            @sec(P, "tensor")
            def _(t, slot=slot, dk=dk, oo=oo, th=th, half=half):
                P.w(t, "dA", dk)
                if ("q", half) in ev:
                    P.w(t, *ev[("q", half)])
                pa = PA[:, half * 1024:half * 1024 + 1024]
                for n in range(2):
                    for k in range(NT):
                        P.inc(t.matmul(pa[:, n * 512:(n + 1) * 512],
                                       rsl(slot, k)[:, oo * 128:oo * 128 + 128],
                                       bv(BB_HST + k * 2048 + th * 1024 + n * 512, 512),
                                       start=(k == 0), stop=(k == NT - 1)),
                              "cT", 1 if (k == NT - 1 and n == 1) else 0)

            @sec(P, "vector")
            def _(v, oo=oo, th=th, half=half):
                P.w(v, "cT")
                v.tensor_copy(out=bv(BB_KT + oo * 2048 + th * 1024, 1024),
                              in_=PA[:, half * 1024:half * 1024 + 1024])
            ev[("q", half)] = ("cV", P.C["cV"])
    ring_c["busy"][slot] = P.C["cT"]

    slot, dv = bload("bWv", (0, 512))
    for tt in range(NT):
        pq = tt % 2

        @sec(P, "tensor")
        def _(t, slot=slot, dv=dv, tt=tt, pq=pq):
            P.w(t, "dA", dv)
            if tt == 0:
                P.w(t, "cV")
                P.w(t, "cS")
                P.w(t, "cG", vno_ones)
            if ("v", pq) in ev:
                P.w(t, *ev[("v", pq)])
            pa = PA[:, pq * 512:pq * 512 + 512]
            for k in range(NT):
                P.inc(t.matmul(pa, bv(BB_HST + k * 2048 + tt * 128, 128),
                               rsl(slot, k), start=(k == 0), stop=(k == NT - 1)),
                      "cT", 1 if k == NT - 1 else 0)

        @sec(P, "vector")
        def _(v, tt=tt, pq=pq):
            P.w(v, "cT")
            P.w(v, "cG", vno_ones)
            for kv in range(NKV):
                v.tensor_copy(out=bv(BB_VNO + tt * 516 + kv * 129, 128),
                              in_=PA[:, pq * 512 + kv * 128:pq * 512 + kv * 128 + 128])
        ev[("v", pq)] = ("cV", P.C["cV"])
    ring_c["busy"][slot] = P.C["cT"]

    STst = [fv(BB_STST, 1024), fv(BB_STST + 2048, 1024)]
    e50 = L["e50_sb"]
    for qh in range(2):
        nkt = 8 * (qh + 1)
        for H in range(NQH):
            kv = H // (NQH // NKV)
            for kc in range(nkt):
                half = kc % 2

                @sec(P, "tensor")
                def _(t, qh=qh, H=H, kc=kc, half=half, kv=kv):
                    if H == 0 and kc == 0:
                        P.w(t, "cV")
                        P.w(t, "cS")
                    if ("st", half) in ev:
                        P.w(t, *ev[("st", half)])
                    pa = PA[:, half * 1024:half * 1024 + 1024]
                    for n in range(2):
                        P.inc(t.matmul(pa[:, n * 512:(n + 1) * 512],
                                       bv(BB_KT + kv * 2048 + kc * 128, 128),
                                       bv(BB_QT + H * 2048 + qh * 1024 + n * 512, 512),
                                       start=True, stop=True),
                              "cT", 1 if n == 1 else 0)

                dl_ = 128 * (kc - 8 * qh)

                @sec(P, "vector")
                def _(v, kc=kc, half=half, dl_=dl_):
                    P.w(v, "cT")
                    P.w(v, "cS")
                    pa = PA[:, half * 1024:half * 1024 + 1024]
                    st = STst[half]
                    lo = max(dl_, 0)
                    if lo < 1024:
                        v.tensor_scalar(out=st[:, lo:1024], in0=pa[:, lo:1024],
                                        scalar1=-50.0, scalar2=50.0, op0=OP.max, op1=OP.min)
                    if dl_ >= 0:
                        v.scalar_tensor_tensor(out=st[:, dl_:dl_ + 128],
                                               in0=st[:, dl_:dl_ + 128], scalar=50.0,
                                               in1=L["triu_sb"][:],
                                               op0=OP.add, op1=OP.mult)
                        v.tensor_scalar(out=st[:, dl_:dl_ + 128], in0=st[:, dl_:dl_ + 128],
                                        scalar1=-50.0, scalar2=None, op0=OP.add)

                @sec(P, "scalar")
                def _(s, kc=kc, half=half, dl_=dl_, H=H):
                    P.w(s, "cV")
                    if H == 0 and kc == 0:
                        P.w(s, "dA")
                        P.w(s, "cT")
                    st = STst[half]
                    pt = bv(BB_PT + kc * 1024, 1024)
                    if dl_ >= 0:
                        if dl_ + 128 < 1024:
                            s.activation(out=pt[:, dl_ + 128:1024], in_=st[:, dl_ + 128:1024],
                                         func=AF.Exp)
                        s.activation(out=pt[:, dl_:dl_ + 128], in_=st[:, dl_:dl_ + 128],
                                     func=AF.Exp)
                    else:
                        s.activation(out=pt, in_=st, func=AF.Exp)
                ev[("st", half)] = ("cS", P.C["cS"])

            for qtl in range(8):
                gq = qh * 8 + qtl
                half = qtl % 2
                psb = PB[:, half * 512:half * 512 + 129]

                @sec(P, "tensor")
                def _(t, qtl=qtl, gq=gq, half=half, psb=psb, kv=kv, nkt=nkt, qh=qh):
                    P.w(t, "cS")
                    if ("ctx", half) in ev:
                        P.w(t, *ev[("ctx", half)])
                    for kc in range(16):
                        if kc < nkt:
                            lhs = (bv(BB_PT + kc * 1024 + qtl * 128, 128) if kc <= gq
                                   else e50[:, 0:128])
                        else:
                            lhs = e50[:, 0:128]
                        P.inc(t.matmul(psb, lhs, bv(BB_VNO + kc * 516 + kv * 129, 129),
                                       start=(kc == 0), stop=(kc == 15)),
                              "cT", 1 if kc == 15 else 0)

                @sec(P, "vector")
                def _(v, qtl=qtl, half=half, psb=psb):
                    P.w(v, "cT")
                    v.reciprocal(out=sc[:, 14:15], in_=PB[:, half * 512 + 128:half * 512 + 129])
                    v.tensor_scalar(out=bv(BB_CTXST + qtl * 128, 128),
                                    in0=PB[:, half * 512:half * 512 + 128],
                                    scalar1=sc[:, 14:15], scalar2=None, op0=OP.mult)
                ev[("ctx", half)] = ("cV", P.C["cV"])

            transpose16(P, lambda jj: bv(BB_CTXST + jj * 128, 128),
                        lambda jj, H=H: bv(BB_CTXT + H * 1024 + jj * 128, 128), n=8)

        for o2q in range(4):
            slot = o2q % 2
            busy = ring_c["busy"].get(("wo", slot))

            @sec(P, "sync")
            def _(g, o2q=o2q, slot=slot, busy=busy, qh=qh):
                P.w(g, "dW", P.wcount["bWo"])
                P.w(g, "cS")
                P.w(g, "cT")
                if busy is not None:
                    P.w(g, "cT", busy)
                P.dma(g, "dA",
                      bv(BB_PT + slot * 8192, 8192).rearrange("p (s o) -> p s o", o=512),
                      L["wbf"]["bWo"].ap()[:, o2q * 512:(o2q + 1) * 512]
                          .rearrange("(s p) o -> p s o", p=128))
            dwo = P.C["dA"]

            for qtl in range(8):
                gq = qh * 8 + qtl
                half = qtl % 2

                @sec(P, "sync")
                def _(sy, gq=gq, o2q=o2q, half=half):
                    P.w(sy, "cV")
                    P.dma(sy, "dA", fv(BB_RST + half * 1024, 512),
                          L["resid_d"].ap()[gq * 128:(gq + 1) * 128, o2q * 512:(o2q + 1) * 512])
                rld = P.C["dA"]

                @sec(P, "tensor")
                def _(t, qtl=qtl, half=half, dwo=dwo, slot=slot, o2q=o2q):
                    P.w(t, "dA", dwo)
                    if o2q == 0 and qtl == 0:
                        P.w(t, "cV")
                        P.w(t, "cS")
                    if ("wo", half) in ev:
                        P.w(t, *ev[("wo", half)])
                    pa = PA[:, half * 512:half * 512 + 512]
                    for k in range(NT):
                        P.inc(t.matmul(pa, bv(BB_CTXT + k * 1024 + qtl * 128, 128),
                                       bv(BB_PT + slot * 8192 + k * 512, 512),
                                       start=(k == 0), stop=(k == NT - 1)),
                              "cT", 1 if k == NT - 1 else 0)

                @sec(P, "vector")
                def _(v, half=half, rld=rld):
                    P.w(v, "cT")
                    P.w(v, "dA", rld)
                    v.tensor_tensor(out=fv(BB_RST + half * 1024, 512),
                                    in0=PA[:, half * 512:half * 512 + 512],
                                    in1=fv(BB_RST + half * 1024, 512), op=OP.add)
                ev[("wo", half)] = ("cV", P.C["cV"])

                @sec(P, "sync")
                def _(sy, gq=gq, o2q=o2q, half=half):
                    P.w(sy, "cV")
                    P.dma(sy, "dA", L["resid_d"].ap()[gq * 128:(gq + 1) * 128,
                                                      o2q * 512:(o2q + 1) * 512],
                          fv(BB_RST + half * 1024, 512))
            ring_c["busy"][("wo", slot)] = P.C["cT"]


def emit_finish(P):
    L = P.L

    @sec(P, "sync")
    def _(sy):
        P.w(sy, "dW")
        P.w(sy, "cV")
        P.w(sy, "cS")
        P.w(sy, "cT")
        if L["n_sublayers"] < 4:
            P.dma(sy, "dA", L["out"], L["resid_d"].ap()[:, :])
        P.w(sy, "dA")


def kernel(**inputs):
    x = np.ascontiguousarray(inputs["L0_attn_a_x"])
    nc = build(4)
    consts = {
        "c_ident_bf": np.eye(128, dtype=ml_dtypes.bfloat16),
        "c_triu_f32": np.triu(np.ones((128, 128), dtype=np.float32)),
    }
    base = {k: np.ascontiguousarray(v) for k, v in inputs.items() if k != "L0_attn_a_x"}
    in_maps = []
    for c in range(8):
        m = dict(base)
        m.update(consts)
        m["L0_attn_a_x"] = x[c]
        in_maps.append(m)
    res = run_bass_kernel_spmd(nc, in_maps, core_ids=list(range(8)))
    return np.stack([r["out"] for r in res.results], axis=0)


# version 07
# speedup vs baseline: 1.3674x; 1.0263x over previous
import sys
sys.path.insert(0, "/opt/trn_rl_repo")
import math
from contextlib import ExitStack

import numpy as np
import ml_dtypes

import concourse.bass as bass
import concourse.mybir as mybir
from concourse.alu_op_type import AluOpType as OP
from concourse.bass_utils import run_bass_kernel_spmd

F32 = mybir.dt.float32
BF16 = mybir.dt.bfloat16
AF = mybir.ActivationFunctionType
AX = mybir.AxisListType

D = 2048
T = 2048
NT = 16
LCH = 128
NCH = T // LCH
SUB = 16
NSUB = LCH // SUB
HS = 64
NHEAD = 32
HD = 128
NQH = 16
NKV = 4
DKV = NKV * HD
NEXP = 32
IFF = 768
NIT = IFF // 128
TOPK = 8
LN_EPS = 1e-5
SQRT_HD = math.sqrt(128.0)
E50 = float(np.exp(np.float32(-50.0)))

ARN = 100352

SLOT_A = 0
SLOT_B = 32768
SLOT_C = 65536
TAIL = 92160


class Prog:
    def __init__(self, nc, block, sems, L):
        self.nc = nc
        self.block = block
        self.sem = sems
        self.C = {k: 0 for k in sems}
        self.dmaset = {"dW", "dA"}
        self.L = L
        self.wcount = {}

    def inc(self, instr, name, n=1):
        if name in ("cV", "cS", "cG"):
            return instr
        instr.then_inc(self.sem[name], n * (16 if name in self.dmaset else 1))
        self.C[name] += n
        return instr

    def w(self, eng, name, count=None):
        if count is None:
            count = self.C[name]
        assert count <= self.C[name], f"wait on future count {name} {count} > {self.C[name]}"
        eng.wait_ge(self.sem[name], count * (16 if name in self.dmaset else 1))

    def dma(self, eng, q, out, in_, transpose=False, gate=True, **kw):
        if gate and self.C[q] > 0:
            eng.wait_ge(self.sem[q], self.C[q] * 16)
        if transpose:
            instr = eng.dma_start_transpose(out, in_)
        else:
            instr = eng.dma_start(out=out, in_=in_, **kw)
        self.inc(instr, q)
        return self.C[q]


CHAIN_OPS = {
    "tensor_add", "tensor_mul", "tensor_sub", "tensor_max", "tensor_copy",
    "tensor_tensor", "tensor_scalar", "tensor_scalar_add", "tensor_scalar_mul",
    "tensor_scalar_max", "tensor_scalar_min", "tensor_scalar_sub",
    "tensor_single_scalar", "tensor_reduce", "reduce_max", "reduce_sum",
    "reciprocal", "memset", "memzero", "tensor_tensor_scan",
    "scalar_tensor_tensor", "select", "copy_predicated", "activation",
    "add", "mul", "copy", "sqrt", "square", "sign", "iota",
    "partition_broadcast", "partition_all_reduce", "bn_stats", "bn_aggr",
    "stream_shuffle", "affine_select",
}


class Chained:

    def __init__(self, P, eng, ctr):
        object.__setattr__(self, "_P", P)
        object.__setattr__(self, "_e", eng)
        object.__setattr__(self, "_c", ctr)

    def __getattr__(self, name):
        attr = getattr(self._e, name)
        if name not in CHAIN_OPS:
            return attr
        P, eng, ctr = self._P, self._e, self._c

        def wrap(*a, **kw):
            if P.C[ctr] > 0:
                eng.wait_ge(P.sem[ctr], P.C[ctr])
            instr = attr(*a, **kw)
            instr.then_inc(P.sem[ctr], 1)
            P.C[ctr] += 1
            return instr
        return wrap


_CHAIN_CTR = {"vector": "cV", "scalar": "cS", "gpsimd": "cG"}


def phase_fence(P):
    for eng in ("gpsimd", "vector", "scalar", "tensor", "sync"):
        @sec(P, eng)
        def _(e):
            for s_ in ("dA", "dW", "cT", "cV", "cS", "cG"):
                P.w(e, s_)


def sec(P, engname):
    deco = getattr(P.block, engname)
    ctr = _CHAIN_CTR.get(engname)
    if ctr is None:
        return deco

    def outer(fn):
        def inner(eng):
            return fn(Chained(P, eng, ctr))
        return deco(inner)
    return outer


def ln_tile(P, xt, affine, xn_out):
    L = P.L
    sc = L["sc_sb"]
    uu = L["ln_u"]
    npass = 2 if affine else 1
    src = xt
    for p in range(npass):
        @sec(P, "vector")
        def _(v, src=src):
            P.w(v, "dA")
            P.w(v, "cS")
            v.tensor_tensor(out=L["ln_sq"], in0=src, in1=src, op=OP.mult)
            v.tensor_reduce(out=sc[:, 0:1], in_=src, axis=AX.X, op=OP.add)
            v.tensor_scalar(out=sc[:, 0:1], in0=sc[:, 0:1], scalar1=1.0 / D, scalar2=None, op0=OP.mult)
            v.tensor_reduce(out=sc[:, 1:2], in_=L["ln_sq"], axis=AX.X, op=OP.add)
            v.tensor_scalar(out=sc[:, 1:2], in0=sc[:, 1:2], scalar1=1.0 / D, scalar2=None, op0=OP.mult)
            v.tensor_tensor(out=sc[:, 2:3], in0=sc[:, 0:1], in1=sc[:, 0:1], op=OP.mult)
            v.tensor_tensor(out=sc[:, 1:2], in0=sc[:, 1:2], in1=sc[:, 2:3], op=OP.subtract)
            v.tensor_scalar(out=sc[:, 1:2], in0=sc[:, 1:2], scalar1=LN_EPS, scalar2=None, op0=OP.add)

        @sec(P, "scalar")
        def _(s):
            P.w(s, "cV")
            s.activation(out=sc[:, 3:4], in_=sc[:, 1:2], func=AF.Ln)

        last = (p == npass - 1)

        @sec(P, "vector")
        def _(v, p=p):
            P.w(v, "cS")
            v.tensor_scalar(out=sc[:, 3:4], in0=sc[:, 3:4], scalar1=-0.5, scalar2=None, op0=OP.mult)

        @sec(P, "scalar")
        def _(s):
            P.w(s, "cV")
            s.activation(out=sc[:, 4:5], in_=sc[:, 3:4], func=AF.Exp)

        @sec(P, "vector")
        def _(v, src=src, last=last, p=p):
            P.w(v, "cS")
            if last:
                P.w(v, "cT")
            if not affine:
                P.inc(v.tensor_scalar(out=xn_out, in0=src, scalar1=sc[:, 0:1], scalar2=sc[:, 4:5],
                                      op0=OP.subtract, op1=OP.mult), "cV")
            elif p == 0:
                P.inc(v.tensor_scalar(out=uu, in0=src, scalar1=sc[:, 0:1], scalar2=sc[:, 4:5],
                                      op0=OP.subtract, op1=OP.mult), "cV")
            else:
                v.tensor_scalar(out=uu, in0=uu, scalar1=sc[:, 0:1], scalar2=sc[:, 4:5],
                                op0=OP.subtract, op1=OP.mult)
                v.tensor_tensor(out=uu, in0=uu, in1=L["gB"], op=OP.mult)
                P.inc(v.tensor_tensor(out=xn_out, in0=uu, in1=L["bB"], op=OP.add), "cV")
        src = uu


def transpose16(P, src_fn, dst_fn, n=16):
    L = P.L
    nh = (n + 7) // 8
    for h in range(nh):
        cnt = min(8, n - h * 8)
        pt = L["PT0"] if h % 2 == 0 else L["PT1"]

        @sec(P, "tensor")
        def _(t, h=h, cnt=cnt, pt=pt):
            P.w(t, "cV")
            for j in range(cnt):
                P.inc(t.transpose(pt[:, j * 128:j * 128 + 128], src_fn(h * 8 + j),
                                  L["identb_sb"][:]), "cT", 1 if j == cnt - 1 else 0)

        @sec(P, "vector")
        def _(v, h=h, cnt=cnt, pt=pt):
            P.w(v, "cT")
            for j in range(cnt):
                P.inc(v.tensor_copy(out=dst_fn(h * 8 + j), in_=pt[:, j * 128:j * 128 + 128]),
                      "cV", 1 if j == cnt - 1 else 0)




def build(n_sublayers=4, stage=None, rec_lvl=7, rec_nch=NCH, rec_nsub=NSUB, rec_nhf=2):
    nc = bass.Bass("TRN2", target_bir_lowering=False)

    def din(name, shape):
        return nc.dram_tensor(name, list(shape), F32, kind="ExternalInput").ap()

    L = {"n_sublayers": n_sublayers}
    for nm, shp in [
        ("L0_attn_a_x", (T, D)), ("L0_attn_a_ln_scale", (D,)), ("L0_attn_a_ln_bias", (D,)),
        ("L0_attn_a_Wx", (D, D)), ("L0_attn_a_Ww", (D, D)), ("L0_attn_a_bw", (D,)),
        ("L0_attn_a_Wk", (D, D)), ("L0_attn_a_Wv", (D, D)), ("L0_attn_a_Wr", (D, D)),
        ("L0_attn_a_Wo", (D, D)), ("L0_attn_b_Wq", (D, D)), ("L0_attn_b_Wk", (D, DKV)),
        ("L0_attn_b_Wv", (D, DKV)), ("L0_attn_b_Wo", (D, D)), ("L0_ffn_gate_w", (NEXP, D)),
        ("L0_ffn_w_gate", (NEXP, D, IFF)), ("L0_ffn_w_up", (NEXP, D, IFF)),
        ("L0_ffn_w_down", (NEXP, IFF, D)),
    ]:
        L[nm] = din(nm, shp)
    L["x_in"], L["ln_scale"], L["ln_bias"] = L["L0_attn_a_x"], L["L0_attn_a_ln_scale"], L["L0_attn_a_ln_bias"]
    L["aWx"], L["aWw"], L["a_bw"] = L["L0_attn_a_Wx"], L["L0_attn_a_Ww"], L["L0_attn_a_bw"]
    L["aWk"], L["aWv"], L["aWr"], L["aWo"] = L["L0_attn_a_Wk"], L["L0_attn_a_Wv"], L["L0_attn_a_Wr"], L["L0_attn_a_Wo"]
    L["bWq"], L["bWk"], L["bWv"], L["bWo"] = L["L0_attn_b_Wq"], L["L0_attn_b_Wk"], L["L0_attn_b_Wv"], L["L0_attn_b_Wo"]
    L["gate_w"], L["w_gate"], L["w_up"], L["w_down"] = L["L0_ffn_gate_w"], L["L0_ffn_w_gate"], L["L0_ffn_w_up"], L["L0_ffn_w_down"]

    L["ident_bf"] = nc.dram_tensor("c_ident_bf", [128, 128], BF16, kind="ExternalInput").ap()
    L["triu_f32"] = nc.dram_tensor("c_triu_f32", [128, 128], F32, kind="ExternalInput").ap()
    L["out"] = nc.dram_tensor("out", [T, D], F32, kind="ExternalOutput").ap()

    def scr(name, shape, dt):
        return nc.dram_tensor(name, list(shape), dt)

    L["resid_d"] = scr("resid_d", (T, D), F32)
    wbf = {}
    for nm in ["aWw", "aWk", "aWv", "aWr", "aWo", "bWq", "bWo"]:
        wbf[nm] = scr("wbf_" + nm, (D, D), BF16)
    wbf["bWk"] = scr("wbf_bWk", (D, DKV), BF16)
    wbf["bWv"] = scr("wbf_bWv", (D, DKV), BF16)
    wbf["wg"] = scr("wbf_wg", (NEXP, D, IFF), BF16)
    wbf["wu"] = scr("wbf_wu", (NEXP, D, IFF), BF16)
    wbf["wd"] = scr("wbf_wd", (NEXP, IFF, D), BF16)
    L["wbf"] = wbf
    L["zln_d"] = scr("zln_d", (NCH, 128, D), F32)
    L["kT_d"] = scr("kT_d", (NCH, 128, D), BF16)
    L["rT_d"] = scr("rT_d", (NCH, 128, D), BF16)
    L["vN_d"] = scr("vN_d", (NCH, 128, D), BF16)
    L["xnT_d"] = scr("xnT_d", (NT, 128, D), BF16)
    L["fwT_d"] = scr("fwT_d", (NEXP, T), F32)

    st = ExitStack()
    with st:
        e = st.enter_context
        AR = e(nc.sbuf_tensor("AR", [128, ARN], BF16))
        L["AR"] = AR
        L["state_sb"] = e(nc.sbuf_tensor("state_sb", [128, NT * HS], F32))
        L["state_bf"] = e(nc.sbuf_tensor("state_bf", [128, NT * HS], BF16))
        L["identb_sb"] = e(nc.sbuf_tensor("identb_sb", [128, 128], BF16))
        L["triu_sb"] = e(nc.sbuf_tensor("triu_sb", [128, 128], F32))
        L["e50_sb"] = e(nc.sbuf_tensor("e50_sb", [128, 128], BF16))
        L["gwT_sb"] = e(nc.sbuf_tensor("gwT_sb", [128, NT * NEXP], BF16))
        L["fw_sb"] = e(nc.sbuf_tensor("fw_sb", [128, NT * NEXP], F32))
        L["sc_sb"] = e(nc.sbuf_tensor("sc_sb", [128, 64], F32))
        L["bw_sb"] = e(nc.sbuf_tensor("bw_sb", [128, NT], F32))
        L["eL_sb"] = e(nc.sbuf_tensor("eL_sb", [128, NT], F32))
        L["rt_sb"] = e(nc.sbuf_tensor("rt_sb", [128, 96], F32))
        L["identf_sb"] = e(nc.sbuf_tensor("identf_sb", [128, 128], F32))
        L["PA"] = e(nc.psum_tensor("PA", [128, 2048], F32))
        L["PB"] = e(nc.psum_tensor("PB", [128, 1024], F32))
        L["PT0"] = e(nc.psum_tensor("PT0", [128, 1024], BF16))
        L["PT1"] = e(nc.psum_tensor("PT1", [128, 1024], BF16))
        sems = {}
        for s_ in ["dW", "dA", "cT", "cV", "cS", "cG"]:
            sems[s_] = e(nc.semaphore(s_))
        block = e(nc.Block())

        P = Prog(nc, block, sems, L)
        ARf = AR.ap().bitcast(F32)

        def bv(off, n):
            assert off + n <= ARN
            return AR[:, off:off + n]

        def fv(off, n):
            assert off % 2 == 0 and off + 2 * n <= ARN
            return ARf[:, off // 2: off // 2 + n]

        L["bv"] = bv
        L["fv"] = fv
        L["ln_sq"] = fv(TAIL, 2048)
        L["ln_u"] = fv(TAIL + 4096, 2048)

        L["stage"] = stage
        L["rec_lvl"] = rec_lvl
        L["rec_nch"] = rec_nch
        L["rec_nsub"] = rec_nsub
        L["rec_nhf"] = rec_nhf
        emit_consts_and_w0(P)
        if stage != "w0":
            emit_attn_a(P)
        if n_sublayers >= 2:
            emit_ffn(P, 0)
        if n_sublayers >= 3:
            emit_attn_b(P)
        if n_sublayers >= 4:
            emit_ffn(P, 1)
        emit_finish(P)

    return nc


def emit_consts_and_w0(P):
    L = P.L
    wbf = L["wbf"]

    @sec(P, "sync")
    def _(sy):
        P.dma(sy, "dA", L["identb_sb"][:], L["ident_bf"])
        P.dma(sy, "dA", L["triu_sb"][:], L["triu_f32"])
        P.dma(sy, "dA", L["bw_sb"][:],
              bass.AP(L["a_bw"].tensor, 0, [[1, 128], [128, NT]]),
              allow_slow_non_contiguous=True)

    @sec(P, "gpsimd")
    def _(g):
        g.memset(L["e50_sb"][:], E50)
        g.memset(L["state_sb"][:], 0.0)
        P.inc(g.memset(L["state_bf"][:], 0.0), "cG")
        for nm in ["aWw", "aWk", "aWv", "aWr", "aWo"]:
            P.wcount[nm] = P.dma(g, "dW", wbf[nm].ap(), L[nm])
        for ei in range(NEXP):
            P.dma(g, "dW", wbf["wg"].ap()[ei], L["w_gate"][ei])
            P.dma(g, "dW", wbf["wu"].ap()[ei], L["w_up"][ei])
            P.wcount[f"ffn{ei}"] = P.dma(g, "dW", wbf["wd"].ap()[ei], L["w_down"][ei])
        for nm in ["bWq", "bWk", "bWv", "bWo"]:
            P.wcount[nm] = P.dma(g, "dW", wbf[nm].ap(), L[nm])




def wt_quarter_load(P, slot, wdram, q, pre_wait=None):
    L = P.L
    bv = L["bv"]

    @sec(P, "sync")
    def _(sy):
        if pre_wait:
            for args in pre_wait:
                P.w(sy, *args)
        for j in range(NT):
            P.dma(sy, "dA", bv(SLOT_C + slot * 8192 + j * 512, 512),
                  wdram.ap()[q * 512:(q + 1) * 512, j * 128:j * 128 + 128],
                  transpose=True, gate=(j == 0))
    return P.C["dA"]


def emit_attn_a(P):
    L = P.L
    bv, fv = L["bv"], L["fv"]
    PA, PB = L["PA"], L["PB"]

    def wq(slot, j):
        return bv(SLOT_C + slot * 8192 + j * 512, 512)

    ring_n = {"i": 0}

    def next_slot(wname=None):
        pw = []
        if wname is not None:
            pw.append(("dW", P.wcount[wname]))
        if ring_n["i"] >= 3:
            pw.append(("cT", P.C["cT"]))
        slot = ring_n["i"] % 3
        ring_n["i"] += 1
        return slot, pw

    @sec(P, "gpsimd")
    def _(g):
        for j in range(NT):
            P.dma(g, "dW", bv(SLOT_A + j * 2048, 2048),
                  L["aWx"][j * 128:(j + 1) * 128, :], gate=(j == 0))
    wx_done = P.C["dW"]

    ev_cnt = {}
    for q in range(4):
        slot, pw = next_slot("aWw" if q == 0 else None)
        dbc = wt_quarter_load(P, slot, L["wbf"]["aWw"], q, pre_wait=pw)
        for dt in range(NT):
            pq = dt % 2

            @sec(P, "tensor")
            def _(t, q=q, dt=dt, pq=pq, slot=slot, dbc=dbc):
                if q == 0 and dt == 0:
                    P.w(t, "cV")
                    P.w(t, "dW", wx_done)
                if dt == 0:
                    P.w(t, "dA", dbc)
                if pq in ev_cnt:
                    P.w(t, *ev_cnt[pq])
                pa = PA[:, pq * 512:pq * 512 + 512]
                for k in range(NT):
                    P.inc(t.matmul(pa, bv(SLOT_A + k * 2048 + dt * 128, 128),
                                   wq(slot, k), start=(k == 0), stop=(k == NT - 1)),
                          "cT", 1 if k == NT - 1 else 0)

            @sec(P, "vector")
            def _(v, q=q, dt=dt, pq=pq):
                P.w(v, "cT")
                P.inc(v.tensor_copy(out=bv(SLOT_B + dt * 2048 + q * 512, 512),
                                    in_=PA[:, pq * 512:pq * 512 + 512]), "cV")
            ev_cnt[pq] = ("cV", P.C["cV"])

    if L.get("stage") == "wz":
        return
    L["gB"] = fv(SLOT_C, 2048)
    L["bB"] = fv(SLOT_C + 4096, 2048)
    xbuf = fv(SLOT_C + 8192, 2048)
    xnst = bv(SLOT_C + 16384, 2048)

    @sec(P, "sync")
    def _(sy):
        P.w(sy, "cV")
        P.dma(sy, "dA", L["gB"], bass.AP(L["ln_scale"].tensor, 0, [[0, 128], [1, 2048]]))
        P.dma(sy, "dA", L["bB"], bass.AP(L["ln_bias"].tensor, 0, [[0, 128], [1, 2048]]))

    for i in range(NT):
        @sec(P, "sync")
        def _(sy, i=i):
            P.w(sy, "cV")
            P.dma(sy, "dA", xbuf, L["x_in"][i * 128:(i + 1) * 128, :])
        ln_tile(P, xbuf, True, xnst)
        transpose16(P, lambda jj: xnst[:, jj * 128:jj * 128 + 128],
                    lambda jj, i=i: bv(SLOT_A + jj * 2048 + i * 128, 128))

    if L.get("stage") == "ln":
        return
    zlB = [fv(TAIL, 1024), fv(TAIL + 4096, 1024)]
    zrB = [fv(TAIL + 2048, 1024), fv(TAIL + 6144, 1024)]
    for ot in range(NT):
        for th in range(2):
            half = (2 * ot + th) % 2

            @sec(P, "tensor")
            def _(t, ot=ot, th=th, half=half):
                if ot == 0 and th == 0:
                    P.w(t, "cV")
                if half in ev_cnt:
                    P.w(t, *ev_cnt[half])
                pa = PA[:, half * 1024:half * 1024 + 1024]
                for n in range(2):
                    for k in range(NT):
                        P.inc(t.matmul(pa[:, n * 512:(n + 1) * 512],
                                       bv(SLOT_B + k * 2048 + ot * 128, 128),
                                       bv(SLOT_A + k * 2048 + th * 1024 + n * 512, 512),
                                       start=(k == 0), stop=(k == NT - 1)),
                              "cT", 1 if (k == NT - 1 and n == 1) else 0)

            zl = zlB[half]
            zr = zrB[half]

            @sec(P, "scalar")
            def _(s, ot=ot, half=half, zl=zl):
                P.w(s, "cT")
                P.w(s, "cV")
                pa = PA[:, half * 1024:half * 1024 + 1024]
                s.activation(out=zl, in_=pa, func=AF.Abs, bias=L["bw_sb"][:, ot:ot + 1])
                s.activation(out=zl, in_=zl, func=AF.Exp, scale=-1.0)
                P.inc(s.activation(out=zl, in_=zl, func=AF.Ln, bias=1.0), "cS")

            @sec(P, "vector")
            def _(v, ot=ot, half=half, zl=zl, zr=zr):
                P.w(v, "cS")
                P.w(v, "dA")
                pa = PA[:, half * 1024:half * 1024 + 1024]
                v.tensor_scalar(out=zr, in0=pa, scalar1=L["bw_sb"][:, ot:ot + 1],
                                scalar2=0.0, op0=OP.add, op1=OP.max)
                P.inc(v.scalar_tensor_tensor(out=zr, in0=zr, scalar=-1.0, in1=zl,
                                             op0=OP.mult, op1=OP.subtract), "cV")
            ev_cnt[half] = ("cV", P.C["cV"])

            @sec(P, "sync")
            def _(sy, ot=ot, th=th, zr=zr):
                P.w(sy, "cV")
                P.dma(sy, "dA",
                      L["zln_d"].ap()[th * 8:(th + 1) * 8, :, ot * 128:ot * 128 + 128]
                          .rearrange("c p n -> p c n"),
                      zr.rearrange("p (c n) -> p c n", n=128))

    if L.get("stage") == "z":
        return
    kst = [bv(SLOT_B, 2048), bv(SLOT_B + 2048, 2048)]
    vst = [bv(SLOT_B + 4096, 512), bv(SLOT_B + 4608, 512)]

    for wname, kind in [("aWk", "k"), ("aWv", "v"), ("aWr", "r")]:
        for q in range(4):
            slot, pw = next_slot(wname if q == 0 else None)
            dbc = wt_quarter_load(P, slot, L["wbf"][wname], q, pre_wait=pw)

            if kind in ("k", "r"):
                for oo in range(4):
                    ot = q * 4 + oo
                    stg = kst[oo % 2]
                    for th in range(2):
                        half = th

                        @sec(P, "tensor")
                        def _(t, slot=slot, dbc=dbc, oo=oo, th=th, half=half, q=q):
                            P.w(t, "dA", dbc)
                            if q == 0 and oo == 0 and th == 0:
                                P.w(t, "cV")
                                P.w(t, "cS")
                            if half in ev_cnt:
                                P.w(t, *ev_cnt[half])
                            pa = PA[:, half * 1024:half * 1024 + 1024]
                            for n in range(2):
                                for k in range(NT):
                                    P.inc(t.matmul(pa[:, n * 512:(n + 1) * 512],
                                                   wq(slot, k)[:, oo * 128:oo * 128 + 128],
                                                   bv(SLOT_A + k * 2048 + th * 1024 + n * 512, 512),
                                                   start=(k == 0), stop=(k == NT - 1)),
                                          "cT", 1 if (k == NT - 1 and n == 1) else 0)

                        if kind == "k":
                            @sec(P, "vector")
                            def _(v, th=th, half=half, stg=stg):
                                P.w(v, "cT")
                                P.w(v, "dA")
                                P.inc(v.tensor_copy(out=stg[:, th * 1024:(th + 1) * 1024],
                                                    in_=PA[:, half * 1024:half * 1024 + 1024]), "cV")
                            ev_cnt[half] = ("cV", P.C["cV"])
                        else:
                            @sec(P, "scalar")
                            def _(s, th=th, half=half, stg=stg):
                                P.w(s, "cT")
                                P.w(s, "dA")
                                P.inc(s.activation(out=stg[:, th * 1024:(th + 1) * 1024],
                                                   in_=PA[:, half * 1024:half * 1024 + 1024],
                                                   func=AF.Sigmoid), "cS")
                            ev_cnt[half] = ("cS", P.C["cS"])

                    dstd = L["kT_d"] if kind == "k" else L["rT_d"]

                    @sec(P, "sync")
                    def _(sy, ot=ot, dstd=dstd, kind=kind, stg=stg):
                        P.w(sy, "cV" if kind == "k" else "cS")
                        P.dma(sy, "dA",
                              dstd.ap()[:, :, ot * 128:ot * 128 + 128]
                                  .rearrange("c p n -> p c n"),
                              stg.rearrange("p (c n) -> p c n", n=128))
            else:
                for tt in range(NT):
                    pq = tt % 2
                    stg = vst[tt % 2]

                    @sec(P, "tensor")
                    def _(t, slot=slot, dbc=dbc, tt=tt, pq=pq, q=q):
                        P.w(t, "dA", dbc)
                        if q == 0 and tt == 0:
                            P.w(t, "cV")
                            P.w(t, "cS")
                        key = ("vq", pq)
                        if key in ev_cnt:
                            P.w(t, *ev_cnt[key])
                        pa = PA[:, 1024 + pq * 512:1024 + pq * 512 + 512]
                        for k in range(NT):
                            P.inc(t.matmul(pa, bv(SLOT_A + k * 2048 + tt * 128, 128),
                                           wq(slot, k), start=(k == 0), stop=(k == NT - 1)),
                                  "cT", 1 if k == NT - 1 else 0)

                    @sec(P, "vector")
                    def _(v, tt=tt, pq=pq, stg=stg):
                        P.w(v, "cT")
                        P.w(v, "dA")
                        P.inc(v.tensor_copy(out=stg, in_=PA[:, 1024 + pq * 512:1024 + pq * 512 + 512]), "cV")
                    ev_cnt[("vq", pq)] = ("cV", P.C["cV"])

                    @sec(P, "sync")
                    def _(sy, tt=tt, q=q, stg=stg):
                        P.w(sy, "cV")
                        P.dma(sy, "dA", L["vN_d"].ap()[tt, :, q * 512:q * 512 + 512], stg)

    if L.get("stage") == "kvr":
        return
    emit_attn_a_rec(P)
    if L.get("stage") == "rec":
        return
    emit_attn_a_out(P)


RB_LNA = SLOT_B
RB_CT = SLOT_B + 4096
RB_KT = SLOT_B + 8192
RB_RT = SLOT_B + 10240
RB_VN = SLOT_B + 12288
RB_RT2 = SLOT_B + 14336
RB_KHT = SLOT_B + 16384
RB_KHN = SLOT_B + 18432
RB_MARG = SLOT_B + 20480
RB_EB = SLOT_B + 24576
RB_KB = SLOT_B + 26624
RB_RH = SLOT_B + 28672
RB_ASB = SLOT_B + 28928
RB_ZERO = SLOT_B + 30976


def emit_attn_a_rec(P):
    L = P.L
    lvl = L.get("rec_lvl", 7)
    nch = L.get("rec_nch", NCH)
    nsub = L.get("rec_nsub", NSUB)
    nhf = L.get("rec_nhf", 2)
    bv, fv = L["bv"], L["fv"]
    PA, PB = L["PA"], L["PB"]
    lnaB = [fv(RB_LNA, 2048), fv(SLOT_C, 2048)]
    cT = fv(RB_CT, 2048)
    cT3 = cT.rearrange("p (j s) -> p j s", s=128)
    kTB = [bv(RB_KT, 2048), bv(SLOT_C + 4096, 2048)]
    rTB = [bv(RB_RT, 2048), bv(SLOT_C + 6144, 2048)]
    vNB = [bv(RB_VN, 2048), bv(SLOT_C + 8192, 2048)]
    rt2 = bv(RB_RT2, 2048)
    khT = bv(RB_KHT, 2048)
    khN = bv(RB_KHN, 2048)
    marg = fv(RB_MARG, 2048)
    marg3 = marg.rearrange("p (j s) -> p j s", s=128)
    Eb = bv(RB_EB, 2048)
    kb = bv(RB_KB, 2048)
    rh = bv(RB_RH, 256)
    rhp = bv(RB_ZERO + 768, 512)
    Asb = bv(RB_ASB, 2048)
    zero = fv(RB_ZERO, 128)
    cL_b = cT3[:, :, 127:128].broadcast_to((128, NT, 128))

    @sec(P, "vector")
    def _(v):
        P.w(v, "cS")
        P.w(v, "dA")
        v.memset(rhp, 0.0)
        P.inc(v.memset(zero, 0.0), "cV")

    chunk_ld = {}

    def load_chunk(cc):
        @sec(P, "sync")
        def _(sy, cc=cc):
            P.w(sy, "cT")
            P.dma(sy, "dA", lnaB[cc % 2], L["zln_d"].ap()[cc])
            P.dma(sy, "dA", kTB[cc % 2], L["kT_d"].ap()[cc], gate=False)
            P.dma(sy, "dA", rTB[cc % 2], L["rT_d"].ap()[cc], gate=False)
            P.dma(sy, "dA", vNB[cc % 2], L["vN_d"].ap()[cc], gate=False)
        chunk_ld[cc] = P.C["dA"]

    load_chunk(0)
    for c in range(nch):
        lna = lnaB[c % 2]
        kT = kTB[c % 2]
        rT = rTB[c % 2]
        rT3 = rT.rearrange("p (j s) -> p j s", s=128)
        vN = vNB[c % 2]
        if c + 1 < nch:
            load_chunk(c + 1)

        @sec(P, "vector")
        def _(v, c=c):
            P.w(v, "dA", chunk_ld[c])
            for j in range(NT):
                P.inc(v.tensor_tensor_scan(out=cT[:, j * 128:(j + 1) * 128],
                                           data0=lna[:, j * 128:(j + 1) * 128],
                                           data1=zero, initial=0.0,
                                           op0=OP.add, op1=OP.add),
                      "cV", 1 if j == NT - 1 else 0)

        if lvl < 2:
            continue
        @sec(P, "scalar")
        def _(s):
            P.w(s, "cV")
            P.inc(s.activation(out=Eb, in_=cT, func=AF.Exp), "cS")
            P.inc(s.activation(out=L["eL_sb"][:].rearrange("p (j o) -> p j o", o=1),
                               in_=cT3[:, :, 127:128], func=AF.Exp), "cS")

        @sec(P, "vector")
        def _(v):
            P.w(v, "cS")
            v.tensor_tensor(out=rt2, in0=rT, in1=Eb, op=OP.mult)
            P.inc(v.tensor_tensor(out=marg3, in0=cL_b, in1=cT3, op=OP.subtract), "cV")

        @sec(P, "scalar")
        def _(s):
            P.w(s, "cV")
            P.inc(s.activation(out=Eb, in_=marg, func=AF.Exp), "cS")

        @sec(P, "vector")
        def _(v):
            P.w(v, "cS")
            P.inc(v.tensor_tensor(out=khT, in0=kT, in1=Eb, op=OP.mult), "cV")

        transpose16(P, lambda jj: khT[:, jj * 128:jj * 128 + 128],
                    lambda jj: khN[:, jj * 128:jj * 128 + 128])
        if lvl < 3:
            continue

        for hf in range(nhf):
            j0 = hf * 8
            for i in range(nsub):
                @sec(P, "vector")
                def _(v, i=i, j0=j0):
                    P.w(v, "cS")
                    mh = marg[:, j0 * 128:(j0 + 8) * 128]
                    mh3 = marg3[:, j0:j0 + 8, :]
                    ch = cT[:, j0 * 128:(j0 + 8) * 128]
                    ch3 = cT3[:, j0:j0 + 8, :]
                    if i == 0:
                        P.inc(v.tensor_scalar(out=mh, in0=ch, scalar1=-80.0, scalar2=None,
                                              op0=OP.max), "cV")
                    else:
                        bview = cT3[:, j0:j0 + 8, 16 * i - 1:16 * i].broadcast_to((128, 8, 128))
                        v.tensor_tensor(out=mh3, in0=ch3, in1=bview, op=OP.subtract)
                        P.inc(v.tensor_scalar(out=mh, in0=mh, scalar1=-80.0, scalar2=None,
                                              op0=OP.max), "cV")

                @sec(P, "scalar")
                def _(s, i=i, j0=j0):
                    P.w(s, "cV")
                    P.inc(s.activation(out=Eb[:, j0 * 128:(j0 + 8) * 128],
                                       in_=marg[:, j0 * 128:(j0 + 8) * 128],
                                       func=AF.Exp, scale=-1.0), "cS")
                    P.inc(s.activation(
                        out=rh[:, 0:128].rearrange("p (j s) -> p j s", s=16),
                        in_=marg3[:, j0:j0 + 8, 16 * i:16 * i + 16],
                        func=AF.Exp), "cS")

                @sec(P, "vector")
                def _(v, i=i, j0=j0):
                    P.w(v, "cS")
                    P.w(v, "cT")
                    v.tensor_tensor(out=kb[:, j0 * 128:(j0 + 8) * 128],
                                    in0=kT[:, j0 * 128:(j0 + 8) * 128],
                                    in1=Eb[:, j0 * 128:(j0 + 8) * 128], op=OP.mult)
                    v.tensor_tensor(
                        out=rhp[0:64, 0:128].rearrange("p (j s) -> p j s", s=16),
                        in0=rT3[0:64, j0:j0 + 8, 16 * i:16 * i + 16],
                        in1=rh[0:64, 0:128].rearrange("p (j s) -> p j s", s=16),
                        op=OP.mult)
                    P.inc(v.tensor_tensor(
                        out=rhp[64:128, 256:384].rearrange("p (j s) -> p j s", s=16),
                        in0=rT3[64:128, j0:j0 + 8, 16 * i:16 * i + 16],
                        in1=rh[64:128, 0:128].rearrange("p (j s) -> p j s", s=16),
                        op=OP.mult), "cV")

                if lvl < 4:
                    continue

                @sec(P, "tensor")
                def _(t, i=i, j0=j0):
                    P.w(t, "cV")
                    for jj in range(8):
                        for hp in range(2):
                            hh = 2 * jj + hp
                            P.w(t, "cT")
                            P.inc(t.matmul(
                                PA[:, hh * 128 + 16 * i: hh * 128 + 16 * (i + 1)],
                                kb[:, (j0 + jj) * 128:(j0 + jj) * 128 + 128],
                                rhp[:, hp * 256 + jj * 16: hp * 256 + jj * 16 + 16],
                                start=True, stop=True), "cT", 1)

            if lvl < 5:
                continue
            @sec(P, "vector")
            def _(v, j0=j0):
                P.w(v, "cT")
                for hh in range(16):
                    P.inc(v.tensor_tensor(out=Asb[:, hh * 128:(hh + 1) * 128],
                                          in0=PA[:, hh * 128:(hh + 1) * 128],
                                          in1=L["triu_sb"][:], op=OP.mult),
                          "cV", 1 if hh == 15 else 0)

            if lvl < 6:
                continue
            @sec(P, "tensor")
            def _(t, j0=j0, hf=hf, c=c):
                P.w(t, "cV")
                if c == 0 and hf == 0:
                    P.w(t, "cG")
                for pr in range(8):
                    jj = j0 + pr
                    for hp in range(2):
                        h = 2 * jj + hp
                        hh = 2 * pr + hp
                        ps = PB[hp * 64:hp * 64 + 64, pr * 128:pr * 128 + 128]
                        P.w(t, "cT")
                        P.inc(t.matmul(ps, vN[:, h * 64:h * 64 + 64],
                                       Asb[:, hh * 128:(hh + 1) * 128],
                                       start=True, stop=False), "cT", 1)
                        P.w(t, "cT")
                        P.inc(t.matmul(ps,
                                       L["state_bf"][hp * 64:hp * 64 + 64, jj * 64:jj * 64 + 64],
                                       rt2[hp * 64:hp * 64 + 64, jj * 128:jj * 128 + 128],
                                       start=False, stop=True), "cT", 1)

            @sec(P, "vector")
            def _(v, j0=j0, c=c):
                P.w(v, "cT")
                for pr in range(8):
                    jj = j0 + pr
                    P.inc(v.tensor_copy(out=bv(SLOT_A + jj * 2048 + c * 128, 128),
                                        in_=PB[:, pr * 128:pr * 128 + 128]),
                          "cV", 1 if pr == 7 else 0)

        if lvl < 7:
            continue
        @sec(P, "tensor")
        def _(t, c=c):
            P.w(t, "cV")
            for pr in range(8):
                for hp in range(2):
                    h = 2 * pr + hp
                    P.w(t, "cT")
                    P.inc(t.matmul(PB[hp * 64:hp * 64 + 64, pr * 64:pr * 64 + 64],
                                   khN[:, h * 64:h * 64 + 64],
                                   vN[:, h * 64:h * 64 + 64],
                                   start=True, stop=True), "cT", 1)
            for pr in range(8, 16):
                for hp in range(2):
                    h = 2 * pr + hp
                    P.w(t, "cT")
                    P.inc(t.matmul(PA[hp * 64:hp * 64 + 64, (pr - 8) * 64:(pr - 8) * 64 + 64],
                                   khN[:, h * 64:h * 64 + 64],
                                   vN[:, h * 64:h * 64 + 64],
                                   start=True, stop=True), "cT", 1)

        @sec(P, "vector")
        def _(v, c=c):
            P.w(v, "cT")
            for j in range(NT):
                v.tensor_scalar(out=L["state_sb"][:, j * 64:(j + 1) * 64],
                                in0=L["state_sb"][:, j * 64:(j + 1) * 64],
                                scalar1=L["eL_sb"][:, j:j + 1], scalar2=None, op0=OP.mult)
                src = PB[:, j * 64:(j + 1) * 64] if j < 8 else PA[:, (j - 8) * 64:(j - 8) * 64 + 64]
                v.tensor_tensor(out=L["state_sb"][:, j * 64:(j + 1) * 64],
                                in0=L["state_sb"][:, j * 64:(j + 1) * 64],
                                in1=src, op=OP.add)
                P.inc(v.tensor_copy(out=L["state_bf"][:, j * 64:(j + 1) * 64],
                                    in_=L["state_sb"][:, j * 64:(j + 1) * 64]),
                      "cV", 1 if j == NT - 1 else 0)


def emit_attn_a_out(P):
    L = P.L
    bv, fv = L["bv"], L["fv"]
    PA = L["PA"]

    xb = [fv(SLOT_B, 512), fv(SLOT_B + 1024, 512)]
    yb = [fv(SLOT_B + 2048, 512), fv(SLOT_B + 3072, 512)]

    ev = {}
    for o2q in range(4):
        pw = [("dW", P.wcount["aWo"]), ("cT", P.C["cT"])]
        slot = o2q % 3

        @sec(P, "sync")
        def _(sy, pw=pw):
            for args in pw:
                P.w(sy, *args)
        dbc = wt_quarter_load(P, slot, L["wbf"]["aWo"], o2q)
        for tt in range(NT):
            pq = tt % 2

            @sec(P, "sync")
            def _(sy, tt=tt, o2q=o2q, pq=pq):
                P.w(sy, "cV")
                P.dma(sy, "dA", xb[pq],
                      L["x_in"][tt * 128:(tt + 1) * 128, o2q * 512:o2q * 512 + 512])

            @sec(P, "tensor")
            def _(t, tt=tt, pq=pq, dbc=dbc, o2q=o2q, slot=slot):
                P.w(t, "dA", dbc)
                if o2q == 0 and tt == 0:
                    P.w(t, "cV")
                if pq in ev:
                    P.w(t, *ev[pq])
                pa = PA[:, pq * 512:pq * 512 + 512]
                for k in range(NT):
                    P.inc(t.matmul(pa, bv(SLOT_A + k * 2048 + tt * 128, 128),
                                   wq_slot(L, slot, k), start=(k == 0), stop=(k == NT - 1)),
                          "cT", 1 if k == NT - 1 else 0)

            @sec(P, "vector")
            def _(v, tt=tt, pq=pq, o2q=o2q):
                P.w(v, "cT")
                P.w(v, "dA")
                P.inc(v.tensor_tensor(out=yb[pq], in0=PA[:, pq * 512:pq * 512 + 512],
                                      in1=xb[pq], op=OP.add), "cV")
            ev[pq] = ("cV", P.C["cV"])

            @sec(P, "sync")
            def _(sy, tt=tt, pq=pq, o2q=o2q):
                P.w(sy, "cV")
                P.dma(sy, "dA", L["resid_d"].ap()[tt * 128:(tt + 1) * 128,
                                                  o2q * 512:o2q * 512 + 512], yb[pq])


def wq_slot(L, slot, j):
    return L["bv"](SLOT_C + slot * 8192 + j * 512, 512)


FC_XNT = 0
FC_RING = 16384
FC_G = 40960
FC_GU0 = 47104
FC_GU1 = 53248
FC_FW = 59392
FC_OUT = 63488
FC_RST = 96256
FL_XB = SLOT_C + 8192
FL_XN = SLOT_C + 16384
FL_XNT = SLOT_C + 18432
FL_GW = 16384
FL_FWT = 20480


def emit_ffn(P, call):
    L = P.L
    bv, fv = L["bv"], L["fv"]
    PA, PB = L["PA"], L["PB"]
    rt = L["rt_sb"]
    sc = L["sc_sb"]
    phase_fence(P)

    if call == 0:
        gwst = bv(FL_GW, 2048)

        @sec(P, "gpsimd")
        def _(g):
            P.w(g, "cT")
            P.dma(g, "dW", gwst[0:NEXP, :], L["gate_w"])
        gwc = P.C["dW"]

        @sec(P, "vector")
        def _(v):
            v.tensor_copy(out=L["identf_sb"][:], in_=L["identb_sb"][:])

        for h in range(2):
            pt = L["PT0"] if h % 2 == 0 else L["PT1"]

            @sec(P, "tensor")
            def _(t, h=h, pt=pt):
                P.w(t, "dW", gwc)
                P.w(t, "cV")
                for j in range(8):
                    jj = h * 8 + j
                    P.w(t, "cT")
                    P.inc(t.transpose(pt[:, j * 128:j * 128 + 32],
                                      gwst[0:NEXP, jj * 128:jj * 128 + 128],
                                      L["identb_sb"][0:NEXP, 0:NEXP]), "cT", 1)

            @sec(P, "vector")
            def _(v, h=h, pt=pt):
                P.w(v, "cT")
                for j in range(8):
                    jj = h * 8 + j
                    v.tensor_copy(out=L["gwT_sb"][:, jj * 32:(jj + 1) * 32],
                                  in_=pt[:, j * 128:j * 128 + 32])

    xbuf = fv(FL_XB, 2048)
    xnst = bv(FL_XN, 2048)
    xntst = bv(FL_XNT, 2048)
    for i in range(NT):
        @sec(P, "sync")
        def _(sy, i=i):
            P.w(sy, "cV")
            P.w(sy, "cT")
            P.dma(sy, "dA", xbuf, L["resid_d"].ap()[i * 128:(i + 1) * 128, :])
        ln_tile(P, xbuf, False, xnst)
        transpose16(P, lambda jj: xnst[:, jj * 128:jj * 128 + 128],
                    lambda jj: xntst[:, jj * 128:jj * 128 + 128])

        @sec(P, "sync")
        def _(sy, i=i):
            P.w(sy, "cV")
            P.dma(sy, "dA", L["xnT_d"].ap()[i], xntst)

        @sec(P, "tensor")
        def _(t, i=i):
            P.w(t, "cV")
            for k in range(NT):
                P.inc(t.matmul(PB[:, 0:32], xntst[:, k * 128:k * 128 + 128],
                               L["gwT_sb"][:, k * 32:(k + 1) * 32],
                               start=(k == 0), stop=(k == NT - 1)),
                      "cT", 1 if k == NT - 1 else 0)

        @sec(P, "vector")
        def _(v, i=i):
            P.w(v, "cT")
            v.tensor_copy(out=rt[:, 0:32], in_=PB[:, 0:32])
            v.tensor_reduce(out=sc[:, 8:9], in_=rt[:, 0:32], axis=AX.X, op=OP.max)
            v.tensor_scalar(out=sc[:, 8:9], in0=sc[:, 8:9], scalar1=-1.0, scalar2=None, op0=OP.mult)

        @sec(P, "scalar")
        def _(s, i=i):
            P.w(s, "cV")
            s.activation(out=rt[:, 32:64], in_=rt[:, 0:32], func=AF.Exp,
                         bias=sc[:, 8:9], accum_out=sc[:, 9:10])

        @sec(P, "vector")
        def _(v, i=i):
            P.w(v, "cS")
            v.reciprocal(out=sc[:, 10:11], in_=sc[:, 9:10])
            v.tensor_scalar(out=rt[:, 0:32], in0=rt[:, 32:64], scalar1=sc[:, 10:11],
                            scalar2=None, op0=OP.mult)
            v.tensor_copy(out=rt[:, 32:64], in_=rt[:, 0:32])
            for kk in range(TOPK):
                v.tensor_reduce(out=sc[:, 11:12], in_=rt[:, 32:64], axis=AX.X, op=OP.max)
                if kk < TOPK - 1:
                    v.tensor_scalar(out=rt[:, 64:96], in0=rt[:, 32:64], scalar1=sc[:, 11:12],
                                    scalar2=None, op0=OP.is_ge)
                    v.scalar_tensor_tensor(out=rt[:, 32:64], in0=rt[:, 64:96], scalar=-2.0,
                                           in1=rt[:, 32:64], op0=OP.mult, op1=OP.add)
            v.tensor_scalar(out=rt[:, 64:96], in0=rt[:, 0:32], scalar1=sc[:, 11:12],
                            scalar2=None, op0=OP.is_ge)
            v.tensor_tensor(out=rt[:, 32:64], in0=rt[:, 0:32], in1=rt[:, 64:96], op=OP.mult)
            v.tensor_reduce(out=sc[:, 12:13], in_=rt[:, 32:64], axis=AX.X, op=OP.add)
            v.tensor_scalar(out=sc[:, 12:13], in0=sc[:, 12:13], scalar1=1e-9, scalar2=None, op0=OP.max)
            v.reciprocal(out=sc[:, 13:14], in_=sc[:, 12:13])
            v.tensor_scalar(out=L["fw_sb"][:, i * 32:(i + 1) * 32], in0=rt[:, 32:64],
                            scalar1=sc[:, 13:14], scalar2=None, op0=OP.mult)

    fwtst = fv(FL_FWT, 2048)
    for i in range(NT):
        @sec(P, "tensor")
        def _(t, i=i):
            P.w(t, "cV")
            P.w(t, "cT")
            P.inc(t.transpose(PB[0:NEXP, 128:256],
                              L["fw_sb"][:, i * 32:(i + 1) * 32], L["identf_sb"][:]), "cT", 1)

        @sec(P, "vector")
        def _(v, i=i):
            P.w(v, "cT")
            v.tensor_copy(out=fwtst[0:NEXP, i * 128:(i + 1) * 128], in_=PB[0:NEXP, 128:256])

    @sec(P, "sync")
    def _(sy):
        P.w(sy, "cV")
        P.dma(sy, "dA", L["fwT_d"].ap()[:, :], fwtst[0:NEXP, :])
    fwt_stored = P.C["dA"]

    ring_c = {"i": 0, "busy": {}}
    WSTRIDE = {"wg": 768, "wu": 768, "wd": 2048}

    def wload(name, e, first=False):
        slot = ring_c["i"] % 2
        ring_c["i"] += 1
        busy = ring_c["busy"].get(slot)

        @sec(P, "sync")
        def _(g, name=name, e=e, slot=slot, busy=busy, first=first):
            if first:
                P.w(g, "cV")
                P.w(g, "cT")
            P.w(g, "dW", P.wcount[f"ffn{e}"])
            if busy is not None:
                P.w(g, "cT", busy)
            wi = {"wg": 768, "wu": 768, "wd": 2048}[name]
            P.dma(g, "dA",
                  bv(FC_RING + slot * 12288, 12288).rearrange("p (s i) -> p s i", i=wi),
                  L["wbf"][name].ap()[e].rearrange("(s p) i -> p s i", p=128))
        return slot, P.C["dA"]

    ev = {}
    for ch in range(2):
        @sec(P, "sync")
        def _(sy, ch=ch):
            P.w(sy, "cT")
            for j in range(NT):
                P.dma(sy, "dA", bv(FC_XNT + j * 1024, 1024),
                      L["xnT_d"].ap()[ch * 8:(ch + 1) * 8, :, j * 128:(j + 1) * 128]
                          .rearrange("c p n -> p c n"), gate=(j == 0))
        xnt_done = P.C["dA"]

        for pair in range(16):
            for sub, guoff in ((0, FC_GU0), (1, FC_GU1)):
                e = 2 * pair + sub
                sg, dg = wload("wg", e, first=(ch == 0 and pair == 0 and sub == 0))
                su, du = wload("wu", e)

                @sec(P, "sync")
                def _(sy, e=e, ch=ch, sub=sub):
                    P.w(sy, "cV")
                    P.dma(sy, "dA", fv(FC_FW + sub * 2048, 1024),
                          bass.AP(L["fwT_d"], e * T + ch * 1024, [[0, 128], [1, 1024]]))
                dfw = P.C["dA"]

                for it in range(NIT):
                    @sec(P, "tensor")
                    def _(t, it=it, dg=dg, sg=sg):
                        if it == 0:
                            P.w(t, "dA", dg)
                            P.w(t, "dA", xnt_done)
                        if "pa01" in ev:
                            P.w(t, *ev["pa01"])
                        for n in range(2):
                            for k in range(NT):
                                P.inc(t.matmul(PA[:, n * 512:(n + 1) * 512],
                                               bv(FC_RING + sg * 12288 + k * 768 + it * 128, 128),
                                               bv(FC_XNT + k * 1024 + n * 512, 512),
                                               start=(k == 0), stop=(k == NT - 1)),
                                      "cT", 1 if (k == NT - 1 and n == 1) else 0)

                    @sec(P, "scalar")
                    def _(s, it=it):
                        P.w(s, "cT")
                        P.w(s, "cV")
                        s.activation(out=bv(FC_G + it * 1024, 1024), in_=PA[:, 0:1024],
                                     func=AF.Silu)
                    ev["pa01"] = ("cS", P.C["cS"])

                    @sec(P, "tensor")
                    def _(t, it=it, du=du, su=su):
                        if it == 0:
                            P.w(t, "dA", du)
                        if "pa23" in ev:
                            P.w(t, *ev["pa23"])
                        for n in range(2):
                            for k in range(NT):
                                P.inc(t.matmul(PA[:, 1024 + n * 512:1024 + (n + 1) * 512],
                                               bv(FC_RING + su * 12288 + k * 768 + it * 128, 128),
                                               bv(FC_XNT + k * 1024 + n * 512, 512),
                                               start=(k == 0), stop=(k == NT - 1)),
                                      "cT", 1 if (k == NT - 1 and n == 1) else 0)

                    @sec(P, "vector")
                    def _(v, it=it, sub=sub, guoff=guoff, dfw=dfw):
                        P.w(v, "cT")
                        P.w(v, "dA", dfw)
                        P.w(v, "cS")
                        gu = bv(guoff + it * 1024, 1024)
                        v.tensor_tensor(out=gu, in0=PA[:, 1024:2048],
                                        in1=fv(FC_FW + sub * 2048, 1024), op=OP.mult)
                        v.tensor_tensor(out=gu, in0=gu, in1=bv(FC_G + it * 1024, 1024),
                                        op=OP.mult)
                    ev["pa23"] = ("cV", P.C["cV"])
                ring_c["busy"][sg] = P.C["cT"]
                ring_c["busy"][su] = P.C["cT"]

            dslots = [wload("wd", 2 * pair)[0:2], wload("wd", 2 * pair + 1)[0:2]]

            for tt in range(8):
                for hh2 in range(2):
                    use_pb = (tt * 2 + hh2) % 2 == 0
                    ps = PB[:, 0:1024] if use_pb else PA[:, 0:1024]
                    key = "pb" if use_pb else "pa01"

                    @sec(P, "tensor")
                    def _(t, tt=tt, hh2=hh2, ps=ps, key=key, dslots=dslots):
                        if tt == 0 and hh2 == 0:
                            P.w(t, "dA", dslots[1][1])
                            P.w(t, "cV")
                        if key in ev:
                            P.w(t, *ev[key])
                        for sub, (sd, _dc) in enumerate(dslots):
                            guoff = FC_GU0 if sub == 0 else FC_GU1
                            for it in range(NIT):
                                for n in range(2):
                                    P.inc(t.matmul(
                                        ps[:, n * 512:(n + 1) * 512],
                                        bv(guoff + it * 1024 + tt * 128, 128),
                                        bv(FC_RING + sd * 12288 + it * 2048
                                           + hh2 * 1024 + n * 512, 512),
                                        start=(sub == 0 and it == 0),
                                        stop=(sub == 1 and it == NIT - 1)),
                                        "cT", 1 if (sub == 1 and it == NIT - 1 and n == 1) else 0)

                    @sec(P, "vector")
                    def _(v, tt=tt, hh2=hh2, ps=ps, pair=pair):
                        P.w(v, "cT")
                        dst = fv(FC_OUT + tt * 4096 + hh2 * 2048, 1024)
                        if pair == 0:
                            v.tensor_copy(out=dst, in_=ps)
                        else:
                            v.tensor_tensor(out=dst, in0=dst, in1=ps, op=OP.add)
                    ev[key] = ("cV", P.C["cV"])
            for sd, _dc in dslots:
                ring_c["busy"][sd] = P.C["cT"]

        for tt in range(8):
            gt = ch * 8 + tt

            @sec(P, "sync")
            def _(sy, gt=gt):
                P.w(sy, "cV")
                P.dma(sy, "dA", fv(FC_RST, 2048), L["resid_d"].ap()[gt * 128:(gt + 1) * 128, :])
            rld = P.C["dA"]

            @sec(P, "vector")
            def _(v, tt=tt, rld=rld):
                P.w(v, "dA", rld)
                dst = fv(FC_OUT + tt * 4096, 2048)
                v.tensor_tensor(out=dst, in0=dst, in1=fv(FC_RST, 2048), op=OP.add)

            @sec(P, "sync")
            def _(sy, gt=gt, tt=tt, call=call):
                P.w(sy, "cV")
                dst = L["out"] if (call == 1 and L["n_sublayers"] >= 4) else L["resid_d"].ap()[:, :]
                P.dma(sy, "dA", dst[gt * 128:(gt + 1) * 128, :],
                      fv(FC_OUT + tt * 4096, 2048))


BB_HST = 0
BB_QT = 32768
BB_KT = 65536
BB_VNO = 73728
BB_RING = 82048
BB_PT = 0
BB_CTXT = 16384
BB_STST = 82048
BB_CTXST = 86144
BB_RST = 87296


def emit_attn_b(P):
    L = P.L
    bv, fv = L["bv"], L["fv"]
    PA, PB = L["PA"], L["PB"]
    sc = L["sc_sb"]
    phase_fence(P)

    xbuf = fv(BB_QT, 2048)
    xnst = bv(BB_QT + 8192, 2048)
    for i in range(NT):
        @sec(P, "sync")
        def _(sy, i=i):
            P.w(sy, "cV")
            P.w(sy, "cT")
            P.dma(sy, "dA", xbuf, L["resid_d"].ap()[i * 128:(i + 1) * 128, :])
        ln_tile(P, xbuf, False, xnst)
        transpose16(P, lambda jj: xnst[:, jj * 128:jj * 128 + 128],
                    lambda jj, i=i: bv(BB_HST + jj * 2048 + i * 128, 128))

    ring_c = {"i": 0, "busy": {}}

    def bload(name, colrange, first=False):
        slot = ring_c["i"] % 2
        ring_c["i"] += 1
        busy = ring_c["busy"].get(slot)
        c0, cn = colrange

        @sec(P, "sync")
        def _(g, name=name, slot=slot, busy=busy, first=first, c0=c0, cn=cn):
            if first:
                P.w(g, "cV")
                P.w(g, "cT")
            P.w(g, "dW", P.wcount[name])
            if busy is not None:
                P.w(g, "cT", busy)
            P.dma(g, "dA",
                  bv(BB_RING + slot * 8192, 16 * cn).rearrange("p (s o) -> p s o", o=cn),
                  L["wbf"][name].ap()[:, c0:c0 + cn].rearrange("(s p) o -> p s o", p=128))
        return slot, P.C["dA"]

    def rsl(slot, k, cn=512):
        return bv(BB_RING + slot * 8192 + k * cn, cn)

    @sec(P, "gpsimd")
    def _(g):
        P.w(g, "cT")
        P.w(g, "cV")
        g.memset(bv(BB_VNO, 8256), 1.0)
    vno_ones = P.C["cG"]

    ev = {}
    for q in range(4):
        slot, dq = bload("bWq", (q * 512, 512), first=(q == 0))
        for oo in range(4):
            ot = q * 4 + oo
            for th in range(2):
                half = th

                @sec(P, "tensor")
                def _(t, slot=slot, dq=dq, oo=oo, th=th, half=half, q=q):
                    P.w(t, "dA", dq)
                    if q == 0 and oo == 0 and th == 0:
                        P.w(t, "cV")
                        P.w(t, "cS")
                    if ("q", half) in ev:
                        P.w(t, *ev[("q", half)])
                    pa = PA[:, half * 1024:half * 1024 + 1024]
                    for n in range(2):
                        for k in range(NT):
                            P.inc(t.matmul(pa[:, n * 512:(n + 1) * 512],
                                           rsl(slot, k)[:, oo * 128:oo * 128 + 128],
                                           bv(BB_HST + k * 2048 + th * 1024 + n * 512, 512),
                                           start=(k == 0), stop=(k == NT - 1)),
                                  "cT", 1 if (k == NT - 1 and n == 1) else 0)

                @sec(P, "vector")
                def _(v, ot=ot, th=th, half=half):
                    P.w(v, "cT")
                    v.tensor_scalar(out=bv(BB_QT + ot * 2048 + th * 1024, 1024),
                                    in0=PA[:, half * 1024:half * 1024 + 1024],
                                    scalar1=SQRT_HD, scalar2=None, op0=OP.mult)
                ev[("q", half)] = ("cV", P.C["cV"])
        ring_c["busy"][slot] = P.C["cT"]

    slot, dk = bload("bWk", (0, 512))
    for oo in range(4):
        for th in range(2):
            half = th

            @sec(P, "tensor")
            def _(t, slot=slot, dk=dk, oo=oo, th=th, half=half):
                P.w(t, "dA", dk)
                if ("q", half) in ev:
                    P.w(t, *ev[("q", half)])
                pa = PA[:, half * 1024:half * 1024 + 1024]
                for n in range(2):
                    for k in range(NT):
                        P.inc(t.matmul(pa[:, n * 512:(n + 1) * 512],
                                       rsl(slot, k)[:, oo * 128:oo * 128 + 128],
                                       bv(BB_HST + k * 2048 + th * 1024 + n * 512, 512),
                                       start=(k == 0), stop=(k == NT - 1)),
                              "cT", 1 if (k == NT - 1 and n == 1) else 0)

            @sec(P, "vector")
            def _(v, oo=oo, th=th, half=half):
                P.w(v, "cT")
                v.tensor_copy(out=bv(BB_KT + oo * 2048 + th * 1024, 1024),
                              in_=PA[:, half * 1024:half * 1024 + 1024])
            ev[("q", half)] = ("cV", P.C["cV"])
    ring_c["busy"][slot] = P.C["cT"]

    slot, dv = bload("bWv", (0, 512))
    for tt in range(NT):
        pq = tt % 2

        @sec(P, "tensor")
        def _(t, slot=slot, dv=dv, tt=tt, pq=pq):
            P.w(t, "dA", dv)
            if tt == 0:
                P.w(t, "cV")
                P.w(t, "cS")
                P.w(t, "cG", vno_ones)
            if ("v", pq) in ev:
                P.w(t, *ev[("v", pq)])
            pa = PA[:, pq * 512:pq * 512 + 512]
            for k in range(NT):
                P.inc(t.matmul(pa, bv(BB_HST + k * 2048 + tt * 128, 128),
                               rsl(slot, k), start=(k == 0), stop=(k == NT - 1)),
                      "cT", 1 if k == NT - 1 else 0)

        @sec(P, "vector")
        def _(v, tt=tt, pq=pq):
            P.w(v, "cT")
            P.w(v, "cG", vno_ones)
            for kv in range(NKV):
                v.tensor_copy(out=bv(BB_VNO + tt * 516 + kv * 129, 128),
                              in_=PA[:, pq * 512 + kv * 128:pq * 512 + kv * 128 + 128])
        ev[("v", pq)] = ("cV", P.C["cV"])
    ring_c["busy"][slot] = P.C["cT"]

    STst = [fv(BB_STST, 1024), fv(BB_STST + 2048, 1024)]
    e50 = L["e50_sb"]
    for qh in range(2):
        nkt = 8 * (qh + 1)
        for H in range(NQH):
            kv = H // (NQH // NKV)
            for kc in range(nkt):
                half = kc % 2

                @sec(P, "tensor")
                def _(t, qh=qh, H=H, kc=kc, half=half, kv=kv):
                    if H == 0 and kc == 0:
                        P.w(t, "cV")
                        P.w(t, "cS")
                    if ("st", half) in ev:
                        P.w(t, *ev[("st", half)])
                    pa = PA[:, half * 1024:half * 1024 + 1024]
                    for n in range(2):
                        P.inc(t.matmul(pa[:, n * 512:(n + 1) * 512],
                                       bv(BB_KT + kv * 2048 + kc * 128, 128),
                                       bv(BB_QT + H * 2048 + qh * 1024 + n * 512, 512),
                                       start=True, stop=True),
                              "cT", 1 if n == 1 else 0)

                dl_ = 128 * (kc - 8 * qh)

                @sec(P, "vector")
                def _(v, kc=kc, half=half, dl_=dl_):
                    P.w(v, "cT")
                    P.w(v, "cS")
                    pa = PA[:, half * 1024:half * 1024 + 1024]
                    st = STst[half]
                    lo = max(dl_, 0)
                    if lo < 1024:
                        v.tensor_scalar(out=st[:, lo:1024], in0=pa[:, lo:1024],
                                        scalar1=-50.0, scalar2=50.0, op0=OP.max, op1=OP.min)
                    if dl_ >= 0:
                        v.scalar_tensor_tensor(out=st[:, dl_:dl_ + 128],
                                               in0=st[:, dl_:dl_ + 128], scalar=50.0,
                                               in1=L["triu_sb"][:],
                                               op0=OP.add, op1=OP.mult)
                        v.tensor_scalar(out=st[:, dl_:dl_ + 128], in0=st[:, dl_:dl_ + 128],
                                        scalar1=-50.0, scalar2=None, op0=OP.add)

                @sec(P, "scalar")
                def _(s, kc=kc, half=half, dl_=dl_, H=H):
                    P.w(s, "cV")
                    if H == 0 and kc == 0:
                        P.w(s, "dA")
                        P.w(s, "cT")
                    st = STst[half]
                    pt = bv(BB_PT + kc * 1024, 1024)
                    if dl_ >= 0:
                        if dl_ + 128 < 1024:
                            s.activation(out=pt[:, dl_ + 128:1024], in_=st[:, dl_ + 128:1024],
                                         func=AF.Exp)
                        s.activation(out=pt[:, dl_:dl_ + 128], in_=st[:, dl_:dl_ + 128],
                                     func=AF.Exp)
                    else:
                        s.activation(out=pt, in_=st, func=AF.Exp)
                ev[("st", half)] = ("cS", P.C["cS"])

            for qtl in range(8):
                gq = qh * 8 + qtl
                half = qtl % 2
                psb = PB[:, half * 512:half * 512 + 129]

                @sec(P, "tensor")
                def _(t, qtl=qtl, gq=gq, half=half, psb=psb, kv=kv, nkt=nkt, qh=qh):
                    P.w(t, "cS")
                    if ("ctx", half) in ev:
                        P.w(t, *ev[("ctx", half)])
                    for kc in range(16):
                        if kc < nkt:
                            lhs = (bv(BB_PT + kc * 1024 + qtl * 128, 128) if kc <= gq
                                   else e50[:, 0:128])
                        else:
                            lhs = e50[:, 0:128]
                        P.inc(t.matmul(psb, lhs, bv(BB_VNO + kc * 516 + kv * 129, 129),
                                       start=(kc == 0), stop=(kc == 15)),
                              "cT", 1 if kc == 15 else 0)

                @sec(P, "vector")
                def _(v, qtl=qtl, half=half, psb=psb):
                    P.w(v, "cT")
                    v.reciprocal(out=sc[:, 14:15], in_=PB[:, half * 512 + 128:half * 512 + 129])
                    v.tensor_scalar(out=bv(BB_CTXST + qtl * 128, 128),
                                    in0=PB[:, half * 512:half * 512 + 128],
                                    scalar1=sc[:, 14:15], scalar2=None, op0=OP.mult)
                ev[("ctx", half)] = ("cV", P.C["cV"])

            transpose16(P, lambda jj: bv(BB_CTXST + jj * 128, 128),
                        lambda jj, H=H: bv(BB_CTXT + H * 1024 + jj * 128, 128), n=8)

        for o2q in range(4):
            slot = o2q % 2
            busy = ring_c["busy"].get(("wo", slot))

            @sec(P, "sync")
            def _(g, o2q=o2q, slot=slot, busy=busy, qh=qh):
                P.w(g, "dW", P.wcount["bWo"])
                P.w(g, "cS")
                P.w(g, "cT")
                if busy is not None:
                    P.w(g, "cT", busy)
                P.dma(g, "dA",
                      bv(BB_PT + slot * 8192, 8192).rearrange("p (s o) -> p s o", o=512),
                      L["wbf"]["bWo"].ap()[:, o2q * 512:(o2q + 1) * 512]
                          .rearrange("(s p) o -> p s o", p=128))
            dwo = P.C["dA"]

            @sec(P, "sync")
            def _(sy, qh=qh, o2q=o2q):
                P.w(sy, "cV")
                for qtl in range(8):
                    gq = qh * 8 + qtl
                    P.dma(sy, "dA", fv(BB_RST + qtl * 1024, 512),
                          L["resid_d"].ap()[gq * 128:(gq + 1) * 128,
                                            o2q * 512:(o2q + 1) * 512],
                          gate=(qtl == 0))
            rld = P.C["dA"]

            for qtl in range(8):
                gq = qh * 8 + qtl
                half = qtl % 2

                @sec(P, "tensor")
                def _(t, qtl=qtl, half=half, dwo=dwo, slot=slot, o2q=o2q):
                    P.w(t, "dA", dwo)
                    if o2q == 0 and qtl == 0:
                        P.w(t, "cV")
                        P.w(t, "cS")
                    if ("wo", half) in ev:
                        P.w(t, *ev[("wo", half)])
                    pa = PA[:, half * 512:half * 512 + 512]
                    for k in range(NT):
                        P.inc(t.matmul(pa, bv(BB_CTXT + k * 1024 + qtl * 128, 128),
                                       bv(BB_PT + slot * 8192 + k * 512, 512),
                                       start=(k == 0), stop=(k == NT - 1)),
                              "cT", 1 if k == NT - 1 else 0)

                @sec(P, "vector")
                def _(v, qtl=qtl, half=half, rld=rld):
                    P.w(v, "cT")
                    P.w(v, "dA", rld)
                    v.tensor_tensor(out=fv(BB_RST + qtl * 1024, 512),
                                    in0=PA[:, half * 512:half * 512 + 512],
                                    in1=fv(BB_RST + qtl * 1024, 512), op=OP.add)
                ev[("wo", half)] = ("cV", P.C["cV"])

                @sec(P, "sync")
                def _(sy, gq=gq, o2q=o2q, qtl=qtl):
                    P.w(sy, "cV")
                    P.dma(sy, "dA", L["resid_d"].ap()[gq * 128:(gq + 1) * 128,
                                                      o2q * 512:(o2q + 1) * 512],
                          fv(BB_RST + qtl * 1024, 512))
            ring_c["busy"][("wo", slot)] = P.C["cT"]


def emit_finish(P):
    L = P.L

    @sec(P, "sync")
    def _(sy):
        P.w(sy, "dW")
        P.w(sy, "cV")
        P.w(sy, "cS")
        P.w(sy, "cT")
        if L["n_sublayers"] < 4:
            P.dma(sy, "dA", L["out"], L["resid_d"].ap()[:, :])
        P.w(sy, "dA")


def kernel(**inputs):
    x = np.ascontiguousarray(inputs["L0_attn_a_x"])
    nc = build(4)
    consts = {
        "c_ident_bf": np.eye(128, dtype=ml_dtypes.bfloat16),
        "c_triu_f32": np.triu(np.ones((128, 128), dtype=np.float32)),
    }
    base = {k: np.ascontiguousarray(v) for k, v in inputs.items() if k != "L0_attn_a_x"}
    in_maps = []
    for c in range(8):
        m = dict(base)
        m.update(consts)
        m["L0_attn_a_x"] = x[c]
        in_maps.append(m)
    res = run_bass_kernel_spmd(nc, in_maps, core_ids=list(range(8)))
    return np.stack([r["out"] for r in res.results], axis=0)
